# version 37
# baseline (speedup 1.0000x reference)
import numpy as np
import ml_dtypes
from contextlib import ExitStack

import concourse.bass as bass
import concourse.mybir as mybir
from concourse.bass_utils import run_bass_kernel_spmd

F32 = mybir.dt.float32
BF16 = mybir.dt.bfloat16
AF = mybir.ActivationFunctionType
ALU = mybir.AluOpType
AX = mybir.AxisListType

NCORES = 8
NT = 4096
C = 128
NCH = 32
NG = 8
H = 4
D = 1024
EPS = 1e-5
SCL = 0.0625
NEG8 = -0.125


def _sems(nc, E, names):
    return {n: E(nc.semaphore(n)) for n in names}


def _serial(e, selfsem, semcnt):
    last = [None]

    def W():
        if last[0] is not None:
            e.wait_ge(last[0][0], last[0][1])

    def fin(ins, inc=None, n=1):
        sem = selfsem if inc is None else inc
        v = 1 if inc is None else n
        ins.then_inc(sem, v)
        semcnt[sem] = semcnt.get(sem, 0) + v
        last[0] = (sem, semcnt[sem])
        return ins

    return W, fin



L1_STAGES = {"ln", "proj", "chunk", "summ"}

L1_SPILLS = [("sp_q", [NG, 128, 4096], BF16), ("sp_xn", [NG, 128, 4096], BF16),
             ("sp_s", [NCH, 128, 256], BF16),
             ("sp_glp", [NCH, 128, 512], F32),
             ("sp_v", [NCH, 128, 1024], BF16),
             ("sp_hv", [NCH, 64, 1024], BF16), ("sp_da", [NCH, 1, 256], F32),
             ("sp_wo", [1024, 1024], BF16), ("sp_fwT", [1024, 1024], BF16)]


def _decl_l1_inputs(nc):
    d = {}
    d["xseg"] = nc.dram_tensor("xseg", [NT, D], F32, kind="ExternalInput").ap()
    d["Wq"] = nc.dram_tensor("Wq", [D, D], F32, kind="ExternalInput").ap()
    d["Wk"] = nc.dram_tensor("Wk", [D, D], F32, kind="ExternalInput").ap()
    d["Wv"] = nc.dram_tensor("Wv", [D, D], F32, kind="ExternalInput").ap()
    d["Wf"] = nc.dram_tensor("Wf", [D, 256], F32, kind="ExternalInput").ap()
    d["trineg"] = nc.dram_tensor("trineg", [C, C], BF16, kind="ExternalInput").ap()
    d["triu_in"] = nc.dram_tensor("triu_in", [C, C], BF16, kind="ExternalInput").ap()
    d["tri4_in"] = nc.dram_tensor("tri4_in", [C, 512], BF16, kind="ExternalInput").ap()
    d["ident_in"] = nc.dram_tensor("ident_in", [C, C], BF16, kind="ExternalInput").ap()
    return d


def build_l1(stages=None, clvl=4):
    nc = bass.Bass("TRN2", target_bir_lowering=False, debug=False)
    d = _decl_l1_inputs(nc)
    d["Wo"] = nc.dram_tensor("Wo", [D, D], F32, kind="ExternalInput").ap()
    d["fw"] = nc.dram_tensor("fw", [D, D], F32, kind="ExternalInput").ap()
    d["fscc"] = nc.dram_tensor("fscc", [128, 8], F32, kind="ExternalInput").ap()
    d["gnwc"] = nc.dram_tensor("gnwc", [128, 2], F32, kind="ExternalInput").ap()
    for nm, shp, dt in L1_SPILLS:
        d[nm] = nc.dram_tensor(nm, shp, dt, kind="ExternalOutput").ap()
    d["summ"] = nc.dram_tensor("summ", [64, 2304], F32, kind="ExternalOutput").ap()
    _emit_l1(nc, d, stages, clvl)
    return nc


def _emit_l1(nc, dmap, stages=None, clvl=4):
    stages = L1_STAGES if stages is None else stages
    ln_on = "ln" in stages
    pj_on = "proj" in stages
    ch_on = "chunk" in stages
    sm_on = "summ" in stages and clvl >= 4
    xseg = dmap["xseg"]; Wq = dmap["Wq"]; Wk = dmap["Wk"]; Wv = dmap["Wv"]
    Wf = dmap["Wf"]; trineg = dmap["trineg"]; triu_in = dmap["triu_in"]
    tri4_in = dmap["tri4_in"]; ident_in = dmap["ident_in"]
    sp_q = dmap["sp_q"]; sp_xn = dmap["sp_xn"]; sp_s = dmap["sp_s"]
    sp_glp = dmap["sp_glp"]; sp_v = dmap["sp_v"]
    sp_hv = dmap["sp_hv"]; sp_da = dmap["sp_da"]; summ = dmap["summ"]
    prep = "Wo" in dmap
    if prep:
        Wo = dmap["Wo"]; fw = dmap["fw"]
        fscc_in = dmap["fscc"]; gnwc_in = dmap["gnwc"]
        sp_wo = dmap["sp_wo"]; sp_fwT = dmap["sp_fwT"]

    with ExitStack() as ctx:
        E = ctx.enter_context
        xin = E(nc.sbuf_tensor("xin", [128, 4, 1024], F32))
        lnst = E(nc.sbuf_tensor("lnst", [128, 2, 8], F32))
        sqs = E(nc.sbuf_tensor("sqs", [128, 1024], F32))
        xn_tm = E(nc.sbuf_tensor("xn_tm", [128, 4, 1024], BF16))
        xnT = E(nc.sbuf_tensor("xnT", [128, 2, 4, 8, 128], BF16))
        wq_sb = E(nc.sbuf_tensor("wq_sb", [128, 8, 1024], BF16))
        wk_sb = E(nc.sbuf_tensor("wk_sb", [128, 8, 1024], BF16))
        wv_sb = E(nc.sbuf_tensor("wv_sb", [128, 8, 1024], BF16))
        wf_sb = E(nc.sbuf_tensor("wf_sb", [128, 8, 256], BF16))
        qT_sb = E(nc.sbuf_tensor("qT_sb", [128, 4, 8, 128], BF16))
        kT_sb = E(nc.sbuf_tensor("kT_sb", [128, 4, 8, 128], BF16))
        qsil = E(nc.sbuf_tensor("qsil", [128, 2, 512], F32))
        k_tm = E(nc.sbuf_tensor("k_tm", [128, 2, 1024], BF16))
        v_sb = E(nc.sbuf_tensor("v_sb", [128, 2, 1024], BF16))
        fA = E(nc.sbuf_tensor("fA", [128, 256], F32))
        fB = E(nc.sbuf_tensor("fB", [128, 256], F32))
        fC = E(nc.sbuf_tensor("fC", [128, 256], F32))
        g0 = E(nc.sbuf_tensor("g0", [128, 256], F32))
        g0h = E(nc.sbuf_tensor("g0h", [128, 256], BF16))
        g0l = E(nc.sbuf_tensor("g0l", [128, 256], BF16))
        eg = E(nc.sbuf_tensor("eg", [128, 256], F32))
        s_sb = E(nc.sbuf_tensor("s_sb", [128, 256], F32))
        EnG = E(nc.sbuf_tensor("EnG", [128, 256], F32))
        snt = E(nc.sbuf_tensor("snt", [128, 2, 256], BF16))
        s2 = E(nc.sbuf_tensor("s2", [128, 256], BF16))
        glp = E(nc.sbuf_tensor("glp", [128, 2, 512], F32))
        gcs = glp[:, :, 0:256]
        lps = glp[:, :, 256:512]
        egt_f = E(nc.sbuf_tensor("egt_f", [64, 2, 256], F32))
        egt_bf = E(nc.sbuf_tensor("egt_bf", [1, 256], BF16))
        egt2_bf = E(nc.sbuf_tensor("egt2_bf", [1, 512], BF16))
        gacc = E(nc.sbuf_tensor("gacc", [64, 256], F32))
        dacc = E(nc.sbuf_tensor("dacc", [1, 2, 256], F32))
        smaskT = E(nc.sbuf_tensor("smaskT", [128, 4, 128], BF16))
        hkl = E(nc.sbuf_tensor("hkl", [128, 8, 64], BF16))
        hkt = E(nc.sbuf_tensor("hkt", [128, 512], F32))
        hvl = E(nc.sbuf_tensor("hvl", [64, 4, 256], BF16))
        hvsp = E(nc.sbuf_tensor("hvsp", [64, 4, 1024], BF16))
        ecol = E(nc.sbuf_tensor("ecol", [64, 4], F32))
        onesb = E(nc.sbuf_tensor("onesb", [1, 128], BF16))
        neg8c = E(nc.sbuf_tensor("neg8c", [128, 128], BF16))
        epsb = E(nc.sbuf_tensor("epsb", [128, 1], F32))
        trin_sb = E(nc.sbuf_tensor("trin_sb", [128, 128], BF16))
        triu_sb = E(nc.sbuf_tensor("triu_sb", [128, 128], BF16))
        tri4_sb = E(nc.sbuf_tensor("tri4_sb", [128, 512], BF16))
        ident = E(nc.sbuf_tensor("ident", [128, 128], BF16))
        summ_sb = E(nc.sbuf_tensor("summ_sb", [64, 2304], F32))
        hktT = E(nc.sbuf_tensor("hktT", [64, 8, 128], BF16))
        if prep:
            ftmp = E(nc.sbuf_tensor("ftmp", [128, 2, 1024], BF16))
            ftmpT = E(nc.sbuf_tensor("ftmpT", [128, 2, 8, 128], BF16))
            wotmp = E(nc.sbuf_tensor("wotmp", [128, 2, 1024], BF16))
            fscc_sb = E(nc.sbuf_tensor("fscc_sb", [128, 8], F32))
            gnwc_sb = E(nc.sbuf_tensor("gnwc_sb", [128, 2], F32))

        pjA = E(nc.psum_tensor("pjA", [128, 512], F32))
        pjB = E(nc.psum_tensor("pjB", [128, 512], F32))
        pb4 = E(nc.psum_tensor("pb4", [128, 512], F32))
        pvA = E(nc.psum_tensor("pvA", [128, 512], F32))
        pvB = E(nc.psum_tensor("pvB", [128, 512], F32))
        pSb = E(nc.psum_tensor("pSb", [128, 512], F32))
        pdkT = E(nc.psum_tensor("pdkT", [128, 512], F32))
        PSB = E(nc.psum_tensor("PSB", [128, 1024], BF16))
        pj = [pjA[:, :], pjB[:, :]]
        pEb256 = pjA[:, 0:256]
        pLp = pjA[:, 256:512]
        pEb512 = pjB[:, :]
        pvh = [pvA, pvB]

        def pdv(h):
            return pvh[h // 2][0:64, (h % 2) * 256:(h % 2) * 256 + 256]
        pf = pb4[:, 0:256]
        pGt = pb4[:, 0:256]
        pGc = pb4[:, 256:512]
        pS = pSb[:, :]
        pec = pSb[0:64, 0:4]
        pdk = pdkT[:, :]

        S = _sems(nc, E, [
            "wld", "ini", "lna", "lnv", "lns", "lnr", "lnx",
            "qkp", "qke", "qsc", "qsp",
            "fp", "fc", "gp", "ghl", "gc", "eg_", "eng", "egt", "egb", "eb", "ece",
            "vp", "ve", "st", "sm", "sn", "s2d", "lp", "lps_", "gcs_",
            "dk", "hk", "dv", "hv", "hvc", "gac", "dac", "hkTp", "hkT",
            "sum_", "smd",
            "xl0", "xl1", "xl2", "xl3", "xt0", "xt1", "kt0", "kt1",
            "xnsp0", "xnsp1",
            "spgl0", "spgl1", "sps0", "sps1",
            "spv0", "spv1", "sphv0", "sphv1", "sphv2", "sphv3",
            "speg0", "speg1",
            "fld0", "fld1", "wol0", "wol1", "ftp0", "ftp1",
            "fsp0", "fsp1", "wsp0", "wsp1", "fsg", "wfd", "wcl",
            "slfp", "slfa", "slfv"])
        SEMCNT = {}

        block = E(nc.Block())

        @block.sync
        def _(e):
            if prep:
                e.dma_start(out=fscc_sb[:, :],
                            in_=fscc_in[:, :]).then_inc(S["wcl"], 16)
                e.dma_start(out=gnwc_sb[:, :],
                            in_=gnwc_in[:, :]).then_inc(S["wcl"], 16)

            def _sync_prep(i):
                if i < 8:
                    ot = i
                    e.wait_ge(S["fsg"], ot + 1)
                    if ot >= 2:
                        e.wait_ge(S["fsp%d" % (ot % 2)], 16 * ((ot - 2) // 2 + 1))
                    e.dma_start(out=ftmpT[:, ot % 2, :, :],
                                in_=ftmp[:, ot % 2, :].rearrange(
                                    "p (a c) -> p a c", a=8),
                                transpose=True).then_inc(S["ftp%d" % (ot % 2)], 16)
                    e.wait_ge(S["ftp%d" % (ot % 2)], 16 * (ot // 2 + 1))
                    e.dma_start(out=sp_fwT[ot * 128:(ot + 1) * 128, :],
                                in_=ftmpT[:, ot % 2, :, :].rearrange(
                                    "p a c -> p (a c)")
                                ).then_inc(S["fsp%d" % (ot % 2)], 16)
                else:
                    dt = i - 8
                    e.wait_ge(S["wfd"], dt + 1)
                    e.dma_start(out=sp_wo[dt * 128:(dt + 1) * 128, :],
                                in_=wotmp[:, dt % 2, :]
                                ).then_inc(S["wsp%d" % (dt % 2)], 16)

            for g in range(NG):
                for tt in range(4):
                    t = 4 * g + tt
                    if t >= 4:
                        e.wait_ge(S["lnx"], t - 3)
                    e.dma_start(out=xin[:, t % 4, :],
                                in_=xseg[t * 128:(t + 1) * 128, :]
                                ).then_inc(S["xl%d" % (t % 4)], 16)
                for tt in range(4):
                    t = 4 * g + tt
                    e.wait_ge(S["lnx"], t + 1)
                    if g >= 2:
                        if pj_on:
                            e.wait_ge(S["xnsp%d" % (g % 2)], 16 * (g // 2))
                        if ch_on:
                            e.wait_ge(S["vp"], 4 * (g - 1))
                    e.dma_start(
                        out=xnT[:, g % 2, tt, :, :],
                        in_=xn_tm[:, t % 4, :].rearrange("p (a c) -> p a c", a=8),
                        transpose=True).then_inc(S["xt%d" % (g % 2)], 16)
                for cc in range(4):
                    if not ch_on or clvl < 3:
                        break
                    c = 4 * g + cc
                    e.wait_ge(S["qke"], 16 * g + 16)
                    if c >= 2:
                        e.wait_ge(S["dk"], c - 1)
                    e.dma_start(
                        out=k_tm[:, c % 2, :].rearrange("p (a c) -> p a c", a=8),
                        in_=kT_sb[:, cc, :, :],
                        transpose=True).then_inc(S["kt%d" % (c % 2)], 16)
                if prep and g >= 1:
                    _sync_prep(2 * (g - 1))
                    _sync_prep(2 * (g - 1) + 1)
            if prep:
                _sync_prep(14)
                _sync_prep(15)
                for nm in ("fsp", "wsp"):
                    e.wait_ge(S[nm + "0"], 64)
                    e.wait_ge(S[nm + "1"], 64)

        @block.gpsimd
        def _(e):
            e.dma_start(out=wq_sb[:, :, :],
                        in_=Wq.rearrange("(a p) f -> p a f", p=128)).then_inc(S["wld"], 16)
            e.dma_start(out=wk_sb[:, :, :],
                        in_=Wk.rearrange("(a p) f -> p a f", p=128)).then_inc(S["wld"], 16)
            e.dma_start(out=wv_sb[:, :, :],
                        in_=Wv.rearrange("(a p) f -> p a f", p=128)).then_inc(S["wld"], 16)
            e.dma_start(out=wf_sb[:, :, :],
                        in_=Wf.rearrange("(a p) f -> p a f", p=128)).then_inc(S["wld"], 16)
            e.dma_start(out=trin_sb[:, :], in_=trineg[:, :]).then_inc(S["wld"], 16)
            e.dma_start(out=triu_sb[:, :], in_=triu_in[:, :]).then_inc(S["wld"], 16)
            e.dma_start(out=tri4_sb[:, :], in_=tri4_in[:, :]).then_inc(S["wld"], 16)
            e.dma_start(out=ident[:, :], in_=ident_in[:, :]).then_inc(S["wld"], 16)
            W, fin = _serial(e, S["slfp"], SEMCNT)
            fin(e.memset(onesb[:, :], 1.0))
            W(); fin(e.memset(neg8c[:, :], NEG8))
            W(); fin(e.memset(epsb[:, :], EPS))
            W(); fin(e.memset(hkl[:, :, :], 0.0))
            W(); fin(e.memset(hvl[:, :, :], 0.0))
            W(); fin(e.memset(hvsp[:, :, :], 0.0))
            W(); fin(e.memset(summ_sb[:, :], 0.0))
            W(); fin(e.memset(gacc[:, :], 0.0), S["ini"])

            if prep:
                for ot in range(8):
                    if ot >= 2:
                        e.wait_ge(S["ftp%d" % (ot % 2)],
                                  16 * ((ot - 2) // 2 + 1))
                    e.dma_start(out=ftmp[:, ot % 2, :],
                                in_=fw[ot * 128:(ot + 1) * 128, :]
                                ).then_inc(S["fld%d" % (ot % 2)], 16)
                for dt in range(8):
                    if dt >= 2:
                        e.wait_ge(S["wsp%d" % (dt % 2)],
                                  16 * ((dt - 2) // 2 + 1))
                    e.dma_start(out=wotmp[:, dt % 2, :],
                                in_=Wo[dt * 128:(dt + 1) * 128, :]
                                ).then_inc(S["wol%d" % (dt % 2)], 16)

            if not sm_on:
                return
            e.wait_ge(S["hkT"], 1)
            e.wait_ge(S["hv"], NCH)
            e.wait_ge(S["gac"], NCH)
            W(); fin(e.tensor_copy(summ_sb[:, 0:1024],
                                   hktT.rearrange("p a c -> p (a c)")[:, :]))
            W(); fin(e.tensor_copy(summ_sb[:, 1024:2048],
                                   hvl.rearrange("p a m -> p (a m)")[:, :]))
            W(); fin(e.tensor_copy(summ_sb[0:1, 2048:2304], gacc[0:1, :]), S["sum_"])

        @block.scalar
        def _(e):
            W, fin = _serial(e, S["slfa"], SEMCNT)
            e.wait_ge(S["ini"], 1)

            def spills(c):
                p = c % 2
                e.wait_ge(S["gcs_"], c + 1)
                if clvl >= 3:
                    e.wait_ge(S["lps_"], c + 1)
                    e.dma_start(out=sp_glp[c],
                                in_=glp[:, p, :]).then_inc(S["spgl%d" % p], 16)
                else:
                    e.dma_start(out=sp_glp[c].rearrange(
                                    "p (a f) -> p a f", a=2)[:, 0, :],
                                in_=gcs[:, p, :]).then_inc(S["spgl%d" % p], 16)
                e.wait_ge(S["sn"], c + 1)
                e.dma_start(out=sp_s[c], in_=snt[:, p, :]).then_inc(S["sps%d" % p], 16)
                e.wait_ge(S["dac"], c + 1)
                e.dma_start(out=sp_da[c], in_=dacc[:, p, :]).then_inc(S["speg%d" % p], 16)
                if clvl >= 2:
                    e.wait_ge(S["ve"], c + 1)
                    e.dma_start(out=sp_v[c], in_=v_sb[:, p, :]).then_inc(S["spv%d" % p], 16)
                if clvl >= 4:
                    e.wait_ge(S["hvc"], c)
                    e.dma_start(out=sp_hv[c],
                                in_=hvsp[:, c % 4, :]).then_inc(S["sphv%d" % (c % 4)], 16)

            for g in range(NG):
                def _sq(t):
                    e.wait_ge(S["xl%d" % (t % 4)], 16 * (t // 4 + 1))
                    if t >= 2:
                        e.wait_ge(S["lnx"], t - 1)
                    W()
                    fin(e.activation(sqs[:, :], xin[:, t % 4, :], AF.Square,
                                     accum_out=lnst[:, t % 2, 0:1]), S["lna"])

                def _sqrt(t):
                    e.wait_ge(S["lnv"], t + 1)
                    W()
                    fin(e.activation(lnst[:, t % 2, 5:6], lnst[:, t % 2, 4:5],
                                     AF.Sqrt, bias=epsb[:, 0:1]), S["lns"])

                def _apply(t):
                    e.wait_ge(S["lnr"], t + 1)
                    if g >= 1:
                        e.wait_ge(S["xt%d" % ((g - 1) % 2)],
                                  64 * ((g - 1) // 2 + 1))
                    W()
                    fin(e.activation(xn_tm[:, t % 4, :], xin[:, t % 4, :],
                                     AF.Identity, bias=lnst[:, t % 2, 7:8],
                                     scale=lnst[:, t % 2, 6:7]), S["lnx"])
                t0 = 4 * g
                _sq(t0); _sq(t0 + 1); _sqrt(t0); _apply(t0)
                _sq(t0 + 2); _sqrt(t0 + 1); _apply(t0 + 1)
                _sq(t0 + 3); _sqrt(t0 + 2); _apply(t0 + 2)
                _sqrt(t0 + 3); _apply(t0 + 3)
                if g >= 1 and pj_on:
                    e.wait_ge(S["qsc"], 8 * g)
                    e.dma_start(out=sp_q[g - 1].rearrange("p (a c) -> p a c", a=8),
                                in_=qT_sb.rearrange("p t a c -> p (t a c)")[:, :].rearrange(
                          "p (a c) -> p a c", a=8)).then_inc(S["qsp"], 16)
                    e.wait_ge(S["xt%d" % ((g - 1) % 2)],
                              64 * ((g - 1) // 2 + 1))
                    e.dma_start(out=sp_xn[g - 1].rearrange("p (a c) -> p a c", a=8),
                                in_=xnT[:, (g - 1) % 2, :, :, :].rearrange(
                          "p t a c -> p (t a c)").rearrange(
                          "p (a c) -> p a c", a=8)).then_inc(S["xnsp%d" % ((g - 1) % 2)], 16)
                for idx in range(16):
                    if not pj_on:
                        break
                    e.wait_ge(S["qkp"], 16 * g + idx + 1)
                    if idx < 8:
                        W()
                        fin(e.activation(qsil[:, idx % 2, :], pj[idx % 2],
                                         AF.Silu), S["qke"])
                        W()
                        fin(e.activation(qT_sb[:, :, idx, :],
                                         qsil[:, idx % 2, :].rearrange(
                                             "p (a c) -> p a c", a=4),
                                         AF.Identity, scale=SCL), S["qsc"])
                    else:
                        W()
                        fin(e.activation(kT_sb[:, :, idx - 8, :], pj[idx % 2],
                                         AF.Silu), S["qke"])
                for cc in range(4):
                    if not ch_on:
                        break
                    c = 4 * g + cc
                    if c >= 2:
                        spills(c - 2)
                    e.wait_ge(S["fp"], c + 1)
                    e.wait_ge(S["gc"], c)
                    W(); fin(e.activation(fA[:, :], pf, AF.Exp, scale=-1.0))
                    W(); fin(e.activation(g0[:, :], fA[:, :], AF.Ln,
                                          bias=1.0), S["fc"])
                    if clvl >= 2:
                        e.wait_ge(S["vp"], c + 1)
                        if c >= 2:
                            e.wait_ge(S["spv%d" % (c % 2)], 16 * (c // 2))
                        W(); fin(e.activation(v_sb[:, c % 2, 0:512], pvA[:, :],
                                              AF.Identity))
                        W(); fin(e.activation(v_sb[:, c % 2, 512:1024], pvB[:, :],
                                              AF.Identity), S["ve"])
                    e.wait_ge(S["fc"], c + 1)
                    W(); fin(e.activation(eg[:, :], g0[:, :], AF.Exp,
                                          scale=NEG8), S["eg_"])
                    e.wait_ge(S["gc"], c + 1)
                    W(); fin(e.activation(EnG[:, :], pGc, AF.Exp,
                                          scale=-1.0), S["eng"])
                    if c >= 2:
                        e.wait_ge(S["speg%d" % (c % 2)], 16 * (c // 2))
                    W(); fin(e.activation(egt_f[:, c % 2, :], pGt[0:64, :],
                                          AF.Exp), S["egt"])
                    if c >= 2:
                        e.wait_ge(S["spgl%d" % (c % 2)], 16 * (c // 2))
                    W(); fin(e.activation(gcs[:, c % 2, :], pGc,
                                          AF.Identity), S["gcs_"])
            if ch_on:
                spills(30)
                spills(31)
            if not pj_on:
                e.wait_ge(S["xt0"], 64 * 4)
                e.wait_ge(S["xt1"], 64 * 4)
                return
            e.wait_ge(S["qsc"], 64)
            e.dma_start(out=sp_q[7].rearrange("p (a c) -> p a c", a=8),
                        in_=qT_sb.rearrange("p t a c -> p (t a c)")[:, :].rearrange(
                          "p (a c) -> p a c", a=8)).then_inc(S["qsp"], 16)
            e.wait_ge(S["xt1"], 64 * 4)
            e.dma_start(out=sp_xn[7].rearrange("p (a c) -> p a c", a=8),
                        in_=xnT[:, 1, :, :, :].rearrange(
                          "p t a c -> p (t a c)").rearrange(
                          "p (a c) -> p a c", a=8)).then_inc(S["xnsp1"], 16)
            if sm_on:
                e.wait_ge(S["sum_"], 1)
                e.dma_start(out=summ[:, :], in_=summ_sb[:, :]).then_inc(S["smd"], 16)
                e.wait_ge(S["smd"], 16)
            if ch_on:
                nms = ["spgl", "sps", "speg"]
                if clvl >= 2:
                    nms.append("spv")
                for nm in nms:
                    e.wait_ge(S[nm + "0"], 16 * 16)
                    e.wait_ge(S[nm + "1"], 16 * 16)
                if clvl >= 4:
                    for q_ in range(4):
                        e.wait_ge(S["sphv%d" % q_], 16 * 8)
            e.wait_ge(S["qsp"], 16 * NG)
            e.wait_ge(S["xnsp0"], 16 * 4)
            e.wait_ge(S["xnsp1"], 16 * 4)

        @block.vector
        def _(e):
            W, fin = _serial(e, S["slfv"], SEMCNT)

            def _dve_prep(i):
                if i < 8:
                    ot = i
                    e.wait_ge(S["fld%d" % (ot % 2)], 16 * (ot // 2 + 1))
                    if ot == 0:
                        e.wait_ge(S["wcl"], 32)
                    vv = ftmp[:, ot % 2, :]
                    W(); fin(e.tensor_scalar(vv, vv, 0.0, 2.0,
                                             ALU.is_ge, ALU.mult))
                    W(); fin(e.tensor_scalar_add(vv, vv, -1.0))
                    W(); fin(e.tensor_scalar_mul(vv, vv, fscc_sb[:, ot:ot + 1]),
                             S["fsg"])
                else:
                    dt = i - 8
                    e.wait_ge(S["wol%d" % (dt % 2)], 16 * (dt // 2 + 1))
                    vv = wotmp[:, dt % 2, :]
                    W(); fin(e.tensor_scalar_mul(vv, vv,
                                                 gnwc_sb[:, dt % 2:dt % 2 + 1]),
                             S["wfd"])

            for g in range(NG):
                for tt in range(4):
                    t = 4 * g + tt
                    e.wait_ge(S["xl%d" % (t % 4)], 16 * (t // 4 + 1))
                    if t >= 2:
                        e.wait_ge(S["lnx"], t - 1)
                    W(); fin(e.reduce_sum(lnst[:, t % 2, 1:2], xin[:, t % 4, :],
                                          axis=AX.X))
                    W(); fin(e.tensor_scalar_mul(lnst[:, t % 2, 2:3],
                                                 lnst[:, t % 2, 1:2], 1.0 / D))
                    W(); fin(e.tensor_mul(lnst[:, t % 2, 3:4],
                                          lnst[:, t % 2, 2:3],
                                          lnst[:, t % 2, 2:3]))
                    e.wait_ge(S["lna"], t + 1)
                    W()
                    fin(e.scalar_tensor_tensor(lnst[:, t % 2, 4:5],
                                               lnst[:, t % 2, 0:1], 1.0 / D,
                                               lnst[:, t % 2, 3:4],
                                               ALU.mult, ALU.subtract), S["lnv"])
                    e.wait_ge(S["lns"], t + 1)
                    W(); fin(e.reciprocal(lnst[:, t % 2, 6:7],
                                          lnst[:, t % 2, 5:6]))
                    W(); fin(e.scalar_tensor_tensor(lnst[:, t % 2, 7:8],
                                                    lnst[:, t % 2, 2:3], -1.0,
                                                    lnst[:, t % 2, 6:7],
                                                    ALU.mult, ALU.mult), S["lnr"])
                for cc in range(4):
                    if not ch_on:
                        break
                    c = 4 * g + cc
                    e.wait_ge(S["fc"], c + 1)
                    if c >= 1:
                        e.wait_ge(S["gc"], c)
                    W(); fin(e.tensor_copy(g0h[:, :], g0[:, :]))
                    W(); fin(e.tensor_sub(g0l[:, :], g0[:, :], g0h[:, :]),
                             S["ghl"])
                    if clvl >= 2:
                        e.wait_ge(S["st"], c + 1)
                        W(); fin(e.tensor_mul(
                            smaskT.rearrange("p a c -> p (a c)")[:, :],
                            pS[:, :], tri4_sb[:, :]), S["sm"])
                    e.wait_ge(S["eg_"], c + 1)
                    W(); fin(e.tensor_scalar(s_sb[:, :], eg[:, :], -1.0, 1.0,
                                             ALU.mult, ALU.add))
                    e.wait_ge(S["eng"], c + 1)
                    if c >= 2:
                        e.wait_ge(S["sps%d" % (c % 2)], 16 * (c // 2))
                    W(); fin(e.tensor_mul(snt[:, c % 2, :], s_sb[:, :],
                                          EnG[:, :]), S["sn"])
                    if clvl >= 2:
                        e.wait_ge(S["egt"], c + 1)
                        W(); fin(e.tensor_copy(egt_bf[:, :], egt_f[0:1, c % 2, :]))
                        ev = egt2_bf.rearrange("p (a t m) -> p a t m", a=4, t=2)
                        gv = egt_bf.rearrange("p (a m) -> p a m", a=4)
                        W(); fin(e.tensor_copy(ev[:, :, 0, :], gv[:, :, :]))
                        W(); fin(e.tensor_copy(ev[:, :, 1, :], gv[:, :, :]), S["egb"])
                    e.wait_ge(S["gc"], c + 1)
                    e.wait_ge(S["gcs_"], c + 1)
                    W(); fin(e.tensor_add(gacc[:, :], gacc[:, :], pGt[0:64, :]), S["gac"])
                    if c >= 2:
                        e.wait_ge(S["speg%d" % (c % 2)], 16 * (c // 2))
                    W(); fin(e.tensor_copy(dacc[:, c % 2, :], gacc[0:1, :]),
                             S["dac"])
                    if clvl >= 2:
                        e.wait_ge(S["eb"], c + 1)
                        W(); fin(e.tensor_copy(ecol[:, :], pec), S["ece"])
                        W(); fin(e.tensor_mul(s2[:, :], snt[:, c % 2, :],
                                              pEb256), S["s2d"])
                    if clvl >= 3:
                        e.wait_ge(S["lp"], c + 1)
                        if c >= 2:
                            e.wait_ge(S["spgl%d" % (c % 2)], 16 * (c // 2))
                        W(); fin(e.tensor_copy(lps[:, c % 2, :], pLp), S["lps_"])
                        e.wait_ge(S["dk"], c + 1)
                        W(); fin(e.tensor_add(hkt[:, :],
                                              hkl.rearrange("p a m -> p (a m)")[:, :],
                                              pdk))
                        W(); fin(e.tensor_mul(hkl.rearrange("p a m -> p (a m)")[:, :],
                                              hkt[:, :], pEb512), S["hk"])
                    if clvl >= 4:
                        e.wait_ge(S["dv"], c + 1)
                        e.wait_ge(S["ece"], c + 1)
                        for h in range(H):
                            W()
                            fin(e.scalar_tensor_tensor(
                                hvl[:, h, :], hvl[:, h, :], ecol[:, h:h + 1],
                                pdv(h),
                                ALU.mult, ALU.add), S["hv"] if h == 3 else None)
                    if clvl >= 4 and c < NCH - 1:
                        e.wait_ge(S["hv"], c + 1)
                        if c >= 3:
                            e.wait_ge(S["sphv%d" % ((c + 1) % 4)],
                                      16 * ((c - 3) // 4 + 1))
                        W()
                        fin(e.tensor_copy(hvsp[:, (c + 1) % 4, :],
                                          hvl.rearrange("p a m -> p (a m)")[:, :]),
                            S["hvc"])
                if prep:
                    _dve_prep(2 * g)
                    _dve_prep(2 * g + 1)
            if sm_on:
                e.wait_ge(S["hkTp"], 1)
                W(); fin(e.tensor_copy(hktT.rearrange("p a c -> p (a c)")[:, :],
                                       PSB[0:64, :]), S["hkT"])

        @block.tensor
        def _(e):
            e.wait_ge(S["wld"], 128)
            e.wait_ge(S["ini"], 1)
            for g in range(NG):
                if not pj_on:
                    break
                e.wait_ge(S["xt%d" % (g % 2)], 64 * (g // 2 + 1))
                if g >= 1:
                    e.wait_ge(S["qsp"], 16 * g)
                    if ch_on:
                        if clvl >= 2:
                            e.wait_ge(S["st"], 4 * g)
                            e.wait_ge(S["s2d"], 4 * g)
                        if clvl >= 3:
                            e.wait_ge(S["kt0"], 16 * 2 * g)
                            e.wait_ge(S["kt1"], 16 * 2 * g)
                            e.wait_ge(S["lps_"], 4 * g)
                            e.wait_ge(S["hk"], 4 * g)
                for idx in range(16):
                    if idx >= 2:
                        e.wait_ge(S["qke"], 16 * g + idx - 1)
                    wsb = wq_sb if idx < 8 else wk_sb
                    ft = idx % 8
                    for dt in range(8):
                        op = e.matmul(pj[idx % 2],
                                      wsb[:, dt, ft * 128:(ft + 1) * 128],
                                      xnT[:, g % 2, :, dt, :],
                                      start=(dt == 0), stop=(dt == 7))
                    op.then_inc(S["qkp"], 1)
                for cc in range(4):
                    if not ch_on:
                        break
                    c = 4 * g + cc
                    ts = slice(cc * 128, (cc + 1) * 128)
                    e.wait_ge(S["fc"], c)
                    e.wait_ge(S["egt"], c)
                    e.wait_ge(S["gac"], c)
                    e.wait_ge(S["gcs_"], c)
                    for dt in range(8):
                        op = e.matmul(pf, xnT[:, g % 2, cc, dt, :], wf_sb[:, dt, :],
                                      start=(dt == 0), stop=(dt == 7))
                    op.then_inc(S["fp"], 1)
                    if clvl < 2:
                        e.wait_ge(S["fc"], c + 1)
                        e.matmul(pGc, trin_sb[:, :], g0[:, :], start=True, stop=True)
                        op = e.matmul(pGt, neg8c[:, :], g0[:, :], start=True, stop=True)
                        op.then_inc(S["gc"], 1)
                        continue
                    e.wait_ge(S["ve"], c)
                    if clvl >= 4:
                        e.wait_ge(S["hv"], c)
                    for dt in range(8):
                        op = e.matmul(pvh[0][:, :],
                                      xnT[:, g % 2, cc, dt, :],
                                      wv_sb[:, dt, 0:512],
                                      start=(dt == 0), stop=(dt == 7))
                    e.wait_ge(S["ghl"], c + 1)
                    e.wait_ge(S["eng"], c)
                    e.wait_ge(S["gcs_"], c)
                    e.matmul(pGc, trin_sb[:, :], g0h[:, :], start=True, stop=False)
                    e.matmul(pGc, trin_sb[:, :], g0l[:, :], start=False, stop=True)
                    e.matmul(pGt, neg8c[:, :], g0h[:, :], start=True, stop=False)
                    op = e.matmul(pGt, neg8c[:, :], g0l[:, :], start=False, stop=True)
                    op.then_inc(S["gc"], 1)
                    for dt in range(8):
                        op = e.matmul(pvh[1][:, :],
                                      xnT[:, g % 2, cc, dt, :],
                                      wv_sb[:, dt, 512:1024],
                                      start=(dt == 0), stop=(dt == 7))
                    op.then_inc(S["vp"], 1)
                    e.wait_ge(S["sm"], c)
                    e.wait_ge(S["ece"], c)
                    e.wait_ge(S["qsc"], 8 * (g + 1))
                    e.wait_ge(S["qke"], 16 * (g + 1))
                    for h in range(H):
                        e.matmul(pS[:, h * 128:(h + 1) * 128],
                                 kT_sb[:, cc, 2 * h, :], qT_sb[:, cc, 2 * h, :],
                                 start=True, stop=False)
                        op = e.matmul(pS[:, h * 128:(h + 1) * 128],
                                      kT_sb[:, cc, 2 * h + 1, :],
                                      qT_sb[:, cc, 2 * h + 1, :],
                                      start=False, stop=True)
                    op.then_inc(S["st"], 1)
                    e.wait_ge(S["egb"], c + 1)
                    e.wait_ge(S["s2d"], c)
                    e.wait_ge(S["lps_"], c)
                    e.wait_ge(S["hk"], c)
                    e.wait_ge(S["sm"], c + 1)
                    e.matmul(pEb256, onesb[:, :], egt_bf[:, :], start=True, stop=True)
                    e.matmul(pEb512, onesb[:, :], egt2_bf[:, :], start=True, stop=True)
                    for h in range(H):
                        op = e.matmul(pec[:, h:h + 1],
                                      egt_bf[0:1, h * 64:(h + 1) * 64],
                                      onesb[0:1, 0:1], start=True, stop=True)
                    op.then_inc(S["eb"], 1)
                    if clvl < 3:
                        continue
                    e.wait_ge(S["sn"], c + 1)
                    e.wait_ge(S["s2d"], c + 1)
                    e.wait_ge(S["lps_"], c)
                    for h in range(H):
                        e.matmul(pLp[:, h * 64:(h + 1) * 64],
                                 qT_sb[:, cc, 2 * h, :], hkl[:, 2 * h, :],
                                 start=True, stop=False)
                        e.matmul(pLp[:, h * 64:(h + 1) * 64],
                                 qT_sb[:, cc, 2 * h + 1, :], hkl[:, 2 * h + 1, :],
                                 start=False, stop=False)
                        op = e.matmul(pLp[:, h * 64:(h + 1) * 64],
                                      smaskT[:, h, :],
                                      snt[:, c % 2, h * 64:(h + 1) * 64],
                                      start=False, stop=True)
                    op.then_inc(S["lp"], 1)
                    e.wait_ge(S["s2d"], c + 1)
                    e.wait_ge(S["kt%d" % (c % 2)], 16 * (c // 2 + 1))
                    for blk in range(8):
                        h = blk // 2
                        op = e.matmul(pdk[:, blk * 64:(blk + 1) * 64],
                                      k_tm[:, c % 2, blk * 128:(blk + 1) * 128],
                                      snt[:, c % 2, h * 64:(h + 1) * 64],
                                      start=True, stop=True)
                    op.then_inc(S["dk"], 1)
                    if clvl < 4:
                        continue
                    e.wait_ge(S["ve"], c + 1)
                    for h in range(H):
                        op = e.matmul(pdv(h),
                                      s2[:, h * 64:(h + 1) * 64],
                                      v_sb[:, c % 2, h * 256:(h + 1) * 256],
                                      start=True, stop=True)
                    op.then_inc(S["dv"], 1)
            if sm_on:
                e.wait_ge(S["hk"], NCH)
                for blk in range(8):
                    op = e.transpose(PSB[0:64, blk * 128:(blk + 1) * 128],
                                     hkl[:, blk, :], ident[:, :])
                op.then_inc(S["hkTp"], 1)



def _decl_l2_inputs(nc):
    d = {}
    d["Wo"] = nc.dram_tensor("Wo", [D, D], F32, kind="ExternalInput").ap()
    d["fw"] = nc.dram_tensor("fw", [D, D], F32, kind="ExternalInput").ap()
    d["fscc"] = nc.dram_tensor("fscc", [128, 8], F32, kind="ExternalInput").ap()
    d["gnwc"] = nc.dram_tensor("gnwc", [128, 2], F32, kind="ExternalInput").ap()
    d["fbias"] = nc.dram_tensor("fbias", [D], F32, kind="ExternalInput").ap()
    d["sel9"] = nc.dram_tensor("sel9", [9, 8], BF16, kind="ExternalInput").ap()
    d["negrow"] = nc.dram_tensor("negrow", [1, 256], F32, kind="ExternalInput").ap()
    return d


def build_l2():
    nc = bass.Bass("TRN2", target_bir_lowering=False, debug=False)
    d = _decl_l2_inputs(nc)
    d["xseg"] = nc.dram_tensor("xseg", [NT, D], F32, kind="ExternalInput").ap()
    d["triu_in"] = nc.dram_tensor("triu_in", [C, C], BF16, kind="ExternalInput").ap()
    d["tri4_in"] = nc.dram_tensor("tri4_in", [C, 512], BF16, kind="ExternalInput").ap()
    d["ident_in"] = nc.dram_tensor("ident_in", [C, C], BF16, kind="ExternalInput").ap()
    d["ag"] = nc.dram_tensor("ag", [512, 2304], F32, kind="ExternalInput").ap()
    for nm, shp, dt in L1_SPILLS:
        d[nm] = nc.dram_tensor(nm, shp, dt, kind="ExternalInput").ap()
    d["outseg"] = nc.dram_tensor("outseg", [NT, D], F32, kind="ExternalOutput").ap()
    _emit_l2(nc, d, cc=None)
    return nc


def _emit_l2(nc, dmap, cc=None):
    xseg = dmap["xseg"]; fbias = dmap["fbias"]; triu_in = dmap["triu_in"]
    tri4_in = dmap["tri4_in"]; ident_in = dmap["ident_in"]
    sel9 = dmap["sel9"]; negrow = dmap["negrow"]
    sp_wo = dmap["sp_wo"]; sp_fwT = dmap["sp_fwT"]
    ag = dmap["ag"]
    sp_q = dmap["sp_q"]; sp_xn = dmap["sp_xn"]; sp_s = dmap["sp_s"]
    sp_glp = dmap["sp_glp"]; sp_v = dmap["sp_v"]
    sp_hv = dmap["sp_hv"]; sp_da = dmap["sp_da"]; outseg = dmap["outseg"]

    with ExitStack() as ctx:
        E = ctx.enter_context
        wo_sb = E(nc.sbuf_tensor("wo_sb", [128, 8, 1024], BF16))
        fwT = E(nc.sbuf_tensor("fwT", [128, 8, 8, 128], BF16))
        fbi_r = E(nc.sbuf_tensor("fbi_r", [1, 1024], F32))
        fbi_b = E(nc.sbuf_tensor("fbi_b", [1, 1024], BF16))
        triu_sb = E(nc.sbuf_tensor("triu_sb2", [128, 128], BF16))
        tri4_sb = E(nc.sbuf_tensor("tri4_sb2", [128, 512], BF16))
        ident = E(nc.sbuf_tensor("ident2", [128, 128], BF16))
        onesb = E(nc.sbuf_tensor("onesb2", [1, 128], BF16))
        one1 = E(nc.sbuf_tensor("one1", [1, 1], BF16))
        epsb = E(nc.sbuf_tensor("epsb2", [128, 1], F32))
        sel_sb = E(nc.sbuf_tensor("sel_sb", [9, 8], BF16))
        gall = E(nc.sbuf_tensor("gall", [9, 256], F32))
        gallb = E(nc.sbuf_tensor("gallb", [9, 256], BF16))
        cfT4 = E(nc.sbuf_tensor("cfT4", [64, 4, 8], F32))
        hkr = E(nc.sbuf_tensor("hkr", [64, 2, 1024], F32))
        hvr = E(nc.sbuf_tensor("hvr", [64, 2, 1024], F32))
        hkiT = E(nc.sbuf_tensor("hkiT", [64, 1024], F32))
        hkiTb = E(nc.sbuf_tensor("hkiTb", [64, 8, 128], BF16))
        hvi = E(nc.sbuf_tensor("hvi", [64, 4, 256], F32))
        hki = E(nc.sbuf_tensor("hki", [128, 8, 64], BF16))
        qg = E(nc.sbuf_tensor("qg", [128, 2, 4, 8, 128], BF16))
        linit = E(nc.sbuf_tensor("linit", [128, 32, 256], BF16))
        xnT2 = E(nc.sbuf_tensor("xnT2", [128, 2, 4, 8, 128], BF16))
        glp2 = E(nc.sbuf_tensor("glp2", [128, 2, 512], F32))
        gc2 = glp2[:, :, 0:256]
        lp2 = glp2[:, :, 256:512]
        sn2 = E(nc.sbuf_tensor("sn2", [128, 3, 256], BF16))
        v2 = E(nc.sbuf_tensor("v2", [128, 2, 1024], BF16))
        hv2 = E(nc.sbuf_tensor("hv2", [64, 2, 1024], BF16))
        x2 = E(nc.sbuf_tensor("x2", [128, 2, 1024], F32))
        da2 = E(nc.sbuf_tensor("da2", [1, 4, 256], F32))
        dbf = E(nc.sbuf_tensor("dbf", [1, 256], BF16))
        dcol = E(nc.sbuf_tensor("dcol", [64, 4], F32))
        A_sb = E(nc.sbuf_tensor("A_sb", [128, 2, 256], F32))
        lscr = E(nc.sbuf_tensor("lscr", [128, 256], F32))
        lscr2 = E(nc.sbuf_tensor("lscr2", [128, 256], F32))
        lsub = E(nc.sbuf_tensor("lsub", [128, 256], F32))
        esm = E(nc.sbuf_tensor("esm", [128, 256], F32))
        nmax = E(nc.sbuf_tensor("nmax", [128, 4], F32))
        rsum = E(nc.sbuf_tensor("rsum", [128, 4], F32))
        rcp = E(nc.sbuf_tensor("rcp", [128, 4], F32))
        pd_sb = E(nc.sbuf_tensor("pd_sb", [128, 2, 256], BF16))
        tp_sb = E(nc.sbuf_tensor("tp_sb", [64, 8, 128], BF16))
        wm_sb = E(nc.sbuf_tensor("wm_sb", [128, 4, 128], BF16))
        hv0 = E(nc.sbuf_tensor("hv0", [64, 2, 4, 256], BF16))
        sqscr = E(nc.sbuf_tensor("sqscr", [128, 256], F32))
        ssum = E(nc.sbuf_tensor("ssum", [128, 4], F32))
        rstd = E(nc.sbuf_tensor("rstd", [128, 4], F32))
        o_n = E(nc.sbuf_tensor("o_n", [128, 1024], BF16))
        onT = E(nc.sbuf_tensor("onT", [128, 2, 8, 128], BF16))
        ffnc = E(nc.sbuf_tensor("ffnc", [128, 4, 1024], F32))
        out_sb = E(nc.sbuf_tensor("out_sb", [128, 2, 1024], F32))

        pDbc = E(nc.psum_tensor("pDbc", [128, 512], F32))
        pWc = E(nc.psum_tensor("pWc", [128, 512], F32))
        poA = E(nc.psum_tensor("poA", [128, 512], F32))
        poB = E(nc.psum_tensor("poB", [128, 512], F32))
        pfat = E(nc.psum_tensor("pfat", [128, 1024], F32))
        PSB = E(nc.psum_tensor("PSB2", [128, 1024], BF16))
        pLib = E(nc.psum_tensor("pLib", [128, 512], F32))
        pDb = pDbc[:, 0:256]
        pDc = pDbc[0:64, 256:260]
        pLi = pLib[:, 0:256]
        pGsT = pWc[0:64, 0:32]
        pHkT = PSB[:, 0:512]
        pW = pWc[:, :]
        pfa = pfat[:, :]

        def po(h):
            t = poA if h < 2 else poB
            return t[:, (h % 2) * 256:(h % 2) * 256 + 256]

        S = _sems(nc, E, [
            "ldc", "ldw", "ini2", "galb", "galL", "gst", "cft", "cmb", "hkib",
            "hkitp", "hki_",
            "fbi", "li", "lie",
            "aexp", "lnd", "dbf_", "pdm",
            "dcol_", "lt", "lass", "nmax_", "lsub_", "e_", "rcp_", "pd",
            "T_", "tp", "wt", "wm", "hv0_", "o_", "oev", "sq", "rstd_", "on",
            "ffn", "ffnc_", "wo", "out1", "out2",
            "ldr0", "ldr1", "qg0", "qg1", "xng0", "xng1",
            "v2d0", "v2d1", "hv2d0", "hv2d1", "x2d0", "x2d1",
            "gld0", "gld1", "sn2d0", "sn2d1", "sn2d2",
            "egd0", "egd1", "egd2", "egd3", "ont0", "ont1", "ost0", "ost1",
            "ccd", "slfp2", "slfa2", "slfv2"])
        SEMCNT = {}

        block = E(nc.Block())

        @block.sync
        def _(e):
            e.dma_start(out=wo_sb[:, :, :],
                        in_=sp_wo.rearrange("(a p) f -> p a f", p=128)
                        ).then_inc(S["ldw"], 16)
            e.dma_start(out=fwT.rearrange("p a b c -> p a (b c)"),
                        in_=sp_fwT.rearrange("(a p) f -> p a f", p=128)
                        ).then_inc(S["ldw"], 16)
            e.dma_start(out=triu_sb[:, :], in_=triu_in[:, :]).then_inc(S["ldc"], 16)
            e.dma_start(out=tri4_sb[:, :], in_=tri4_in[:, :]).then_inc(S["ldc"], 16)
            e.dma_start(out=ident[:, :], in_=ident_in[:, :]).then_inc(S["ldc"], 16)
            e.dma_start(out=sel_sb[:, :], in_=sel9[:, :]).then_inc(S["ldc"], 16)
            e.dma_start(out=fbi_r[:, :],
                        in_=fbias.rearrange("(a d) -> a d", a=1)).then_inc(S["ldc"], 16)
            if cc is not None:
                e.wait_ge(S["ccd"], 1)
            e.dma_start(out=gall[0:8, :],
                        in_=ag.rearrange("(r i) c -> r i c", i=64)[:, 0, 2048:2304]
                        ).then_inc(S["galL"], 16)
            e.dma_start(out=gall[8:9, :], in_=negrow[:, :]).then_inc(S["galL"], 16)
            for r in range(8):
                if r >= 2:
                    e.wait_ge(S["cmb"], r - 1)
                e.dma_start(out=hkr[:, r % 2, :],
                            in_=ag[r * 64:r * 64 + 64, 0:1024]).then_inc(S["ldr%d" % (r % 2)], 16)
                e.dma_start(out=hvr[:, r % 2, :],
                            in_=ag[r * 64:r * 64 + 64, 1024:2048]).then_inc(S["ldr%d" % (r % 2)], 16)
            def _qgload(g):
                e.dma_start(out=qg[:, g % 2, :, :, :].rearrange(
                                "p t a c -> p (t a c)").rearrange(
                                "p (a c) -> p a c", a=8),
                            in_=sp_q[g].rearrange("p (a c) -> p a c", a=8)
                            ).then_inc(S["qg%d" % (g % 2)], 16)

            _qgload(0)
            _qgload(1)
            for c in range(NCH):
                g = c // 4
                if c % 4 == 0:
                    if g >= 2:
                        e.wait_ge(S["ffn"], 4 * (g - 1))
                    e.dma_start(out=xnT2[:, g % 2, :, :, :].rearrange(
                                    "p t a c -> p (t a c)").rearrange(
                                    "p (a c) -> p a c", a=8),
                                in_=sp_xn[g].rearrange("p (a c) -> p a c", a=8)
                                ).then_inc(S["xng%d" % (g % 2)], 16)
                    if c >= 4 and g + 1 < NG:
                        e.wait_ge(S["li"], c)
                        _qgload(g + 1)
                if c >= 2:
                    e.wait_ge(S["o_"], c - 1)
                e.dma_start(out=v2[:, c % 2, :], in_=sp_v[c]).then_inc(S["v2d%d" % (c % 2)], 16)
                if c >= 2:
                    e.wait_ge(S["hv0_"], c - 1)
                e.dma_start(out=hv2[:, c % 2, :], in_=sp_hv[c]).then_inc(S["hv2d%d" % (c % 2)], 16)
                if c >= 2:
                    e.wait_ge(S["out1"], c - 1)
                e.dma_start(out=x2[:, c % 2, :],
                            in_=xseg[c * 128:(c + 1) * 128, :]).then_inc(S["x2d%d" % (c % 2)], 16)
                if c >= 3:
                    e.wait_ge(S["out2"], c - 2)
                    e.dma_start(out=outseg[(c - 3) * 128:(c - 2) * 128, :],
                                in_=out_sb[:, (c - 3) % 2, :]
                                ).then_inc(S["ost%d" % ((c - 3) % 2)], 16)
            for ct in (NCH - 3, NCH - 2, NCH - 1):
                e.wait_ge(S["out2"], ct + 1)
                e.dma_start(out=outseg[ct * 128:(ct + 1) * 128, :],
                            in_=out_sb[:, ct % 2, :]).then_inc(S["ost%d" % (ct % 2)], 16)
            e.wait_ge(S["ost0"], 16 * 16)
            e.wait_ge(S["ost1"], 16 * 16)

        @block.gpsimd
        def _(e):
            if cc is not None:
                e.collective_compute(
                    "AllGather", ALU.bypass,
                    replica_groups=[list(range(NCORES))],
                    ins=[cc[0]], outs=[cc[1]],
                ).then_inc(S["ccd"])
            W, fin = _serial(e, S["slfp2"], SEMCNT)
            fin(e.memset(onesb[:, :], 1.0))
            W(); fin(e.memset(one1[:, :], 1.0))
            W(); fin(e.memset(epsb[:, :], EPS))
            W(); fin(e.memset(hkiT[:, :], 0.0))
            W(); fin(e.memset(hvi[:, :, :], 0.0))
            W(); fin(e.memset(da2[:, :, :], 0.0), S["ini2"])
            for c in range(NCH):
                if c >= 3:
                    t3 = c - 3
                    e.wait_ge(S["out1"], t3 + 1)
                    e.wait_ge(S["ffnc_"], t3 + 1)
                    W(); fin(e.tensor_add(out_sb[:, t3 % 2, :],
                                          out_sb[:, t3 % 2, :],
                                          ffnc[:, t3 % 4, :]), S["out2"])
            for t3 in (NCH - 3, NCH - 2, NCH - 1):
                e.wait_ge(S["out1"], t3 + 1)
                e.wait_ge(S["ffnc_"], t3 + 1)
                W(); fin(e.tensor_add(out_sb[:, t3 % 2, :],
                                      out_sb[:, t3 % 2, :],
                                      ffnc[:, t3 % 4, :]), S["out2"])

        @block.scalar
        def _(e):
            W, fin = _serial(e, S["slfa2"], SEMCNT)
            e.wait_ge(S["ini2"], 1)
            e.wait_ge(S["gst"], 1)
            W(); fin(e.activation(cfT4.rearrange("p a r -> p (a r)")[:, :],
                                  pGsT, AF.Exp), S["cft"])
            e.dma_start(out=da2[:, 0, :], in_=sp_da[0]).then_inc(S["egd0"], 16)
            for c0 in range(2):
                e.dma_start(out=glp2[:, c0 % 2, :],
                            in_=sp_glp[c0]).then_inc(S["gld%d" % c0], 16)
                e.dma_start(out=sn2[:, c0 % 3, :], in_=sp_s[c0]).then_inc(S["sn2d%d" % c0], 16)

            e.wait_ge(S["li"], 1)
            W(); fin(e.activation(linit[:, 0, :], pLi, AF.Identity), S["lie"])

            def _act_tail(t):
                e.wait_ge(S["o_"], t + 1)
                for h in range(H):
                    W()
                    fin(e.activation(sqscr[:, :], po(h),
                                     AF.Square, accum_out=ssum[:, h:h + 1]),
                        S["sq"])
                W(); fin(e.activation(rstd[:, :], ssum[:, :], AF.Ln,
                                      scale=1.0 / 256.0, bias=epsb[:, 0:1]))
                W(); fin(e.activation(rstd[:, :], rstd[:, :], AF.Exp,
                                      scale=-0.5), S["rstd_"])
                if t >= 1:
                    e.wait_ge(S["ont%d" % ((t - 1) % 2)], 16 * ((t - 1) // 2 + 1))
                for h in range(H):
                    W()
                    fin(e.activation(o_n[:, h * 256:(h + 1) * 256], po(h),
                                     AF.Identity, scale=rstd[:, h:h + 1]),
                        S["on"] if h == 3 else None)
                e.wait_ge(S["on"], t + 1)
                if t >= 2:
                    e.wait_ge(S["wo"], t - 1)
                e.dma_start(out=onT[:, t % 2, :, :],
                            in_=o_n.rearrange("p (a c) -> p a c", a=8),
                            transpose=True).then_inc(S["ont%d" % (t % 2)], 16)

            W(); fin(e.activation(dbf[:, :], da2[:, 3, :], AF.Exp), S["dbf_"])
            for c in range(NCH):
                if c + 1 < NCH:
                    e.wait_ge(S["li"], c + 2)
                    W(); fin(e.activation(linit[:, c + 1, :], pLi,
                                          AF.Identity), S["lie"])
                e.wait_ge(S["gld%d" % (c % 2)], 16 * (c // 2 + 1))
                if c >= 2:
                    e.wait_ge(S["pd"], c - 1)
                W(); fin(e.activation(A_sb[:, c % 2, :], gc2[:, c % 2, :],
                                      AF.Exp), S["aexp"])
                e.wait_ge(S["lsub_"], c + 1)
                for h in range(H):
                    W()
                    fin(e.activation(esm[:, h * 64:(h + 1) * 64],
                                     lsub[:, h * 64:(h + 1) * 64], AF.Exp,
                                     accum_out=rsum[:, h:h + 1]),
                        S["e_"] if h == 3 else None)
                if c >= 1:
                    _act_tail(c - 1)
                if c + 2 < NCH:
                    cn = c + 2
                    e.wait_ge(S["lass"], c + 1)
                    e.dma_start(out=glp2[:, cn % 2, :],
                                in_=sp_glp[cn]).then_inc(S["gld%d" % (cn % 2)], 16)
                    e.wait_ge(S["T_"], cn - 2)
                    e.dma_start(out=sn2[:, cn % 3, :], in_=sp_s[cn]).then_inc(S["sn2d%d" % (cn % 3)], 16)
                    e.dma_start(out=da2[:, (cn - 1) % 4, :],
                                in_=sp_da[cn - 1]).then_inc(S["egd%d" % ((cn - 1) % 4)], 16)
                if c + 1 < NCH:
                    e.wait_ge(S["egd%d" % (c % 4)], 16 * (c // 4 + 1))
                    e.wait_ge(S["pdm"], c + 1)
                    W(); fin(e.activation(dbf[:, :], da2[:, c % 4, :],
                                          AF.Exp), S["dbf_"])
            _act_tail(NCH - 1)

        @block.vector
        def _(e):
            W, fin = _serial(e, S["slfv2"], SEMCNT)
            e.wait_ge(S["ldc"], 80)
            fin(e.tensor_copy(fbi_b[:, :], fbi_r[:, :]), S["fbi"])
            e.wait_ge(S["galL"], 32)
            W(); fin(e.tensor_copy(gallb[:, :], gall[:, :]), S["galb"])
            e.wait_ge(S["cft"], 1)
            for r in range(8):
                e.wait_ge(S["ldr%d" % (r % 2)], 32 * (r // 2 + 1))
                for blk in range(8):
                    hh = blk // 2
                    W()
                    fin(e.scalar_tensor_tensor(
                        hkiT[:, blk * 128:(blk + 1) * 128],
                        hkr[:, r % 2, blk * 128:(blk + 1) * 128],
                        cfT4[:, hh, r:r + 1],
                        hkiT[:, blk * 128:(blk + 1) * 128],
                        ALU.mult, ALU.add))
                for hh in range(H):
                    W()
                    fin(e.scalar_tensor_tensor(
                        hvi[:, hh, :], hvr[:, r % 2, hh * 256:(hh + 1) * 256],
                        cfT4[:, hh, r:r + 1], hvi[:, hh, :],
                        ALU.mult, ALU.add), S["cmb"] if hh == 3 else None)
            W(); fin(e.tensor_copy(hkiTb.rearrange("p a c -> p (a c)")[:, :],
                                   hkiT[:, :]), S["hkib"])
            e.wait_ge(S["hkitp"], 1)
            W(); fin(e.tensor_copy(hki.rearrange("p a m -> p (a m)")[:, :],
                                   pHkT), S["hki_"])
            def _dve_clip(t):
                e.wait_ge(S["ffn"], t + 1)
                if t >= 4:
                    e.wait_ge(S["out2"], t - 3)
                W(); fin(e.tensor_scalar(ffnc[:, t % 4, :], pfa, 100.0, -100.0,
                                         ALU.min, ALU.max), S["ffnc_"])

            def _dve_tp(t):
                e.wait_ge(S["T_"], t + 1)
                W(); fin(e.tensor_copy(tp_sb.rearrange("p a c -> p (a c)")[:, :],
                                       PSB[0:64, :]), S["tp"])

            def _dve_wm(t):
                e.wait_ge(S["wt"], t + 1)
                W(); fin(e.tensor_mul(wm_sb.rearrange("p a c -> p (a c)")[:, :],
                                      pW[:, :], tri4_sb[:, :]), S["wm"])

            def _dve_tail2(t):
                e.wait_ge(S["wo"], t + 1)
                e.wait_ge(S["x2d%d" % (t % 2)], 16 * (t // 2 + 1))
                if t >= 2:
                    e.wait_ge(S["ost%d" % (t % 2)], 16 * (t // 2))
                W(); fin(e.tensor_add(out_sb[:, t % 2, :], x2[:, t % 2, :],
                                      pfa), S["out1"])

            for c in range(NCH):
                if c >= 1:
                    _dve_clip(c - 1)
                e.wait_ge(S["pdm"], c + 1)
                W(); fin(e.tensor_copy(dcol[:, :], pDc), S["dcol_"])
                e.wait_ge(S["lie"], c + 1)
                W(); fin(e.tensor_mul(lscr[:, :], linit[:, c, :], pDb), S["lt"])
                e.wait_ge(S["gld%d" % (c % 2)], 16 * (c // 2 + 1))
                W(); fin(e.tensor_add(lscr2[:, :], lscr[:, :], lp2[:, c % 2, :]))
                if c >= 1:
                    _dve_tp(c - 1)
                if c >= 2:
                    _dve_tail2(c - 2)
                e.wait_ge(S["aexp"], c + 1)
                W(); fin(e.tensor_mul(lscr2[:, :], lscr2[:, :], A_sb[:, c % 2, :]),
                         S["lass"])
                W(); fin(e.reduce_max(nmax[:, :],
                                      lscr2.rearrange("p (s m) -> p s m", s=4),
                                      axis=AX.X, negate=True), S["nmax_"])
                if c >= 1:
                    _dve_wm(c - 1)
                for h in range(H):
                    W()
                    fin(e.tensor_scalar_add(lsub[:, h * 64:(h + 1) * 64],
                                            lscr2[:, h * 64:(h + 1) * 64],
                                            nmax[:, h:h + 1]),
                        S["lsub_"] if h == 3 else None)
                e.wait_ge(S["e_"], c + 1)
                W(); fin(e.reciprocal(rcp[:, :], rsum[:, :]), S["rcp_"])
                if c >= 3:
                    e.wait_ge(S["T_"], c - 1)
                for h in range(H):
                    W()
                    fin(e.scalar_tensor_tensor(pd_sb[:, c % 2, h * 64:(h + 1) * 64],
                                               esm[:, h * 64:(h + 1) * 64],
                                               rcp[:, h:h + 1],
                                               A_sb[:, c % 2, h * 64:(h + 1) * 64],
                                               ALU.mult, ALU.mult),
                        S["pd"] if h == 3 else None)
                if c >= 2:
                    e.wait_ge(S["o_"], c - 1)
                e.wait_ge(S["hv2d%d" % (c % 2)], 16 * (c // 2 + 1))
                for h in range(H):
                    W()
                    fin(e.scalar_tensor_tensor(hv0[:, c % 2, h, :], hvi[:, h, :],
                                               dcol[:, h:h + 1],
                                               hv2[:, c % 2, h * 256:(h + 1) * 256],
                                               ALU.mult, ALU.add),
                        S["hv0_"] if h == 3 else None)
            _dve_clip(NCH - 1)
            _dve_tp(NCH - 1)
            _dve_wm(NCH - 1)
            _dve_tail2(NCH - 2)
            _dve_tail2(NCH - 1)

        @block.tensor
        def _(e):
            e.wait_ge(S["galb"], 1)
            e.wait_ge(S["ldc"], 80)
            for h in range(H):
                op = e.matmul(pGsT[:, h * 8:(h + 1) * 8],
                              gallb[:, h * 64:(h + 1) * 64], sel_sb[:, :],
                              start=True, stop=True)
            op.then_inc(S["gst"], 1)
            e.wait_ge(S["hkib"], 1)
            for blk in range(8):
                op = e.transpose(pHkT[:, blk * 64:(blk + 1) * 64],
                                 hkiTb[:, blk, :], ident[0:64, 0:64])
            op.then_inc(S["hkitp"], 1)
            def _linit(c2):
                g2 = c2 // 4
                cc2 = c2 % 4
                if c2 == 0:
                    e.wait_ge(S["hki_"], 1)
                e.wait_ge(S["qg%d" % (g2 % 2)], 16 * (g2 // 2 + 1))
                e.wait_ge(S["lie"], c2)
                for h in range(H):
                    e.matmul(pLi[:, h * 64:(h + 1) * 64],
                             qg[:, g2 % 2, cc2, 2 * h, :],
                             hki[:, 2 * h, :], start=True, stop=False)
                    op = e.matmul(pLi[:, h * 64:(h + 1) * 64],
                                  qg[:, g2 % 2, cc2, 2 * h + 1, :],
                                  hki[:, 2 * h + 1, :], start=False, stop=True)
                op.then_inc(S["li"], 1)

            _linit(0)
            _linit(1)

            def _pe_d(p):
                e.wait_ge(S["dbf_"], p + 1)
                e.wait_ge(S["lt"], p)
                e.wait_ge(S["dcol_"], p)
                e.matmul(pDb, onesb[:, :], dbf[:, :], start=True, stop=True)
                for h in range(H):
                    op = e.matmul(pDc[:, h:h + 1], dbf[0:1, h * 64:(h + 1) * 64],
                                  one1[:, :], start=True, stop=True)
                op.then_inc(S["pdm"], 1)

            def _pe_ffn(p, half):
                g = p // 4
                cc = p % 4
                if half == 0:
                    e.wait_ge(S["xng%d" % (g % 2)], 16 * (g // 2 + 1))
                    e.wait_ge(S["ldw"], 32)
                    e.wait_ge(S["fbi"], 1)
                    e.wait_ge(S["ffnc_"], p)
                    if p >= 1:
                        e.wait_ge(S["out1"], p - 1)
                for dt in range(8):
                    e.matmul(pfa[:, half * 512:(half + 1) * 512],
                             xnT2[:, g % 2, cc, dt, :],
                             fwT[:, half * 4:(half + 1) * 4, dt, :],
                             start=(dt == 0), stop=False)
                op = e.matmul(pfa[:, half * 512:(half + 1) * 512],
                              onesb[:, :],
                              fbi_b[0:1, half * 512:(half + 1) * 512],
                              start=False, stop=True)
                if half == 1:
                    op.then_inc(S["ffn"], 1)

            def _pe_T(p):
                e.wait_ge(S["pd"], p + 1)
                e.wait_ge(S["sn2d%d" % (p % 3)], 16 * (p // 3 + 1))
                e.wait_ge(S["tp"], p)
                if p == 0:
                    e.wait_ge(S["hki_"], 1)
                for h in range(H):
                    e.transpose(PSB[0:64, h * 128:(h + 1) * 128],
                                sn2[:, p % 3, h * 64:(h + 1) * 64], ident[:, :])
                for h in range(H):
                    op = e.transpose(PSB[0:64, 512 + h * 128:512 + (h + 1) * 128],
                                     pd_sb[:, p % 2, h * 64:(h + 1) * 64],
                                     ident[:, :])
                op.then_inc(S["T_"], 1)

            def _pe_wt(p):
                e.wait_ge(S["tp"], p + 1)
                e.wait_ge(S["wm"], p)
                for h in range(H):
                    op = e.matmul(pW[:, h * 128:(h + 1) * 128],
                                  tp_sb[:, h, :], tp_sb[:, 4 + h, :],
                                  start=True, stop=True)
                op.then_inc(S["wt"], 1)

            def _pe_o(p):
                e.wait_ge(S["wm"], p + 1)
                e.wait_ge(S["hv0_"], p + 1)
                e.wait_ge(S["v2d%d" % (p % 2)], 16 * (p // 2 + 1))
                if p >= 1:
                    e.wait_ge(S["on"], p)
                for h in range(H):
                    e.matmul(po(h),
                             wm_sb[:, h, :], v2[:, p % 2, h * 256:(h + 1) * 256],
                             start=True, stop=False)
                    op = e.matmul(po(h),
                                  tp_sb[:, 4 + h, :], hv0[:, p % 2, h, :],
                                  start=False, stop=True)
                op.then_inc(S["o_"], 1)

            def _pe_wo(t):
                e.wait_ge(S["ont%d" % (t % 2)], 16 * (t // 2 + 1))
                e.wait_ge(S["ffnc_"], min(t + 2, NCH))
                e.wait_ge(S["out1"], t)
                for vt in range(8):
                    for half in range(2):
                        op = e.matmul(pfa[:, half * 512:(half + 1) * 512],
                                      onT[:, t % 2, vt, :],
                                      wo_sb[:, vt, half * 512:(half + 1) * 512],
                                      start=(vt == 0), stop=(vt == 7))
                op.then_inc(S["wo"], 1)

            for c in range(NCH):
                _pe_d(c)
                if c >= 1:
                    _pe_T(c - 1)
                if c >= 2:
                    _pe_wo(c - 2)
                if c >= 1:
                    _pe_wt(c - 1)
                    _pe_o(c - 1)
                _pe_ffn(c, 0)
                _pe_ffn(c, 1)
                if c + 2 < NCH:
                    _linit(c + 2)
            _pe_T(NCH - 1)
            _pe_wo(NCH - 2)
            _pe_wt(NCH - 1)
            _pe_o(NCH - 1)
            _pe_wo(NCH - 1)


def build_fused():
    nc = bass.Bass("TRN2", target_bir_lowering=False, debug=False)
    d1 = _decl_l1_inputs(nc)
    d2 = _decl_l2_inputs(nc)
    d = {**d1, **d2}
    d["xseg"] = d1["xseg"]
    d["triu_in"] = d1["triu_in"]
    d["ident_in"] = d1["ident_in"]
    for nm, shp, dt in L1_SPILLS:
        d[nm] = nc.dram_tensor(nm, shp, dt).ap()
    summ_h = nc.dram_tensor("summ_int", [64, 2304], F32)
    ag_h = nc.dram_tensor("ag_int", [512, 2304], F32)
    d["summ"] = summ_h.ap()
    d["ag"] = ag_h.ap()
    d["outseg"] = nc.dram_tensor("outseg", [NT, D], F32,
                                 kind="ExternalOutput").ap()
    _emit_l1(nc, d)
    nc.reset()
    _emit_l2(nc, d, cc=(summ_h.ap().opt(), ag_h.ap().opt()))
    return nc



def _host_consts():
    j = np.arange(C)[:, None]
    i = np.arange(C)[None, :]
    tri_ji = (j <= i).astype(np.float32)
    trineg = np.ascontiguousarray((tri_ji * np.float32(NEG8)).astype(ml_dtypes.bfloat16))
    triu_b = np.ascontiguousarray(tri_ji.astype(ml_dtypes.bfloat16))
    tri4_b = np.ascontiguousarray(np.tile(tri_ji, (1, 4)).astype(ml_dtypes.bfloat16))
    ident = np.ascontiguousarray(np.eye(C, dtype=ml_dtypes.bfloat16))
    return trineg, triu_b, tri4_b, ident


def _sel9(core):
    b0 = 4 * (core // 4)
    rank = core
    sel = np.zeros((9, 8), np.float32)
    for r in range(8):
        if b0 <= r < rank:
            for u in range(8):
                if r < u < rank:
                    sel[u, r] = 1.0
        else:
            sel[8, r] = 1.0
    return np.ascontiguousarray(sel.astype(ml_dtypes.bfloat16))


_NCF = None


def kernel(**inputs):
    global _NCF
    x = np.ascontiguousarray(np.asarray(inputs["L0_attn_x"], np.float32))
    B, T, _ = x.shape
    xf = x.reshape(NCORES, NT, D)
    trineg, triu_b, tri4_b, ident = _host_consts()
    negrow = np.full((1, 256), -1e9, np.float32)

    if _NCF is None:
        _NCF = build_fused()
    ins = []
    for cidx in range(NCORES):
        ins.append({
            "xseg": np.ascontiguousarray(xf[cidx]),
            "Wq": np.asarray(inputs["L0_attn_Wq"], np.float32),
            "Wk": np.asarray(inputs["L0_attn_Wk"], np.float32),
            "Wv": np.asarray(inputs["L0_attn_Wv"], np.float32),
            "Wf": np.asarray(inputs["L0_attn_Wf"], np.float32),
            "Wo": np.asarray(inputs["L0_attn_Wo"], np.float32),
            "gnwc": np.ascontiguousarray(
                np.asarray(inputs["L0_attn_g_norm_w"], np.float32
                           ).reshape(2, 128).T),
            "fw": np.asarray(inputs["L0_ffn_weights"], np.float32),
            "fscc": np.ascontiguousarray(
                np.asarray(inputs["L0_ffn_scale"], np.float32
                           ).reshape(8, 128).T),
            "fbias": np.asarray(inputs["L0_ffn_bias"], np.float32),
            "trineg": trineg, "triu_in": triu_b, "tri4_in": tri4_b,
            "ident_in": ident,
            "sel9": _sel9(cidx), "negrow": negrow,
        })
    res = run_bass_kernel_spmd(_NCF, ins, core_ids=list(range(NCORES)))
    out = np.concatenate([res.results[cidx]["outseg"][None]
                          for cidx in range(NCORES)], axis=0)
    return np.ascontiguousarray(out.reshape(B, T, D).astype(np.float32))


# version 38
# speedup vs baseline: 1.0399x; 1.0399x over previous
import numpy as np
import ml_dtypes
from contextlib import ExitStack

import concourse.bass as bass
import concourse.mybir as mybir
from concourse.bass_utils import run_bass_kernel_spmd

F32 = mybir.dt.float32
BF16 = mybir.dt.bfloat16
AF = mybir.ActivationFunctionType
ALU = mybir.AluOpType
AX = mybir.AxisListType

NCORES = 8
NT = 4096
C = 128
NCH = 32
NG = 8
H = 4
D = 1024
EPS = 1e-5
SCL = 0.0625
NEG8 = -0.125


def _sems(nc, E, names):
    return {n: E(nc.semaphore(n)) for n in names}


def _serial(e, selfsem, semcnt):
    last = [None]

    def W():
        if last[0] is not None:
            e.wait_ge(last[0][0], last[0][1])

    def fin(ins, inc=None, n=1):
        sem = selfsem if inc is None else inc
        v = 1 if inc is None else n
        ins.then_inc(sem, v)
        semcnt[sem] = semcnt.get(sem, 0) + v
        last[0] = (sem, semcnt[sem])
        return ins

    return W, fin



L1_STAGES = {"ln", "proj", "chunk", "summ"}

L1_SPILLS = [("sp_q", [NG, 128, 4096], BF16), ("sp_xn", [NG, 128, 4096], BF16),
             ("sp_s", [NCH, 128, 256], BF16),
             ("sp_glp", [NCH, 128, 512], F32),
             ("sp_v", [NCH, 128, 1024], BF16),
             ("sp_hv", [NCH, 64, 1024], BF16), ("sp_eg", [NCH, 1, 256], F32),
             ("sp_wo", [1024, 1024], BF16), ("sp_fwT", [1024, 1024], BF16)]


def _decl_l1_inputs(nc):
    d = {}
    d["xseg"] = nc.dram_tensor("xseg", [NT, D], F32, kind="ExternalInput").ap()
    d["Wq"] = nc.dram_tensor("Wq", [D, D], F32, kind="ExternalInput").ap()
    d["Wk"] = nc.dram_tensor("Wk", [D, D], F32, kind="ExternalInput").ap()
    d["Wv"] = nc.dram_tensor("Wv", [D, D], F32, kind="ExternalInput").ap()
    d["Wf"] = nc.dram_tensor("Wf", [D, 256], F32, kind="ExternalInput").ap()
    d["trineg"] = nc.dram_tensor("trineg", [C, C], BF16, kind="ExternalInput").ap()
    d["triu_in"] = nc.dram_tensor("triu_in", [C, C], BF16, kind="ExternalInput").ap()
    d["tri4_in"] = nc.dram_tensor("tri4_in", [C, 512], BF16, kind="ExternalInput").ap()
    d["ident_in"] = nc.dram_tensor("ident_in", [C, C], BF16, kind="ExternalInput").ap()
    return d


def build_l1(stages=None, clvl=4):
    nc = bass.Bass("TRN2", target_bir_lowering=False, debug=False)
    d = _decl_l1_inputs(nc)
    d["Wo"] = nc.dram_tensor("Wo", [D, D], F32, kind="ExternalInput").ap()
    d["fw"] = nc.dram_tensor("fw", [D, D], F32, kind="ExternalInput").ap()
    d["fscc"] = nc.dram_tensor("fscc", [128, 8], F32, kind="ExternalInput").ap()
    d["gnwc"] = nc.dram_tensor("gnwc", [128, 2], F32, kind="ExternalInput").ap()
    for nm, shp, dt in L1_SPILLS:
        d[nm] = nc.dram_tensor(nm, shp, dt, kind="ExternalOutput").ap()
    d["summ"] = nc.dram_tensor("summ", [64, 2304], F32, kind="ExternalOutput").ap()
    _emit_l1(nc, d, stages, clvl)
    return nc


def _emit_l1(nc, dmap, stages=None, clvl=4):
    stages = L1_STAGES if stages is None else stages
    ln_on = "ln" in stages
    pj_on = "proj" in stages
    ch_on = "chunk" in stages
    sm_on = "summ" in stages and clvl >= 4
    xseg = dmap["xseg"]; Wq = dmap["Wq"]; Wk = dmap["Wk"]; Wv = dmap["Wv"]
    Wf = dmap["Wf"]; trineg = dmap["trineg"]; triu_in = dmap["triu_in"]
    tri4_in = dmap["tri4_in"]; ident_in = dmap["ident_in"]
    sp_q = dmap["sp_q"]; sp_xn = dmap["sp_xn"]; sp_s = dmap["sp_s"]
    sp_glp = dmap["sp_glp"]; sp_v = dmap["sp_v"]
    sp_hv = dmap["sp_hv"]; sp_eg = dmap["sp_eg"]; summ = dmap["summ"]
    prep = "Wo" in dmap
    if prep:
        Wo = dmap["Wo"]; fw = dmap["fw"]
        fscc_in = dmap["fscc"]; gnwc_in = dmap["gnwc"]
        sp_wo = dmap["sp_wo"]; sp_fwT = dmap["sp_fwT"]

    with ExitStack() as ctx:
        E = ctx.enter_context
        xin = E(nc.sbuf_tensor("xin", [128, 4, 1024], F32))
        lnst = E(nc.sbuf_tensor("lnst", [128, 2, 8], F32))
        sqs = E(nc.sbuf_tensor("sqs", [128, 1024], F32))
        xn_tm = E(nc.sbuf_tensor("xn_tm", [128, 4, 1024], BF16))
        xnT = E(nc.sbuf_tensor("xnT", [128, 2, 4, 8, 128], BF16))
        wq_sb = E(nc.sbuf_tensor("wq_sb", [128, 8, 1024], BF16))
        wk_sb = E(nc.sbuf_tensor("wk_sb", [128, 8, 1024], BF16))
        wv_sb = E(nc.sbuf_tensor("wv_sb", [128, 8, 1024], BF16))
        wf_sb = E(nc.sbuf_tensor("wf_sb", [128, 8, 256], BF16))
        qT_sb = E(nc.sbuf_tensor("qT_sb", [128, 4, 8, 128], BF16))
        kT_sb = E(nc.sbuf_tensor("kT_sb", [128, 4, 8, 128], BF16))
        qsil = E(nc.sbuf_tensor("qsil", [128, 2, 512], F32))
        k_tm = E(nc.sbuf_tensor("k_tm", [128, 3, 1024], BF16))
        v_sb = E(nc.sbuf_tensor("v_sb", [128, 2, 1024], BF16))
        fA = E(nc.sbuf_tensor("fA", [128, 256], F32))
        fB = E(nc.sbuf_tensor("fB", [128, 256], F32))
        fC = E(nc.sbuf_tensor("fC", [128, 256], F32))
        g0 = E(nc.sbuf_tensor("g0", [128, 256], F32))
        g0h = E(nc.sbuf_tensor("g0h", [128, 256], BF16))
        g0l = E(nc.sbuf_tensor("g0l", [128, 256], BF16))
        eg = E(nc.sbuf_tensor("eg", [128, 256], F32))
        s_sb = E(nc.sbuf_tensor("s_sb", [128, 256], F32))
        EnG = E(nc.sbuf_tensor("EnG", [128, 256], F32))
        snt = E(nc.sbuf_tensor("snt", [128, 2, 256], BF16))
        s2 = E(nc.sbuf_tensor("s2", [128, 256], BF16))
        glp = E(nc.sbuf_tensor("glp", [128, 2, 512], F32))
        gcs = glp[:, :, 0:256]
        lps = glp[:, :, 256:512]
        egt_f = E(nc.sbuf_tensor("egt_f", [64, 2, 256], F32))
        egt_bf = E(nc.sbuf_tensor("egt_bf", [1, 256], BF16))
        egt2_bf = E(nc.sbuf_tensor("egt2_bf", [1, 512], BF16))
        gacc = E(nc.sbuf_tensor("gacc", [64, 256], F32))
        smaskT = E(nc.sbuf_tensor("smaskT", [128, 4, 128], BF16))
        hkl = E(nc.sbuf_tensor("hkl", [128, 8, 64], BF16))
        hkt = E(nc.sbuf_tensor("hkt", [128, 512], F32))
        hvl = E(nc.sbuf_tensor("hvl", [64, 4, 256], BF16))
        hvsp = E(nc.sbuf_tensor("hvsp", [64, 4, 1024], BF16))
        ecol = E(nc.sbuf_tensor("ecol", [64, 4], F32))
        onesb = E(nc.sbuf_tensor("onesb", [1, 128], BF16))
        neg8c = E(nc.sbuf_tensor("neg8c", [128, 128], BF16))
        epsb = E(nc.sbuf_tensor("epsb", [128, 1], F32))
        trin_sb = E(nc.sbuf_tensor("trin_sb", [128, 128], BF16))
        triu_sb = E(nc.sbuf_tensor("triu_sb", [128, 128], BF16))
        tri4_sb = E(nc.sbuf_tensor("tri4_sb", [128, 512], BF16))
        ident = E(nc.sbuf_tensor("ident", [128, 128], BF16))
        summ_sb = E(nc.sbuf_tensor("summ_sb", [64, 2304], F32))
        hktT = E(nc.sbuf_tensor("hktT", [64, 8, 128], BF16))
        if prep:
            ftmp = E(nc.sbuf_tensor("ftmp", [128, 2, 1024], BF16))
            ftmpT = E(nc.sbuf_tensor("ftmpT", [128, 2, 8, 128], BF16))
            wotmp = E(nc.sbuf_tensor("wotmp", [128, 2, 1024], BF16))
            fscc_sb = E(nc.sbuf_tensor("fscc_sb", [128, 8], F32))
            gnwc_sb = E(nc.sbuf_tensor("gnwc_sb", [128, 2], F32))

        pjA = E(nc.psum_tensor("pjA", [128, 512], F32))
        pjB = E(nc.psum_tensor("pjB", [128, 512], F32))
        pb4 = E(nc.psum_tensor("pb4", [128, 512], F32))
        pvA = E(nc.psum_tensor("pvA", [128, 512], F32))
        pvB = E(nc.psum_tensor("pvB", [128, 512], F32))
        pSb = E(nc.psum_tensor("pSb", [128, 512], F32))
        pdkT = E(nc.psum_tensor("pdkT", [128, 512], F32))
        PSB = E(nc.psum_tensor("PSB", [128, 1024], BF16))
        pj = [pjA[:, :], pjB[:, :]]
        pEb256 = pjA[:, 0:256]
        pLp = pjA[:, 256:512]
        pEb512 = pjB[:, :]
        pvh = [pvA, pvB]

        def pdv(h):
            return pvh[h // 2][0:64, (h % 2) * 256:(h % 2) * 256 + 256]
        pf = pb4[:, 0:256]
        pGt = pb4[:, 0:256]
        pGc = pb4[:, 256:512]
        pS = pSb[:, :]
        pec = pSb[0:64, 0:4]
        pdk = pdkT[:, :]

        S = _sems(nc, E, [
            "wld", "ini", "lna", "lnv", "lns", "lnr", "lnx",
            "qkp", "qke", "qsc", "qsp",
            "fp", "fc", "gp", "ghl", "gc", "eg_", "eng", "egt", "egb", "eb", "ece",
            "vp", "ve", "st", "sm", "sn", "s2d", "lp", "lps_", "gcs_",
            "dk", "hk", "dv", "hv", "hvc", "gac", "hkTp", "hkT",
            "sum_", "smd",
            "xl0", "xl1", "xl2", "xl3", "xt0", "xt1", "kt0", "kt1", "kt2",
            "xnsp0", "xnsp1",
            "spgl0", "spgl1", "sps0", "sps1",
            "spv0", "spv1", "sphv0", "sphv1", "sphv2", "sphv3",
            "speg0", "speg1",
            "fld0", "fld1", "wol0", "wol1", "ftp0", "ftp1",
            "fsp0", "fsp1", "wsp0", "wsp1", "fsg", "wfd", "wcl",
            "slfp", "slfa", "slfv"])
        SEMCNT = {}

        block = E(nc.Block())

        @block.sync
        def _(e):
            if prep:
                e.dma_start(out=fscc_sb[:, :],
                            in_=fscc_in[:, :]).then_inc(S["wcl"], 16)
                e.dma_start(out=gnwc_sb[:, :],
                            in_=gnwc_in[:, :]).then_inc(S["wcl"], 16)

            def _sync_prep(i):
                if i < 8:
                    ot = i
                    e.wait_ge(S["fsg"], ot + 1)
                    if ot >= 2:
                        e.wait_ge(S["fsp%d" % (ot % 2)], 16 * ((ot - 2) // 2 + 1))
                    e.dma_start(out=ftmpT[:, ot % 2, :, :],
                                in_=ftmp[:, ot % 2, :].rearrange(
                                    "p (a c) -> p a c", a=8),
                                transpose=True).then_inc(S["ftp%d" % (ot % 2)], 16)
                    e.wait_ge(S["ftp%d" % (ot % 2)], 16 * (ot // 2 + 1))
                    e.dma_start(out=sp_fwT[ot * 128:(ot + 1) * 128, :],
                                in_=ftmpT[:, ot % 2, :, :].rearrange(
                                    "p a c -> p (a c)")
                                ).then_inc(S["fsp%d" % (ot % 2)], 16)
                else:
                    dt = i - 8
                    e.wait_ge(S["wfd"], dt + 1)
                    e.dma_start(out=sp_wo[dt * 128:(dt + 1) * 128, :],
                                in_=wotmp[:, dt % 2, :]
                                ).then_inc(S["wsp%d" % (dt % 2)], 16)

            for g in range(NG):
                for tt in range(4):
                    t = 4 * g + tt
                    if t >= 4:
                        e.wait_ge(S["lnx"], t - 3)
                    e.dma_start(out=xin[:, t % 4, :],
                                in_=xseg[t * 128:(t + 1) * 128, :]
                                ).then_inc(S["xl%d" % (t % 4)], 16)
                for tt in range(4):
                    t = 4 * g + tt
                    e.wait_ge(S["lnx"], t + 1)
                    if g >= 2:
                        if pj_on:
                            e.wait_ge(S["xnsp%d" % (g % 2)], 16 * (g // 2))
                        if ch_on:
                            e.wait_ge(S["vp"], 4 * (g - 1))
                    e.dma_start(
                        out=xnT[:, g % 2, tt, :, :],
                        in_=xn_tm[:, t % 4, :].rearrange("p (a c) -> p a c", a=8),
                        transpose=True).then_inc(S["xt%d" % (g % 2)], 16)
                for cc in range(4):
                    if not ch_on or clvl < 3:
                        break
                    c = 4 * g + cc
                    e.wait_ge(S["qke"], 16 * g + 16)
                    if c >= 3:
                        e.wait_ge(S["dk"], c - 2)
                    e.dma_start(
                        out=k_tm[:, c % 3, :].rearrange("p (a c) -> p a c", a=8),
                        in_=kT_sb[:, cc, :, :],
                        transpose=True).then_inc(S["kt%d" % (c % 3)], 16)
                if prep and g >= 1:
                    _sync_prep(2 * (g - 1))
                    _sync_prep(2 * (g - 1) + 1)
            if prep:
                _sync_prep(14)
                _sync_prep(15)
                for nm in ("fsp", "wsp"):
                    e.wait_ge(S[nm + "0"], 64)
                    e.wait_ge(S[nm + "1"], 64)

        @block.gpsimd
        def _(e):
            e.dma_start(out=wq_sb[:, :, :],
                        in_=Wq.rearrange("(a p) f -> p a f", p=128)).then_inc(S["wld"], 16)
            e.dma_start(out=wk_sb[:, :, :],
                        in_=Wk.rearrange("(a p) f -> p a f", p=128)).then_inc(S["wld"], 16)
            e.dma_start(out=wv_sb[:, :, :],
                        in_=Wv.rearrange("(a p) f -> p a f", p=128)).then_inc(S["wld"], 16)
            e.dma_start(out=wf_sb[:, :, :],
                        in_=Wf.rearrange("(a p) f -> p a f", p=128)).then_inc(S["wld"], 16)
            e.dma_start(out=trin_sb[:, :], in_=trineg[:, :]).then_inc(S["wld"], 16)
            e.dma_start(out=triu_sb[:, :], in_=triu_in[:, :]).then_inc(S["wld"], 16)
            e.dma_start(out=tri4_sb[:, :], in_=tri4_in[:, :]).then_inc(S["wld"], 16)
            e.dma_start(out=ident[:, :], in_=ident_in[:, :]).then_inc(S["wld"], 16)
            W, fin = _serial(e, S["slfp"], SEMCNT)
            fin(e.memset(onesb[:, :], 1.0))
            W(); fin(e.memset(neg8c[:, :], NEG8))
            W(); fin(e.memset(epsb[:, :], EPS))
            W(); fin(e.memset(hkl[:, :, :], 0.0))
            W(); fin(e.memset(hvl[:, :, :], 0.0))
            W(); fin(e.memset(hvsp[:, :, :], 0.0))
            W(); fin(e.memset(summ_sb[:, :], 0.0))
            W(); fin(e.memset(gacc[:, :], 0.0), S["ini"])

            if prep:
                for ot in range(8):
                    if ot >= 2:
                        e.wait_ge(S["ftp%d" % (ot % 2)],
                                  16 * ((ot - 2) // 2 + 1))
                    e.dma_start(out=ftmp[:, ot % 2, :],
                                in_=fw[ot * 128:(ot + 1) * 128, :]
                                ).then_inc(S["fld%d" % (ot % 2)], 16)
                for dt in range(8):
                    if dt >= 2:
                        e.wait_ge(S["wsp%d" % (dt % 2)],
                                  16 * ((dt - 2) // 2 + 1))
                    e.dma_start(out=wotmp[:, dt % 2, :],
                                in_=Wo[dt * 128:(dt + 1) * 128, :]
                                ).then_inc(S["wol%d" % (dt % 2)], 16)

            if not sm_on:
                return
            e.wait_ge(S["hkT"], 1)
            e.wait_ge(S["hv"], NCH)
            e.wait_ge(S["gac"], NCH)
            W(); fin(e.tensor_copy(summ_sb[:, 0:1024],
                                   hktT.rearrange("p a c -> p (a c)")[:, :]))
            W(); fin(e.tensor_copy(summ_sb[:, 1024:2048],
                                   hvl.rearrange("p a m -> p (a m)")[:, :]))
            W(); fin(e.tensor_copy(summ_sb[0:1, 2048:2304], gacc[0:1, :]), S["sum_"])

        @block.scalar
        def _(e):
            W, fin = _serial(e, S["slfa"], SEMCNT)
            e.wait_ge(S["ini"], 1)

            def spills(c):
                p = c % 2
                e.wait_ge(S["gcs_"], c + 1)
                if clvl >= 3:
                    e.wait_ge(S["lps_"], c + 1)
                    e.dma_start(out=sp_glp[c],
                                in_=glp[:, p, :]).then_inc(S["spgl%d" % p], 16)
                else:
                    e.dma_start(out=sp_glp[c].rearrange(
                                    "p (a f) -> p a f", a=2)[:, 0, :],
                                in_=gcs[:, p, :]).then_inc(S["spgl%d" % p], 16)
                e.wait_ge(S["sn"], c + 1)
                e.dma_start(out=sp_s[c], in_=snt[:, p, :]).then_inc(S["sps%d" % p], 16)
                e.wait_ge(S["egt"], c + 1)
                e.dma_start(out=sp_eg[c], in_=egt_f[0:1, p, :]).then_inc(S["speg%d" % p], 16)
                if clvl >= 2:
                    e.wait_ge(S["ve"], c + 1)
                    e.dma_start(out=sp_v[c], in_=v_sb[:, p, :]).then_inc(S["spv%d" % p], 16)
                if clvl >= 4:
                    e.wait_ge(S["hvc"], c)
                    e.dma_start(out=sp_hv[c],
                                in_=hvsp[:, c % 4, :]).then_inc(S["sphv%d" % (c % 4)], 16)

            for g in range(NG):
                def _sq(t):
                    e.wait_ge(S["xl%d" % (t % 4)], 16 * (t // 4 + 1))
                    if t >= 2:
                        e.wait_ge(S["lnx"], t - 1)
                    W()
                    fin(e.activation(sqs[:, :], xin[:, t % 4, :], AF.Square,
                                     accum_out=lnst[:, t % 2, 0:1]), S["lna"])

                def _sqrt(t):
                    e.wait_ge(S["lnv"], t + 1)
                    W()
                    fin(e.activation(lnst[:, t % 2, 5:6], lnst[:, t % 2, 4:5],
                                     AF.Sqrt, bias=epsb[:, 0:1]), S["lns"])

                def _apply(t):
                    e.wait_ge(S["lnr"], t + 1)
                    if g >= 1:
                        e.wait_ge(S["xt%d" % ((g - 1) % 2)],
                                  64 * ((g - 1) // 2 + 1))
                    W()
                    fin(e.activation(xn_tm[:, t % 4, :], xin[:, t % 4, :],
                                     AF.Identity, bias=lnst[:, t % 2, 7:8],
                                     scale=lnst[:, t % 2, 6:7]), S["lnx"])
                t0 = 4 * g
                _sq(t0); _sq(t0 + 1); _sqrt(t0); _apply(t0)
                _sq(t0 + 2); _sqrt(t0 + 1); _apply(t0 + 1)
                _sq(t0 + 3); _sqrt(t0 + 2); _apply(t0 + 2)
                _sqrt(t0 + 3); _apply(t0 + 3)
                if g >= 1 and pj_on:
                    e.wait_ge(S["qsc"], 8 * g)
                    e.dma_start(out=sp_q[g - 1].rearrange("p (a c) -> p a c", a=8),
                                in_=qT_sb.rearrange("p t a c -> p (t a c)")[:, :].rearrange(
                          "p (a c) -> p a c", a=8)).then_inc(S["qsp"], 16)
                    e.wait_ge(S["xt%d" % ((g - 1) % 2)],
                              64 * ((g - 1) // 2 + 1))
                    e.dma_start(out=sp_xn[g - 1].rearrange("p (a c) -> p a c", a=8),
                                in_=xnT[:, (g - 1) % 2, :, :, :].rearrange(
                          "p t a c -> p (t a c)").rearrange(
                          "p (a c) -> p a c", a=8)).then_inc(S["xnsp%d" % ((g - 1) % 2)], 16)
                for idx in range(16):
                    if not pj_on:
                        break
                    e.wait_ge(S["qkp"], 16 * g + idx + 1)
                    if idx < 8:
                        W()
                        fin(e.activation(qsil[:, idx % 2, :], pj[idx % 2],
                                         AF.Silu), S["qke"])
                        W()
                        fin(e.activation(qT_sb[:, :, idx, :],
                                         qsil[:, idx % 2, :].rearrange(
                                             "p (a c) -> p a c", a=4),
                                         AF.Identity, scale=SCL), S["qsc"])
                    else:
                        W()
                        fin(e.activation(kT_sb[:, :, idx - 8, :], pj[idx % 2],
                                         AF.Silu), S["qke"])
                for cc in range(4):
                    if not ch_on:
                        break
                    c = 4 * g + cc
                    if c >= 2:
                        spills(c - 2)
                    e.wait_ge(S["fp"], c + 1)
                    e.wait_ge(S["gc"], c)
                    W(); fin(e.activation(fA[:, :], pf, AF.Exp, scale=-1.0))
                    W(); fin(e.activation(g0[:, :], fA[:, :], AF.Ln,
                                          bias=1.0), S["fc"])
                    if clvl >= 2:
                        e.wait_ge(S["vp"], c + 1)
                        if c >= 2:
                            e.wait_ge(S["spv%d" % (c % 2)], 16 * (c // 2))
                        W(); fin(e.activation(v_sb[:, c % 2, 0:512], pvA[:, :],
                                              AF.Identity))
                        W(); fin(e.activation(v_sb[:, c % 2, 512:1024], pvB[:, :],
                                              AF.Identity), S["ve"])
                    e.wait_ge(S["fc"], c + 1)
                    W(); fin(e.activation(eg[:, :], g0[:, :], AF.Exp,
                                          scale=NEG8), S["eg_"])
                    e.wait_ge(S["gc"], c + 1)
                    W(); fin(e.activation(EnG[:, :], pGc, AF.Exp,
                                          scale=-1.0), S["eng"])
                    if c >= 2:
                        e.wait_ge(S["speg%d" % (c % 2)], 16 * (c // 2))
                    W(); fin(e.activation(egt_f[:, c % 2, :], pGt[0:64, :],
                                          AF.Exp), S["egt"])
                    if c >= 2:
                        e.wait_ge(S["spgl%d" % (c % 2)], 16 * (c // 2))
                    W(); fin(e.activation(gcs[:, c % 2, :], pGc,
                                          AF.Identity), S["gcs_"])
            if ch_on:
                spills(30)
                spills(31)
            if not pj_on:
                e.wait_ge(S["xt0"], 64 * 4)
                e.wait_ge(S["xt1"], 64 * 4)
                return
            e.wait_ge(S["qsc"], 64)
            e.dma_start(out=sp_q[7].rearrange("p (a c) -> p a c", a=8),
                        in_=qT_sb.rearrange("p t a c -> p (t a c)")[:, :].rearrange(
                          "p (a c) -> p a c", a=8)).then_inc(S["qsp"], 16)
            e.wait_ge(S["xt1"], 64 * 4)
            e.dma_start(out=sp_xn[7].rearrange("p (a c) -> p a c", a=8),
                        in_=xnT[:, 1, :, :, :].rearrange(
                          "p t a c -> p (t a c)").rearrange(
                          "p (a c) -> p a c", a=8)).then_inc(S["xnsp1"], 16)
            if sm_on:
                e.wait_ge(S["sum_"], 1)
                e.dma_start(out=summ[:, :], in_=summ_sb[:, :]).then_inc(S["smd"], 16)
                e.wait_ge(S["smd"], 16)
            if ch_on:
                nms = ["spgl", "sps", "speg"]
                if clvl >= 2:
                    nms.append("spv")
                for nm in nms:
                    e.wait_ge(S[nm + "0"], 16 * 16)
                    e.wait_ge(S[nm + "1"], 16 * 16)
                if clvl >= 4:
                    for q_ in range(4):
                        e.wait_ge(S["sphv%d" % q_], 16 * 8)
            e.wait_ge(S["qsp"], 16 * NG)
            e.wait_ge(S["xnsp0"], 16 * 4)
            e.wait_ge(S["xnsp1"], 16 * 4)

        @block.vector
        def _(e):
            W, fin = _serial(e, S["slfv"], SEMCNT)

            def _dve_prep(i):
                if i < 8:
                    ot = i
                    e.wait_ge(S["fld%d" % (ot % 2)], 16 * (ot // 2 + 1))
                    if ot == 0:
                        e.wait_ge(S["wcl"], 32)
                    vv = ftmp[:, ot % 2, :]
                    W(); fin(e.tensor_scalar(vv, vv, 0.0, 2.0,
                                             ALU.is_ge, ALU.mult))
                    W(); fin(e.tensor_scalar_add(vv, vv, -1.0))
                    W(); fin(e.tensor_scalar_mul(vv, vv, fscc_sb[:, ot:ot + 1]),
                             S["fsg"])
                else:
                    dt = i - 8
                    e.wait_ge(S["wol%d" % (dt % 2)], 16 * (dt // 2 + 1))
                    vv = wotmp[:, dt % 2, :]
                    W(); fin(e.tensor_scalar_mul(vv, vv,
                                                 gnwc_sb[:, dt % 2:dt % 2 + 1]),
                             S["wfd"])

            for g in range(NG):
                for tt in range(4):
                    t = 4 * g + tt
                    e.wait_ge(S["xl%d" % (t % 4)], 16 * (t // 4 + 1))
                    if t >= 2:
                        e.wait_ge(S["lnx"], t - 1)
                    W(); fin(e.reduce_sum(lnst[:, t % 2, 1:2], xin[:, t % 4, :],
                                          axis=AX.X))
                    W(); fin(e.tensor_scalar_mul(lnst[:, t % 2, 2:3],
                                                 lnst[:, t % 2, 1:2], 1.0 / D))
                    W(); fin(e.tensor_mul(lnst[:, t % 2, 3:4],
                                          lnst[:, t % 2, 2:3],
                                          lnst[:, t % 2, 2:3]))
                    e.wait_ge(S["lna"], t + 1)
                    W()
                    fin(e.scalar_tensor_tensor(lnst[:, t % 2, 4:5],
                                               lnst[:, t % 2, 0:1], 1.0 / D,
                                               lnst[:, t % 2, 3:4],
                                               ALU.mult, ALU.subtract), S["lnv"])
                    e.wait_ge(S["lns"], t + 1)
                    W(); fin(e.reciprocal(lnst[:, t % 2, 6:7],
                                          lnst[:, t % 2, 5:6]))
                    W(); fin(e.scalar_tensor_tensor(lnst[:, t % 2, 7:8],
                                                    lnst[:, t % 2, 2:3], -1.0,
                                                    lnst[:, t % 2, 6:7],
                                                    ALU.mult, ALU.mult), S["lnr"])
                for cc in range(4):
                    if not ch_on:
                        break
                    c = 4 * g + cc
                    e.wait_ge(S["fc"], c + 1)
                    if c >= 1:
                        e.wait_ge(S["gc"], c)
                    W(); fin(e.tensor_copy(g0h[:, :], g0[:, :]))
                    W(); fin(e.tensor_sub(g0l[:, :], g0[:, :], g0h[:, :]),
                             S["ghl"])
                    if clvl >= 2:
                        e.wait_ge(S["st"], c + 1)
                        W(); fin(e.tensor_mul(
                            smaskT.rearrange("p a c -> p (a c)")[:, :],
                            pS[:, :], tri4_sb[:, :]), S["sm"])
                    e.wait_ge(S["eg_"], c + 1)
                    W(); fin(e.tensor_scalar(s_sb[:, :], eg[:, :], -1.0, 1.0,
                                             ALU.mult, ALU.add))
                    e.wait_ge(S["eng"], c + 1)
                    if c >= 2:
                        e.wait_ge(S["sps%d" % (c % 2)], 16 * (c // 2))
                    W(); fin(e.tensor_mul(snt[:, c % 2, :], s_sb[:, :],
                                          EnG[:, :]), S["sn"])
                    if clvl >= 2:
                        e.wait_ge(S["egt"], c + 1)
                        W(); fin(e.tensor_copy(egt_bf[:, :], egt_f[0:1, c % 2, :]))
                        ev = egt2_bf.rearrange("p (a t m) -> p a t m", a=4, t=2)
                        gv = egt_bf.rearrange("p (a m) -> p a m", a=4)
                        W(); fin(e.tensor_copy(ev[:, :, 0, :], gv[:, :, :]))
                        W(); fin(e.tensor_copy(ev[:, :, 1, :], gv[:, :, :]), S["egb"])
                    e.wait_ge(S["gc"], c + 1)
                    e.wait_ge(S["gcs_"], c + 1)
                    W(); fin(e.tensor_add(gacc[:, :], gacc[:, :], pGt[0:64, :]), S["gac"])
                    if clvl >= 2:
                        e.wait_ge(S["eb"], c + 1)
                        W(); fin(e.tensor_copy(ecol[:, :], pec), S["ece"])
                        W(); fin(e.tensor_mul(s2[:, :], snt[:, c % 2, :],
                                              pEb256), S["s2d"])
                    if clvl >= 3:
                        e.wait_ge(S["lp"], c + 1)
                        if c >= 2:
                            e.wait_ge(S["spgl%d" % (c % 2)], 16 * (c // 2))
                        W(); fin(e.tensor_copy(lps[:, c % 2, :], pLp), S["lps_"])
                        e.wait_ge(S["dk"], c + 1)
                        W(); fin(e.tensor_add(hkt[:, :],
                                              hkl.rearrange("p a m -> p (a m)")[:, :],
                                              pdk))
                        W(); fin(e.tensor_mul(hkl.rearrange("p a m -> p (a m)")[:, :],
                                              hkt[:, :], pEb512), S["hk"])
                    if clvl >= 4:
                        e.wait_ge(S["dv"], c + 1)
                        e.wait_ge(S["ece"], c + 1)
                        for h in range(H):
                            W()
                            fin(e.scalar_tensor_tensor(
                                hvl[:, h, :], hvl[:, h, :], ecol[:, h:h + 1],
                                pdv(h),
                                ALU.mult, ALU.add), S["hv"] if h == 3 else None)
                    if clvl >= 4 and c < NCH - 1:
                        e.wait_ge(S["hv"], c + 1)
                        if c >= 3:
                            e.wait_ge(S["sphv%d" % ((c + 1) % 4)],
                                      16 * ((c - 3) // 4 + 1))
                        W()
                        fin(e.tensor_copy(hvsp[:, (c + 1) % 4, :],
                                          hvl.rearrange("p a m -> p (a m)")[:, :]),
                            S["hvc"])
                if prep:
                    _dve_prep(2 * g)
                    _dve_prep(2 * g + 1)
            if sm_on:
                e.wait_ge(S["hkTp"], 1)
                W(); fin(e.tensor_copy(hktT.rearrange("p a c -> p (a c)")[:, :],
                                       PSB[0:64, :]), S["hkT"])

        @block.tensor
        def _(e):
            e.wait_ge(S["wld"], 128)
            e.wait_ge(S["ini"], 1)
            for g in range(NG):
                if not pj_on:
                    break
                e.wait_ge(S["xt%d" % (g % 2)], 64 * (g // 2 + 1))
                if g >= 1:
                    e.wait_ge(S["qsp"], 16 * g)
                    if ch_on:
                        if clvl >= 2:
                            e.wait_ge(S["st"], 4 * g)
                            e.wait_ge(S["s2d"], 4 * g)
                        if clvl >= 3:
                            for _s in range(3):
                                _n = sum(1 for _c in range(4 * g)
                                         if _c % 3 == _s)
                                e.wait_ge(S["kt%d" % _s], 16 * _n)
                            e.wait_ge(S["lps_"], 4 * g)
                            e.wait_ge(S["hk"], 4 * g)
                for idx in range(16):
                    if idx >= 2:
                        e.wait_ge(S["qke"], 16 * g + idx - 1)
                    wsb = wq_sb if idx < 8 else wk_sb
                    ft = idx % 8
                    for dt in range(8):
                        op = e.matmul(pj[idx % 2],
                                      wsb[:, dt, ft * 128:(ft + 1) * 128],
                                      xnT[:, g % 2, :, dt, :],
                                      start=(dt == 0), stop=(dt == 7))
                    op.then_inc(S["qkp"], 1)
                for cc in range(4):
                    if not ch_on:
                        break
                    c = 4 * g + cc
                    ts = slice(cc * 128, (cc + 1) * 128)
                    e.wait_ge(S["fc"], c)
                    e.wait_ge(S["egt"], c)
                    e.wait_ge(S["gac"], c)
                    e.wait_ge(S["gcs_"], c)
                    for dt in range(8):
                        op = e.matmul(pf, xnT[:, g % 2, cc, dt, :], wf_sb[:, dt, :],
                                      start=(dt == 0), stop=(dt == 7))
                    op.then_inc(S["fp"], 1)
                    if clvl < 2:
                        e.wait_ge(S["fc"], c + 1)
                        e.matmul(pGc, trin_sb[:, :], g0[:, :], start=True, stop=True)
                        op = e.matmul(pGt, neg8c[:, :], g0[:, :], start=True, stop=True)
                        op.then_inc(S["gc"], 1)
                        continue
                    e.wait_ge(S["ve"], c)
                    if clvl >= 4:
                        e.wait_ge(S["hv"], c)
                    for dt in range(8):
                        op = e.matmul(pvh[0][:, :],
                                      xnT[:, g % 2, cc, dt, :],
                                      wv_sb[:, dt, 0:512],
                                      start=(dt == 0), stop=(dt == 7))
                    e.wait_ge(S["ghl"], c + 1)
                    e.wait_ge(S["eng"], c)
                    e.wait_ge(S["gcs_"], c)
                    e.matmul(pGc, trin_sb[:, :], g0h[:, :], start=True, stop=False)
                    e.matmul(pGc, trin_sb[:, :], g0l[:, :], start=False, stop=True)
                    e.matmul(pGt, neg8c[:, :], g0h[:, :], start=True, stop=False)
                    op = e.matmul(pGt, neg8c[:, :], g0l[:, :], start=False, stop=True)
                    op.then_inc(S["gc"], 1)
                    for dt in range(8):
                        op = e.matmul(pvh[1][:, :],
                                      xnT[:, g % 2, cc, dt, :],
                                      wv_sb[:, dt, 512:1024],
                                      start=(dt == 0), stop=(dt == 7))
                    op.then_inc(S["vp"], 1)
                    e.wait_ge(S["sm"], c)
                    e.wait_ge(S["ece"], c)
                    e.wait_ge(S["qsc"], 8 * (g + 1))
                    e.wait_ge(S["qke"], 16 * (g + 1))
                    for h in range(H):
                        e.matmul(pS[:, h * 128:(h + 1) * 128],
                                 kT_sb[:, cc, 2 * h, :], qT_sb[:, cc, 2 * h, :],
                                 start=True, stop=False)
                        op = e.matmul(pS[:, h * 128:(h + 1) * 128],
                                      kT_sb[:, cc, 2 * h + 1, :],
                                      qT_sb[:, cc, 2 * h + 1, :],
                                      start=False, stop=True)
                    op.then_inc(S["st"], 1)
                    e.wait_ge(S["egb"], c + 1)
                    e.wait_ge(S["s2d"], c)
                    e.wait_ge(S["lps_"], c)
                    e.wait_ge(S["hk"], c)
                    e.wait_ge(S["sm"], c + 1)
                    e.matmul(pEb256, onesb[:, :], egt_bf[:, :], start=True, stop=True)
                    e.matmul(pEb512, onesb[:, :], egt2_bf[:, :], start=True, stop=True)
                    for h in range(H):
                        op = e.matmul(pec[:, h:h + 1],
                                      egt_bf[0:1, h * 64:(h + 1) * 64],
                                      onesb[0:1, 0:1], start=True, stop=True)
                    op.then_inc(S["eb"], 1)
                    if clvl < 3:
                        continue
                    e.wait_ge(S["sn"], c + 1)
                    e.wait_ge(S["s2d"], c + 1)
                    e.wait_ge(S["lps_"], c)
                    for h in range(H):
                        e.matmul(pLp[:, h * 64:(h + 1) * 64],
                                 qT_sb[:, cc, 2 * h, :], hkl[:, 2 * h, :],
                                 start=True, stop=False)
                        e.matmul(pLp[:, h * 64:(h + 1) * 64],
                                 qT_sb[:, cc, 2 * h + 1, :], hkl[:, 2 * h + 1, :],
                                 start=False, stop=False)
                        op = e.matmul(pLp[:, h * 64:(h + 1) * 64],
                                      smaskT[:, h, :],
                                      snt[:, c % 2, h * 64:(h + 1) * 64],
                                      start=False, stop=True)
                    op.then_inc(S["lp"], 1)
                    e.wait_ge(S["s2d"], c + 1)
                    e.wait_ge(S["kt%d" % (c % 3)], 16 * (c // 3 + 1))
                    for blk in range(8):
                        h = blk // 2
                        op = e.matmul(pdk[:, blk * 64:(blk + 1) * 64],
                                      k_tm[:, c % 3, blk * 128:(blk + 1) * 128],
                                      snt[:, c % 2, h * 64:(h + 1) * 64],
                                      start=True, stop=True)
                    op.then_inc(S["dk"], 1)
                    if clvl < 4:
                        continue
                    e.wait_ge(S["ve"], c + 1)
                    for h in range(H):
                        op = e.matmul(pdv(h),
                                      s2[:, h * 64:(h + 1) * 64],
                                      v_sb[:, c % 2, h * 256:(h + 1) * 256],
                                      start=True, stop=True)
                    op.then_inc(S["dv"], 1)
            if sm_on:
                e.wait_ge(S["hk"], NCH)
                for blk in range(8):
                    op = e.transpose(PSB[0:64, blk * 128:(blk + 1) * 128],
                                     hkl[:, blk, :], ident[:, :])
                op.then_inc(S["hkTp"], 1)



def _decl_l2_inputs(nc):
    d = {}
    d["Wo"] = nc.dram_tensor("Wo", [D, D], F32, kind="ExternalInput").ap()
    d["fw"] = nc.dram_tensor("fw", [D, D], F32, kind="ExternalInput").ap()
    d["fscc"] = nc.dram_tensor("fscc", [128, 8], F32, kind="ExternalInput").ap()
    d["gnwc"] = nc.dram_tensor("gnwc", [128, 2], F32, kind="ExternalInput").ap()
    d["fbias"] = nc.dram_tensor("fbias", [D], F32, kind="ExternalInput").ap()
    d["sel9"] = nc.dram_tensor("sel9", [9, 8], BF16, kind="ExternalInput").ap()
    d["negrow"] = nc.dram_tensor("negrow", [1, 256], F32, kind="ExternalInput").ap()
    return d


def build_l2():
    nc = bass.Bass("TRN2", target_bir_lowering=False, debug=False)
    d = _decl_l2_inputs(nc)
    d["xseg"] = nc.dram_tensor("xseg", [NT, D], F32, kind="ExternalInput").ap()
    d["triu_in"] = nc.dram_tensor("triu_in", [C, C], BF16, kind="ExternalInput").ap()
    d["tri4_in"] = nc.dram_tensor("tri4_in", [C, 512], BF16, kind="ExternalInput").ap()
    d["ident_in"] = nc.dram_tensor("ident_in", [C, C], BF16, kind="ExternalInput").ap()
    d["ag"] = nc.dram_tensor("ag", [512, 2304], F32, kind="ExternalInput").ap()
    for nm, shp, dt in L1_SPILLS:
        d[nm] = nc.dram_tensor(nm, shp, dt, kind="ExternalInput").ap()
    d["outseg"] = nc.dram_tensor("outseg", [NT, D], F32, kind="ExternalOutput").ap()
    _emit_l2(nc, d, cc=None)
    return nc


def _emit_l2(nc, dmap, cc=None):
    xseg = dmap["xseg"]; fbias = dmap["fbias"]; triu_in = dmap["triu_in"]
    tri4_in = dmap["tri4_in"]; ident_in = dmap["ident_in"]
    sel9 = dmap["sel9"]; negrow = dmap["negrow"]
    sp_wo = dmap["sp_wo"]; sp_fwT = dmap["sp_fwT"]
    ag = dmap["ag"]
    sp_q = dmap["sp_q"]; sp_xn = dmap["sp_xn"]; sp_s = dmap["sp_s"]
    sp_glp = dmap["sp_glp"]; sp_v = dmap["sp_v"]
    sp_hv = dmap["sp_hv"]; sp_eg = dmap["sp_eg"]; outseg = dmap["outseg"]

    with ExitStack() as ctx:
        E = ctx.enter_context
        wo_sb = E(nc.sbuf_tensor("wo_sb", [128, 8, 1024], BF16))
        fwT = E(nc.sbuf_tensor("fwT", [128, 8, 8, 128], BF16))
        fbi_r = E(nc.sbuf_tensor("fbi_r", [1, 1024], F32))
        fbi_b = E(nc.sbuf_tensor("fbi_b", [1, 1024], BF16))
        triu_sb = E(nc.sbuf_tensor("triu_sb2", [128, 128], BF16))
        tri4_sb = E(nc.sbuf_tensor("tri4_sb2", [128, 512], BF16))
        ident = E(nc.sbuf_tensor("ident2", [128, 128], BF16))
        onesb = E(nc.sbuf_tensor("onesb2", [1, 128], BF16))
        one1 = E(nc.sbuf_tensor("one1", [1, 1], BF16))
        epsb = E(nc.sbuf_tensor("epsb2", [128, 1], F32))
        sel_sb = E(nc.sbuf_tensor("sel_sb", [9, 8], BF16))
        gall = E(nc.sbuf_tensor("gall", [9, 256], F32))
        gallb = E(nc.sbuf_tensor("gallb", [9, 256], BF16))
        cfT4 = E(nc.sbuf_tensor("cfT4", [64, 4, 8], F32))
        hkr = E(nc.sbuf_tensor("hkr", [64, 2, 1024], F32))
        hvr = E(nc.sbuf_tensor("hvr", [64, 2, 1024], F32))
        hkiT = E(nc.sbuf_tensor("hkiT", [64, 1024], F32))
        hkiTb = E(nc.sbuf_tensor("hkiTb", [64, 8, 128], BF16))
        hvi = E(nc.sbuf_tensor("hvi", [64, 4, 256], F32))
        hki = E(nc.sbuf_tensor("hki", [128, 8, 64], BF16))
        qg = E(nc.sbuf_tensor("qg", [128, 2, 4, 8, 128], BF16))
        linit = E(nc.sbuf_tensor("linit", [128, 32, 256], BF16))
        xnT2 = E(nc.sbuf_tensor("xnT2", [128, 2, 4, 8, 128], BF16))
        glp2 = E(nc.sbuf_tensor("glp2", [128, 2, 512], F32))
        gc2 = glp2[:, :, 0:256]
        lp2 = glp2[:, :, 256:512]
        sn2 = E(nc.sbuf_tensor("sn2", [128, 3, 256], BF16))
        v2 = E(nc.sbuf_tensor("v2", [128, 2, 1024], BF16))
        hv2 = E(nc.sbuf_tensor("hv2", [64, 2, 1024], BF16))
        x2 = E(nc.sbuf_tensor("x2", [128, 2, 1024], F32))
        eg2 = E(nc.sbuf_tensor("eg2", [1, 4, 256], F32))
        drow = E(nc.sbuf_tensor("drow", [1, 256], F32))
        dbf = E(nc.sbuf_tensor("dbf", [1, 256], BF16))
        dcol = E(nc.sbuf_tensor("dcol", [64, 4], F32))
        A_sb = E(nc.sbuf_tensor("A_sb", [128, 2, 256], F32))
        lscr = E(nc.sbuf_tensor("lscr", [128, 256], F32))
        lscr2 = E(nc.sbuf_tensor("lscr2", [128, 256], F32))
        lsub = E(nc.sbuf_tensor("lsub", [128, 256], F32))
        esm = E(nc.sbuf_tensor("esm", [128, 256], F32))
        nmax = E(nc.sbuf_tensor("nmax", [128, 4], F32))
        rsum = E(nc.sbuf_tensor("rsum", [128, 4], F32))
        rcp = E(nc.sbuf_tensor("rcp", [128, 4], F32))
        pd_sb = E(nc.sbuf_tensor("pd_sb", [128, 2, 256], BF16))
        tp_sb = E(nc.sbuf_tensor("tp_sb", [64, 8, 128], BF16))
        wm_sb = E(nc.sbuf_tensor("wm_sb", [128, 4, 128], BF16))
        hv0 = E(nc.sbuf_tensor("hv0", [64, 2, 4, 256], BF16))
        sqscr = E(nc.sbuf_tensor("sqscr", [128, 256], F32))
        ssum = E(nc.sbuf_tensor("ssum", [128, 4], F32))
        rstd = E(nc.sbuf_tensor("rstd", [128, 4], F32))
        o_n = E(nc.sbuf_tensor("o_n", [128, 1024], BF16))
        onT = E(nc.sbuf_tensor("onT", [128, 2, 8, 128], BF16))
        ffnc = E(nc.sbuf_tensor("ffnc", [128, 4, 1024], F32))
        out_sb = E(nc.sbuf_tensor("out_sb", [128, 2, 1024], F32))

        pDbc = E(nc.psum_tensor("pDbc", [128, 512], F32))
        pWc = E(nc.psum_tensor("pWc", [128, 512], F32))
        poA = E(nc.psum_tensor("poA", [128, 512], F32))
        poB = E(nc.psum_tensor("poB", [128, 512], F32))
        pfat = E(nc.psum_tensor("pfat", [128, 1024], F32))
        PSB = E(nc.psum_tensor("PSB2", [128, 1024], BF16))
        pLib = E(nc.psum_tensor("pLib", [128, 512], F32))
        pDb = pDbc[:, 0:256]
        pDc = pDbc[0:64, 256:260]
        pLi = pLib[:, 0:256]
        pGsT = pWc[0:64, 0:32]
        pHkT = PSB[:, 0:512]
        pW = pWc[:, :]
        pfa = pfat[:, :]

        def po(h):
            t = poA if h < 2 else poB
            return t[:, (h % 2) * 256:(h % 2) * 256 + 256]

        S = _sems(nc, E, [
            "ldc", "ldw", "ini2", "galb", "galL", "gst", "cft", "cmb", "hkib",
            "hkitp", "hki_",
            "fbi", "li", "lie",
            "aexp", "lnd", "dbf_", "pdm",
            "dcol_", "lt", "lass", "nmax_", "lsub_", "e_", "rcp_", "pd",
            "T_", "tp", "wt", "wm", "hv0_", "o_", "oev", "sq", "rstd_", "on",
            "ffn", "ffnc_", "wo", "out1", "out2",
            "ldr0", "ldr1", "qg0", "qg1", "xng0", "xng1",
            "v2d0", "v2d1", "hv2d0", "hv2d1", "x2d0", "x2d1",
            "gld0", "gld1", "sn2d0", "sn2d1", "sn2d2",
            "egd0", "egd1", "egd2", "egd3", "ont0", "ont1", "ost0", "ost1",
            "ccd", "slfp2", "slfa2", "slfv2"])
        SEMCNT = {}

        block = E(nc.Block())

        @block.sync
        def _(e):
            e.dma_start(out=wo_sb[:, :, :],
                        in_=sp_wo.rearrange("(a p) f -> p a f", p=128)
                        ).then_inc(S["ldw"], 16)
            e.dma_start(out=fwT.rearrange("p a b c -> p a (b c)"),
                        in_=sp_fwT.rearrange("(a p) f -> p a f", p=128)
                        ).then_inc(S["ldw"], 16)
            e.dma_start(out=triu_sb[:, :], in_=triu_in[:, :]).then_inc(S["ldc"], 16)
            e.dma_start(out=tri4_sb[:, :], in_=tri4_in[:, :]).then_inc(S["ldc"], 16)
            e.dma_start(out=ident[:, :], in_=ident_in[:, :]).then_inc(S["ldc"], 16)
            e.dma_start(out=sel_sb[:, :], in_=sel9[:, :]).then_inc(S["ldc"], 16)
            e.dma_start(out=fbi_r[:, :],
                        in_=fbias.rearrange("(a d) -> a d", a=1)).then_inc(S["ldc"], 16)
            if cc is not None:
                e.wait_ge(S["ccd"], 1)
            e.dma_start(out=gall[0:8, :],
                        in_=ag.rearrange("(r i) c -> r i c", i=64)[:, 0, 2048:2304]
                        ).then_inc(S["galL"], 16)
            e.dma_start(out=gall[8:9, :], in_=negrow[:, :]).then_inc(S["galL"], 16)
            for r in range(8):
                if r >= 2:
                    e.wait_ge(S["cmb"], r - 1)
                e.dma_start(out=hkr[:, r % 2, :],
                            in_=ag[r * 64:r * 64 + 64, 0:1024]).then_inc(S["ldr%d" % (r % 2)], 16)
                e.dma_start(out=hvr[:, r % 2, :],
                            in_=ag[r * 64:r * 64 + 64, 1024:2048]).then_inc(S["ldr%d" % (r % 2)], 16)
            def _qgload(g):
                e.dma_start(out=qg[:, g % 2, :, :, :].rearrange(
                                "p t a c -> p (t a c)").rearrange(
                                "p (a c) -> p a c", a=8),
                            in_=sp_q[g].rearrange("p (a c) -> p a c", a=8)
                            ).then_inc(S["qg%d" % (g % 2)], 16)

            _qgload(0)
            _qgload(1)
            for c in range(NCH):
                g = c // 4
                if c % 4 == 0:
                    if g >= 2:
                        e.wait_ge(S["ffn"], 4 * (g - 1))
                    e.dma_start(out=xnT2[:, g % 2, :, :, :].rearrange(
                                    "p t a c -> p (t a c)").rearrange(
                                    "p (a c) -> p a c", a=8),
                                in_=sp_xn[g].rearrange("p (a c) -> p a c", a=8)
                                ).then_inc(S["xng%d" % (g % 2)], 16)
                    if c >= 4 and g + 1 < NG:
                        e.wait_ge(S["li"], c)
                        _qgload(g + 1)
                if c >= 2:
                    e.wait_ge(S["o_"], c - 1)
                e.dma_start(out=v2[:, c % 2, :], in_=sp_v[c]).then_inc(S["v2d%d" % (c % 2)], 16)
                if c >= 2:
                    e.wait_ge(S["hv0_"], c - 1)
                e.dma_start(out=hv2[:, c % 2, :], in_=sp_hv[c]).then_inc(S["hv2d%d" % (c % 2)], 16)
                if c >= 2:
                    e.wait_ge(S["out1"], c - 1)
                e.dma_start(out=x2[:, c % 2, :],
                            in_=xseg[c * 128:(c + 1) * 128, :]).then_inc(S["x2d%d" % (c % 2)], 16)
                if c >= 3:
                    e.wait_ge(S["out2"], c - 2)
                    e.dma_start(out=outseg[(c - 3) * 128:(c - 2) * 128, :],
                                in_=out_sb[:, (c - 3) % 2, :]
                                ).then_inc(S["ost%d" % ((c - 3) % 2)], 16)
            for ct in (NCH - 3, NCH - 2, NCH - 1):
                e.wait_ge(S["out2"], ct + 1)
                e.dma_start(out=outseg[ct * 128:(ct + 1) * 128, :],
                            in_=out_sb[:, ct % 2, :]).then_inc(S["ost%d" % (ct % 2)], 16)
            e.wait_ge(S["ost0"], 16 * 16)
            e.wait_ge(S["ost1"], 16 * 16)

        @block.gpsimd
        def _(e):
            if cc is not None:
                e.collective_compute(
                    "AllGather", ALU.bypass,
                    replica_groups=[list(range(NCORES))],
                    ins=[cc[0]], outs=[cc[1]],
                ).then_inc(S["ccd"])
            W, fin = _serial(e, S["slfp2"], SEMCNT)
            fin(e.memset(onesb[:, :], 1.0))
            W(); fin(e.memset(one1[:, :], 1.0))
            W(); fin(e.memset(epsb[:, :], EPS))
            W(); fin(e.memset(hkiT[:, :], 0.0))
            W(); fin(e.memset(hvi[:, :, :], 0.0))
            W(); fin(e.memset(drow[:, :], 1.0), S["ini2"])
            for c in range(NCH):
                if c >= 1:
                    e.wait_ge(S["egd%d" % ((c - 1) % 4)], 16 * ((c - 1) // 4 + 1))
                    W(); fin(e.tensor_mul(drow[:, :], drow[:, :],
                                          eg2[:, (c - 1) % 4, :]), S["lnd"])
                else:
                    W(); fin(e.engine_nop(), S["lnd"])
                if c >= 1:
                    e.wait_ge(S["pdm"], c)
                W(); fin(e.tensor_copy(dbf[:, :], drow[:, :]), S["dbf_"])
                if c >= 3:
                    t3 = c - 3
                    e.wait_ge(S["out1"], t3 + 1)
                    e.wait_ge(S["ffnc_"], t3 + 1)
                    W(); fin(e.tensor_add(out_sb[:, t3 % 2, :],
                                          out_sb[:, t3 % 2, :],
                                          ffnc[:, t3 % 4, :]), S["out2"])
            for t3 in (NCH - 3, NCH - 2, NCH - 1):
                e.wait_ge(S["out1"], t3 + 1)
                e.wait_ge(S["ffnc_"], t3 + 1)
                W(); fin(e.tensor_add(out_sb[:, t3 % 2, :],
                                      out_sb[:, t3 % 2, :],
                                      ffnc[:, t3 % 4, :]), S["out2"])

        @block.scalar
        def _(e):
            W, fin = _serial(e, S["slfa2"], SEMCNT)
            e.wait_ge(S["ini2"], 1)
            e.wait_ge(S["gst"], 1)
            W(); fin(e.activation(cfT4.rearrange("p a r -> p (a r)")[:, :],
                                  pGsT, AF.Exp), S["cft"])
            for c0 in range(2):
                e.dma_start(out=glp2[:, c0 % 2, :],
                            in_=sp_glp[c0]).then_inc(S["gld%d" % c0], 16)
                e.dma_start(out=sn2[:, c0 % 3, :], in_=sp_s[c0]).then_inc(S["sn2d%d" % c0], 16)
                e.dma_start(out=eg2[:, c0 % 4, :], in_=sp_eg[c0]).then_inc(S["egd%d" % c0], 16)

            e.wait_ge(S["li"], 1)
            W(); fin(e.activation(linit[:, 0, :], pLi, AF.Identity), S["lie"])

            def _act_tail(t):
                e.wait_ge(S["o_"], t + 1)
                for h in range(H):
                    W()
                    fin(e.activation(sqscr[:, :], po(h),
                                     AF.Square, accum_out=ssum[:, h:h + 1]),
                        S["sq"])
                W(); fin(e.activation(rstd[:, :], ssum[:, :], AF.Ln,
                                      scale=1.0 / 256.0, bias=epsb[:, 0:1]))
                W(); fin(e.activation(rstd[:, :], rstd[:, :], AF.Exp,
                                      scale=-0.5), S["rstd_"])
                if t >= 1:
                    e.wait_ge(S["ont%d" % ((t - 1) % 2)], 16 * ((t - 1) // 2 + 1))
                for h in range(H):
                    W()
                    fin(e.activation(o_n[:, h * 256:(h + 1) * 256], po(h),
                                     AF.Identity, scale=rstd[:, h:h + 1]),
                        S["on"] if h == 3 else None)
                e.wait_ge(S["on"], t + 1)
                if t >= 2:
                    e.wait_ge(S["wo"], t - 1)
                e.dma_start(out=onT[:, t % 2, :, :],
                            in_=o_n.rearrange("p (a c) -> p a c", a=8),
                            transpose=True).then_inc(S["ont%d" % (t % 2)], 16)

            for c in range(NCH):
                if c + 1 < NCH:
                    e.wait_ge(S["li"], c + 2)
                    W(); fin(e.activation(linit[:, c + 1, :], pLi,
                                          AF.Identity), S["lie"])
                e.wait_ge(S["gld%d" % (c % 2)], 16 * (c // 2 + 1))
                if c >= 2:
                    e.wait_ge(S["pd"], c - 1)
                W(); fin(e.activation(A_sb[:, c % 2, :], gc2[:, c % 2, :],
                                      AF.Exp), S["aexp"])
                e.wait_ge(S["lsub_"], c + 1)
                for h in range(H):
                    W()
                    fin(e.activation(esm[:, h * 64:(h + 1) * 64],
                                     lsub[:, h * 64:(h + 1) * 64], AF.Exp,
                                     accum_out=rsum[:, h:h + 1]),
                        S["e_"] if h == 3 else None)
                if c >= 1:
                    _act_tail(c - 1)
                if c + 2 < NCH:
                    cn = c + 2
                    e.wait_ge(S["lass"], c + 1)
                    e.dma_start(out=glp2[:, cn % 2, :],
                                in_=sp_glp[cn]).then_inc(S["gld%d" % (cn % 2)], 16)
                    e.wait_ge(S["T_"], cn - 2)
                    e.dma_start(out=sn2[:, cn % 3, :], in_=sp_s[cn]).then_inc(S["sn2d%d" % (cn % 3)], 16)
                    if c >= 1:
                        e.wait_ge(S["lnd"], c - 1)
                    e.dma_start(out=eg2[:, cn % 4, :], in_=sp_eg[cn]).then_inc(S["egd%d" % (cn % 4)], 16)
            _act_tail(NCH - 1)

        @block.vector
        def _(e):
            W, fin = _serial(e, S["slfv2"], SEMCNT)
            e.wait_ge(S["ldc"], 80)
            fin(e.tensor_copy(fbi_b[:, :], fbi_r[:, :]), S["fbi"])
            e.wait_ge(S["galL"], 32)
            W(); fin(e.tensor_copy(gallb[:, :], gall[:, :]), S["galb"])
            e.wait_ge(S["cft"], 1)
            for r in range(8):
                e.wait_ge(S["ldr%d" % (r % 2)], 32 * (r // 2 + 1))
                for blk in range(8):
                    hh = blk // 2
                    W()
                    fin(e.scalar_tensor_tensor(
                        hkiT[:, blk * 128:(blk + 1) * 128],
                        hkr[:, r % 2, blk * 128:(blk + 1) * 128],
                        cfT4[:, hh, r:r + 1],
                        hkiT[:, blk * 128:(blk + 1) * 128],
                        ALU.mult, ALU.add))
                for hh in range(H):
                    W()
                    fin(e.scalar_tensor_tensor(
                        hvi[:, hh, :], hvr[:, r % 2, hh * 256:(hh + 1) * 256],
                        cfT4[:, hh, r:r + 1], hvi[:, hh, :],
                        ALU.mult, ALU.add), S["cmb"] if hh == 3 else None)
            W(); fin(e.tensor_copy(hkiTb.rearrange("p a c -> p (a c)")[:, :],
                                   hkiT[:, :]), S["hkib"])
            e.wait_ge(S["hkitp"], 1)
            W(); fin(e.tensor_copy(hki.rearrange("p a m -> p (a m)")[:, :],
                                   pHkT), S["hki_"])
            def _dve_clip(t):
                e.wait_ge(S["ffn"], t + 1)
                if t >= 4:
                    e.wait_ge(S["out2"], t - 3)
                W(); fin(e.tensor_scalar(ffnc[:, t % 4, :], pfa, 100.0, -100.0,
                                         ALU.min, ALU.max), S["ffnc_"])

            def _dve_tp(t):
                e.wait_ge(S["T_"], t + 1)
                W(); fin(e.tensor_copy(tp_sb.rearrange("p a c -> p (a c)")[:, :],
                                       PSB[0:64, :]), S["tp"])

            def _dve_wm(t):
                e.wait_ge(S["wt"], t + 1)
                W(); fin(e.tensor_mul(wm_sb.rearrange("p a c -> p (a c)")[:, :],
                                      pW[:, :], tri4_sb[:, :]), S["wm"])

            def _dve_tail2(t):
                e.wait_ge(S["wo"], t + 1)
                e.wait_ge(S["x2d%d" % (t % 2)], 16 * (t // 2 + 1))
                if t >= 2:
                    e.wait_ge(S["ost%d" % (t % 2)], 16 * (t // 2))
                W(); fin(e.tensor_add(out_sb[:, t % 2, :], x2[:, t % 2, :],
                                      pfa), S["out1"])

            for c in range(NCH):
                if c >= 1:
                    _dve_clip(c - 1)
                e.wait_ge(S["pdm"], c + 1)
                W(); fin(e.tensor_copy(dcol[:, :], pDc), S["dcol_"])
                e.wait_ge(S["lie"], c + 1)
                W(); fin(e.tensor_mul(lscr[:, :], linit[:, c, :], pDb), S["lt"])
                e.wait_ge(S["gld%d" % (c % 2)], 16 * (c // 2 + 1))
                W(); fin(e.tensor_add(lscr2[:, :], lscr[:, :], lp2[:, c % 2, :]))
                if c >= 1:
                    _dve_tp(c - 1)
                if c >= 2:
                    _dve_tail2(c - 2)
                e.wait_ge(S["aexp"], c + 1)
                W(); fin(e.tensor_mul(lscr2[:, :], lscr2[:, :], A_sb[:, c % 2, :]),
                         S["lass"])
                W(); fin(e.reduce_max(nmax[:, :],
                                      lscr2.rearrange("p (s m) -> p s m", s=4),
                                      axis=AX.X, negate=True), S["nmax_"])
                if c >= 1:
                    _dve_wm(c - 1)
                for h in range(H):
                    W()
                    fin(e.tensor_scalar_add(lsub[:, h * 64:(h + 1) * 64],
                                            lscr2[:, h * 64:(h + 1) * 64],
                                            nmax[:, h:h + 1]),
                        S["lsub_"] if h == 3 else None)
                e.wait_ge(S["e_"], c + 1)
                W(); fin(e.reciprocal(rcp[:, :], rsum[:, :]), S["rcp_"])
                if c >= 3:
                    e.wait_ge(S["T_"], c - 1)
                for h in range(H):
                    W()
                    fin(e.scalar_tensor_tensor(pd_sb[:, c % 2, h * 64:(h + 1) * 64],
                                               esm[:, h * 64:(h + 1) * 64],
                                               rcp[:, h:h + 1],
                                               A_sb[:, c % 2, h * 64:(h + 1) * 64],
                                               ALU.mult, ALU.mult),
                        S["pd"] if h == 3 else None)
                if c >= 2:
                    e.wait_ge(S["o_"], c - 1)
                e.wait_ge(S["hv2d%d" % (c % 2)], 16 * (c // 2 + 1))
                for h in range(H):
                    W()
                    fin(e.scalar_tensor_tensor(hv0[:, c % 2, h, :], hvi[:, h, :],
                                               dcol[:, h:h + 1],
                                               hv2[:, c % 2, h * 256:(h + 1) * 256],
                                               ALU.mult, ALU.add),
                        S["hv0_"] if h == 3 else None)
            _dve_clip(NCH - 1)
            _dve_tp(NCH - 1)
            _dve_wm(NCH - 1)
            _dve_tail2(NCH - 2)
            _dve_tail2(NCH - 1)

        @block.tensor
        def _(e):
            e.wait_ge(S["galb"], 1)
            e.wait_ge(S["ldc"], 80)
            for h in range(H):
                op = e.matmul(pGsT[:, h * 8:(h + 1) * 8],
                              gallb[:, h * 64:(h + 1) * 64], sel_sb[:, :],
                              start=True, stop=True)
            op.then_inc(S["gst"], 1)
            e.wait_ge(S["hkib"], 1)
            for blk in range(8):
                op = e.transpose(pHkT[:, blk * 64:(blk + 1) * 64],
                                 hkiTb[:, blk, :], ident[0:64, 0:64])
            op.then_inc(S["hkitp"], 1)
            def _linit(c2):
                g2 = c2 // 4
                cc2 = c2 % 4
                if c2 == 0:
                    e.wait_ge(S["hki_"], 1)
                e.wait_ge(S["qg%d" % (g2 % 2)], 16 * (g2 // 2 + 1))
                e.wait_ge(S["lie"], c2)
                for h in range(H):
                    e.matmul(pLi[:, h * 64:(h + 1) * 64],
                             qg[:, g2 % 2, cc2, 2 * h, :],
                             hki[:, 2 * h, :], start=True, stop=False)
                    op = e.matmul(pLi[:, h * 64:(h + 1) * 64],
                                  qg[:, g2 % 2, cc2, 2 * h + 1, :],
                                  hki[:, 2 * h + 1, :], start=False, stop=True)
                op.then_inc(S["li"], 1)

            _linit(0)
            _linit(1)

            def _pe_d(p):
                e.wait_ge(S["dbf_"], p + 1)
                e.wait_ge(S["lt"], p)
                e.wait_ge(S["dcol_"], p)
                e.matmul(pDb, onesb[:, :], dbf[:, :], start=True, stop=True)
                for h in range(H):
                    op = e.matmul(pDc[:, h:h + 1], dbf[0:1, h * 64:(h + 1) * 64],
                                  one1[:, :], start=True, stop=True)
                op.then_inc(S["pdm"], 1)

            def _pe_ffn(p, half):
                g = p // 4
                cc = p % 4
                if half == 0:
                    e.wait_ge(S["xng%d" % (g % 2)], 16 * (g // 2 + 1))
                    e.wait_ge(S["ldw"], 32)
                    e.wait_ge(S["fbi"], 1)
                    e.wait_ge(S["ffnc_"], p)
                    if p >= 1:
                        e.wait_ge(S["out1"], p - 1)
                for dt in range(8):
                    e.matmul(pfa[:, half * 512:(half + 1) * 512],
                             xnT2[:, g % 2, cc, dt, :],
                             fwT[:, half * 4:(half + 1) * 4, dt, :],
                             start=(dt == 0), stop=False)
                op = e.matmul(pfa[:, half * 512:(half + 1) * 512],
                              onesb[:, :],
                              fbi_b[0:1, half * 512:(half + 1) * 512],
                              start=False, stop=True)
                if half == 1:
                    op.then_inc(S["ffn"], 1)

            def _pe_T(p):
                e.wait_ge(S["pd"], p + 1)
                e.wait_ge(S["sn2d%d" % (p % 3)], 16 * (p // 3 + 1))
                e.wait_ge(S["tp"], p)
                if p == 0:
                    e.wait_ge(S["hki_"], 1)
                for h in range(H):
                    e.transpose(PSB[0:64, h * 128:(h + 1) * 128],
                                sn2[:, p % 3, h * 64:(h + 1) * 64], ident[:, :])
                for h in range(H):
                    op = e.transpose(PSB[0:64, 512 + h * 128:512 + (h + 1) * 128],
                                     pd_sb[:, p % 2, h * 64:(h + 1) * 64],
                                     ident[:, :])
                op.then_inc(S["T_"], 1)

            def _pe_wt(p):
                e.wait_ge(S["tp"], p + 1)
                e.wait_ge(S["wm"], p)
                for h in range(H):
                    op = e.matmul(pW[:, h * 128:(h + 1) * 128],
                                  tp_sb[:, h, :], tp_sb[:, 4 + h, :],
                                  start=True, stop=True)
                op.then_inc(S["wt"], 1)

            def _pe_o(p):
                e.wait_ge(S["wm"], p + 1)
                e.wait_ge(S["hv0_"], p + 1)
                e.wait_ge(S["v2d%d" % (p % 2)], 16 * (p // 2 + 1))
                if p >= 1:
                    e.wait_ge(S["on"], p)
                for h in range(H):
                    e.matmul(po(h),
                             wm_sb[:, h, :], v2[:, p % 2, h * 256:(h + 1) * 256],
                             start=True, stop=False)
                    op = e.matmul(po(h),
                                  tp_sb[:, 4 + h, :], hv0[:, p % 2, h, :],
                                  start=False, stop=True)
                op.then_inc(S["o_"], 1)

            def _pe_wo(t):
                e.wait_ge(S["ont%d" % (t % 2)], 16 * (t // 2 + 1))
                e.wait_ge(S["ffnc_"], min(t + 2, NCH))
                e.wait_ge(S["out1"], t)
                for vt in range(8):
                    for half in range(2):
                        op = e.matmul(pfa[:, half * 512:(half + 1) * 512],
                                      onT[:, t % 2, vt, :],
                                      wo_sb[:, vt, half * 512:(half + 1) * 512],
                                      start=(vt == 0), stop=(vt == 7))
                op.then_inc(S["wo"], 1)

            for c in range(NCH):
                _pe_d(c)
                if c >= 1:
                    _pe_T(c - 1)
                if c >= 2:
                    _pe_wo(c - 2)
                if c >= 1:
                    _pe_wt(c - 1)
                    _pe_o(c - 1)
                _pe_ffn(c, 0)
                _pe_ffn(c, 1)
                if c + 2 < NCH:
                    _linit(c + 2)
            _pe_T(NCH - 1)
            _pe_wo(NCH - 2)
            _pe_wt(NCH - 1)
            _pe_o(NCH - 1)
            _pe_wo(NCH - 1)


def build_fused():
    nc = bass.Bass("TRN2", target_bir_lowering=False, debug=False)
    d1 = _decl_l1_inputs(nc)
    d2 = _decl_l2_inputs(nc)
    d = {**d1, **d2}
    d["xseg"] = d1["xseg"]
    d["triu_in"] = d1["triu_in"]
    d["ident_in"] = d1["ident_in"]
    for nm, shp, dt in L1_SPILLS:
        d[nm] = nc.dram_tensor(nm, shp, dt).ap()
    summ_h = nc.dram_tensor("summ_int", [64, 2304], F32)
    ag_h = nc.dram_tensor("ag_int", [512, 2304], F32)
    d["summ"] = summ_h.ap()
    d["ag"] = ag_h.ap()
    d["outseg"] = nc.dram_tensor("outseg", [NT, D], F32,
                                 kind="ExternalOutput").ap()
    _emit_l1(nc, d)
    nc.reset()
    _emit_l2(nc, d, cc=(summ_h.ap().opt(), ag_h.ap().opt()))
    return nc



def _host_consts():
    j = np.arange(C)[:, None]
    i = np.arange(C)[None, :]
    tri_ji = (j <= i).astype(np.float32)
    trineg = np.ascontiguousarray((tri_ji * np.float32(NEG8)).astype(ml_dtypes.bfloat16))
    triu_b = np.ascontiguousarray(tri_ji.astype(ml_dtypes.bfloat16))
    tri4_b = np.ascontiguousarray(np.tile(tri_ji, (1, 4)).astype(ml_dtypes.bfloat16))
    ident = np.ascontiguousarray(np.eye(C, dtype=ml_dtypes.bfloat16))
    return trineg, triu_b, tri4_b, ident


def _sel9(core):
    b0 = 4 * (core // 4)
    rank = core
    sel = np.zeros((9, 8), np.float32)
    for r in range(8):
        if b0 <= r < rank:
            for u in range(8):
                if r < u < rank:
                    sel[u, r] = 1.0
        else:
            sel[8, r] = 1.0
    return np.ascontiguousarray(sel.astype(ml_dtypes.bfloat16))


_NCF = None


def kernel(**inputs):
    global _NCF
    x = np.ascontiguousarray(np.asarray(inputs["L0_attn_x"], np.float32))
    B, T, _ = x.shape
    xf = x.reshape(NCORES, NT, D)
    trineg, triu_b, tri4_b, ident = _host_consts()
    negrow = np.full((1, 256), -1e9, np.float32)

    if _NCF is None:
        _NCF = build_fused()
    ins = []
    for cidx in range(NCORES):
        ins.append({
            "xseg": np.ascontiguousarray(xf[cidx]),
            "Wq": np.asarray(inputs["L0_attn_Wq"], np.float32),
            "Wk": np.asarray(inputs["L0_attn_Wk"], np.float32),
            "Wv": np.asarray(inputs["L0_attn_Wv"], np.float32),
            "Wf": np.asarray(inputs["L0_attn_Wf"], np.float32),
            "Wo": np.asarray(inputs["L0_attn_Wo"], np.float32),
            "gnwc": np.ascontiguousarray(
                np.asarray(inputs["L0_attn_g_norm_w"], np.float32
                           ).reshape(2, 128).T),
            "fw": np.asarray(inputs["L0_ffn_weights"], np.float32),
            "fscc": np.ascontiguousarray(
                np.asarray(inputs["L0_ffn_scale"], np.float32
                           ).reshape(8, 128).T),
            "fbias": np.asarray(inputs["L0_ffn_bias"], np.float32),
            "trineg": trineg, "triu_in": triu_b, "tri4_in": tri4_b,
            "ident_in": ident,
            "sel9": _sel9(cidx), "negrow": negrow,
        })
    res = run_bass_kernel_spmd(_NCF, ins, core_ids=list(range(NCORES)))
    out = np.concatenate([res.results[cidx]["outseg"][None]
                          for cidx in range(NCORES)], axis=0)
    return np.ascontiguousarray(out.reshape(B, T, D).astype(np.float32))


# version 39
# speedup vs baseline: 1.0473x; 1.0071x over previous
import numpy as np
import ml_dtypes
from contextlib import ExitStack

import concourse.bass as bass
import concourse.mybir as mybir
from concourse.bass_utils import run_bass_kernel_spmd

F32 = mybir.dt.float32
BF16 = mybir.dt.bfloat16
AF = mybir.ActivationFunctionType
ALU = mybir.AluOpType
AX = mybir.AxisListType

NCORES = 8
NT = 4096
C = 128
NCH = 32
NG = 8
H = 4
D = 1024
EPS = 1e-5
SCL = 0.0625
NEG8 = -0.125


def _sems(nc, E, names):
    return {n: E(nc.semaphore(n)) for n in names}


def _serial(e, selfsem, semcnt):
    last = [None]

    def W():
        if last[0] is not None:
            e.wait_ge(last[0][0], last[0][1])

    def fin(ins, inc=None, n=1):
        sem = selfsem if inc is None else inc
        v = 1 if inc is None else n
        ins.then_inc(sem, v)
        semcnt[sem] = semcnt.get(sem, 0) + v
        last[0] = (sem, semcnt[sem])
        return ins

    return W, fin



L1_STAGES = {"ln", "proj", "chunk", "summ"}

L1_SPILLS = [("sp_q", [NG, 128, 4096], BF16), ("sp_xn", [NG, 128, 4096], BF16),
             ("sp_s", [NCH, 128, 256], BF16),
             ("sp_glp", [NCH, 128, 512], F32),
             ("sp_v", [NCH, 128, 1024], BF16),
             ("sp_hv", [NCH, 64, 1024], BF16), ("sp_eg", [NCH, 1, 256], F32),
             ("sp_wo", [1024, 1024], BF16), ("sp_fwT", [1024, 1024], BF16)]


def _decl_l1_inputs(nc):
    d = {}
    d["xseg"] = nc.dram_tensor("xseg", [NT, D], F32, kind="ExternalInput").ap()
    d["Wq"] = nc.dram_tensor("Wq", [D, D], F32, kind="ExternalInput").ap()
    d["Wk"] = nc.dram_tensor("Wk", [D, D], F32, kind="ExternalInput").ap()
    d["Wv"] = nc.dram_tensor("Wv", [D, D], F32, kind="ExternalInput").ap()
    d["Wf"] = nc.dram_tensor("Wf", [D, 256], F32, kind="ExternalInput").ap()
    d["trineg"] = nc.dram_tensor("trineg", [C, C], BF16, kind="ExternalInput").ap()
    d["triu_in"] = nc.dram_tensor("triu_in", [C, C], BF16, kind="ExternalInput").ap()
    d["tri4_in"] = nc.dram_tensor("tri4_in", [C, 512], BF16, kind="ExternalInput").ap()
    d["ident_in"] = nc.dram_tensor("ident_in", [C, C], BF16, kind="ExternalInput").ap()
    return d


def build_l1(stages=None, clvl=4):
    nc = bass.Bass("TRN2", target_bir_lowering=False, debug=False)
    d = _decl_l1_inputs(nc)
    d["Wo"] = nc.dram_tensor("Wo", [D, D], F32, kind="ExternalInput").ap()
    d["fw"] = nc.dram_tensor("fw", [D, D], F32, kind="ExternalInput").ap()
    d["fscc"] = nc.dram_tensor("fscc", [128, 8], F32, kind="ExternalInput").ap()
    d["gnwc"] = nc.dram_tensor("gnwc", [128, 2], F32, kind="ExternalInput").ap()
    for nm, shp, dt in L1_SPILLS:
        d[nm] = nc.dram_tensor(nm, shp, dt, kind="ExternalOutput").ap()
    d["summ"] = nc.dram_tensor("summ", [64, 2304], F32, kind="ExternalOutput").ap()
    _emit_l1(nc, d, stages, clvl)
    return nc


def _emit_l1(nc, dmap, stages=None, clvl=4):
    stages = L1_STAGES if stages is None else stages
    ln_on = "ln" in stages
    pj_on = "proj" in stages
    ch_on = "chunk" in stages
    sm_on = "summ" in stages and clvl >= 4
    xseg = dmap["xseg"]; Wq = dmap["Wq"]; Wk = dmap["Wk"]; Wv = dmap["Wv"]
    Wf = dmap["Wf"]; trineg = dmap["trineg"]; triu_in = dmap["triu_in"]
    tri4_in = dmap["tri4_in"]; ident_in = dmap["ident_in"]
    sp_q = dmap["sp_q"]; sp_xn = dmap["sp_xn"]; sp_s = dmap["sp_s"]
    sp_glp = dmap["sp_glp"]; sp_v = dmap["sp_v"]
    sp_hv = dmap["sp_hv"]; sp_eg = dmap["sp_eg"]; summ = dmap["summ"]
    prep = "Wo" in dmap
    if prep:
        Wo = dmap["Wo"]; fw = dmap["fw"]
        fscc_in = dmap["fscc"]; gnwc_in = dmap["gnwc"]
        sp_wo = dmap["sp_wo"]; sp_fwT = dmap["sp_fwT"]

    with ExitStack() as ctx:
        E = ctx.enter_context
        xin = E(nc.sbuf_tensor("xin", [128, 4, 1024], F32))
        lnst = E(nc.sbuf_tensor("lnst", [128, 2, 8], F32))
        sqs = E(nc.sbuf_tensor("sqs", [128, 1024], F32))
        xn_tm = E(nc.sbuf_tensor("xn_tm", [128, 4, 1024], BF16))
        xnT = E(nc.sbuf_tensor("xnT", [128, 2, 4, 8, 128], BF16))
        wq_sb = E(nc.sbuf_tensor("wq_sb", [128, 8, 1024], BF16))
        wk_sb = E(nc.sbuf_tensor("wk_sb", [128, 8, 1024], BF16))
        wv_sb = E(nc.sbuf_tensor("wv_sb", [128, 8, 1024], BF16))
        wf_sb = E(nc.sbuf_tensor("wf_sb", [128, 8, 256], BF16))
        qT_sb = E(nc.sbuf_tensor("qT_sb", [128, 4, 8, 128], BF16))
        kT_sb = E(nc.sbuf_tensor("kT_sb", [128, 4, 8, 128], BF16))
        qsil = E(nc.sbuf_tensor("qsil", [128, 2, 512], F32))
        k_tm = E(nc.sbuf_tensor("k_tm", [128, 2, 1024], BF16))
        v_sb = E(nc.sbuf_tensor("v_sb", [128, 2, 1024], BF16))
        fA = E(nc.sbuf_tensor("fA", [128, 256], F32))
        fB = E(nc.sbuf_tensor("fB", [128, 256], F32))
        fC = E(nc.sbuf_tensor("fC", [128, 256], F32))
        g0 = E(nc.sbuf_tensor("g0", [128, 256], F32))
        g0h = E(nc.sbuf_tensor("g0h", [128, 256], BF16))
        g0l = E(nc.sbuf_tensor("g0l", [128, 256], BF16))
        eg = E(nc.sbuf_tensor("eg", [128, 256], F32))
        s_sb = E(nc.sbuf_tensor("s_sb", [128, 256], F32))
        EnG = E(nc.sbuf_tensor("EnG", [128, 256], F32))
        snt = E(nc.sbuf_tensor("snt", [128, 2, 256], BF16))
        s2 = E(nc.sbuf_tensor("s2", [128, 256], BF16))
        glp = E(nc.sbuf_tensor("glp", [128, 2, 512], F32))
        gcs = glp[:, :, 0:256]
        lps = glp[:, :, 256:512]
        egt_f = E(nc.sbuf_tensor("egt_f", [64, 2, 256], F32))
        egt_bf = E(nc.sbuf_tensor("egt_bf", [1, 256], BF16))
        egt2_bf = E(nc.sbuf_tensor("egt2_bf", [1, 512], BF16))
        gacc = E(nc.sbuf_tensor("gacc", [64, 256], F32))
        smaskT = E(nc.sbuf_tensor("smaskT", [128, 4, 128], BF16))
        hkl = E(nc.sbuf_tensor("hkl", [128, 8, 64], BF16))
        hkt = E(nc.sbuf_tensor("hkt", [128, 512], F32))
        hvl = E(nc.sbuf_tensor("hvl", [64, 4, 256], BF16))
        hvsp = E(nc.sbuf_tensor("hvsp", [64, 4, 1024], BF16))
        ecol = E(nc.sbuf_tensor("ecol", [64, 4], F32))
        onesb = E(nc.sbuf_tensor("onesb", [1, 128], BF16))
        neg8c = E(nc.sbuf_tensor("neg8c", [128, 128], BF16))
        epsb = E(nc.sbuf_tensor("epsb", [128, 1], F32))
        trin_sb = E(nc.sbuf_tensor("trin_sb", [128, 128], BF16))
        triu_sb = E(nc.sbuf_tensor("triu_sb", [128, 128], BF16))
        tri4_sb = E(nc.sbuf_tensor("tri4_sb", [128, 512], BF16))
        ident = E(nc.sbuf_tensor("ident", [128, 128], BF16))
        summ_sb = E(nc.sbuf_tensor("summ_sb", [64, 2304], F32))
        hktT = E(nc.sbuf_tensor("hktT", [64, 8, 128], BF16))
        if prep:
            ftmp = E(nc.sbuf_tensor("ftmp", [128, 2, 1024], BF16))
            ftmpT = E(nc.sbuf_tensor("ftmpT", [128, 2, 8, 128], BF16))
            wotmp = E(nc.sbuf_tensor("wotmp", [128, 2, 1024], BF16))
            fscc_sb = E(nc.sbuf_tensor("fscc_sb", [128, 8], F32))
            gnwc_sb = E(nc.sbuf_tensor("gnwc_sb", [128, 2], F32))

        pjA = E(nc.psum_tensor("pjA", [128, 512], F32))
        pjB = E(nc.psum_tensor("pjB", [128, 512], F32))
        pb4 = E(nc.psum_tensor("pb4", [128, 512], F32))
        pvA = E(nc.psum_tensor("pvA", [128, 512], F32))
        pvB = E(nc.psum_tensor("pvB", [128, 512], F32))
        pSb = E(nc.psum_tensor("pSb", [128, 512], F32))
        pdkT = E(nc.psum_tensor("pdkT", [128, 512], F32))
        PSB = E(nc.psum_tensor("PSB", [128, 1024], BF16))
        pj = [pjA[:, :], pjB[:, :]]
        pEb256 = pjA[:, 0:256]
        pLp = pjA[:, 256:512]
        pEb512 = pjB[:, :]
        pvh = [pvA, pvB]

        def pdv(h):
            return pvh[h // 2][0:64, (h % 2) * 256:(h % 2) * 256 + 256]
        pf = pb4[:, 0:256]
        pGt = pb4[:, 0:256]
        pGc = pb4[:, 256:512]
        pS = pSb[:, :]
        pec = pSb[0:64, 0:4]
        pdk = pdkT[:, :]

        S = _sems(nc, E, [
            "wld", "ini", "lna", "lnv", "lns", "lnr", "lnx",
            "qkp", "qke", "qsc", "qsp",
            "fp", "fc", "gp", "ghl", "gc", "eg_", "eng", "egt", "egb", "eb", "ece",
            "vp", "ve", "st", "sm", "sn", "s2d", "lp", "lps_", "gcs_",
            "dk", "hk", "dv", "hv", "hvc", "gac", "hkTp", "hkT",
            "sum_", "smd",
            "xl0", "xl1", "xl2", "xl3", "xt0", "xt1", "kt0", "kt1",
            "xnsp0", "xnsp1",
            "spgl0", "spgl1", "sps0", "sps1",
            "spv0", "spv1", "sphv0", "sphv1", "sphv2", "sphv3",
            "speg0", "speg1",
            "fld0", "fld1", "wol0", "wol1", "ftp0", "ftp1",
            "fsp0", "fsp1", "wsp0", "wsp1", "fsg", "wfd", "wcl",
            "slfp", "slfa", "slfv"])
        SEMCNT = {}

        block = E(nc.Block())

        @block.sync
        def _(e):
            if prep:
                e.dma_start(out=fscc_sb[:, :],
                            in_=fscc_in[:, :]).then_inc(S["wcl"], 16)
                e.dma_start(out=gnwc_sb[:, :],
                            in_=gnwc_in[:, :]).then_inc(S["wcl"], 16)

            def _sync_prep(i):
                if i < 8:
                    ot = i
                    e.wait_ge(S["fsg"], ot + 1)
                    if ot >= 2:
                        e.wait_ge(S["fsp%d" % (ot % 2)], 16 * ((ot - 2) // 2 + 1))
                    e.dma_start(out=ftmpT[:, ot % 2, :, :],
                                in_=ftmp[:, ot % 2, :].rearrange(
                                    "p (a c) -> p a c", a=8),
                                transpose=True).then_inc(S["ftp%d" % (ot % 2)], 16)
                    e.wait_ge(S["ftp%d" % (ot % 2)], 16 * (ot // 2 + 1))
                    e.dma_start(out=sp_fwT[ot * 128:(ot + 1) * 128, :],
                                in_=ftmpT[:, ot % 2, :, :].rearrange(
                                    "p a c -> p (a c)")
                                ).then_inc(S["fsp%d" % (ot % 2)], 16)
                else:
                    dt = i - 8
                    e.wait_ge(S["wfd"], dt + 1)
                    e.dma_start(out=sp_wo[dt * 128:(dt + 1) * 128, :],
                                in_=wotmp[:, dt % 2, :]
                                ).then_inc(S["wsp%d" % (dt % 2)], 16)

            for g in range(NG):
                for tt in range(4):
                    t = 4 * g + tt
                    if t >= 4:
                        e.wait_ge(S["lnx"], t - 3)
                    e.dma_start(out=xin[:, t % 4, :],
                                in_=xseg[t * 128:(t + 1) * 128, :]
                                ).then_inc(S["xl%d" % (t % 4)], 16)
                for tt in range(4):
                    t = 4 * g + tt
                    e.wait_ge(S["lnx"], t + 1)
                    if g >= 2:
                        if pj_on:
                            e.wait_ge(S["xnsp%d" % (g % 2)], 16 * (g // 2))
                        if ch_on:
                            e.wait_ge(S["vp"], 4 * (g - 1))
                    e.dma_start(
                        out=xnT[:, g % 2, tt, :, :],
                        in_=xn_tm[:, t % 4, :].rearrange("p (a c) -> p a c", a=8),
                        transpose=True).then_inc(S["xt%d" % (g % 2)], 16)
                for cc in range(4):
                    if not ch_on or clvl < 3:
                        break
                    c = 4 * g + cc
                    e.wait_ge(S["qke"], 16 * g + 16)
                    if c >= 2:
                        e.wait_ge(S["dk"], c - 1)
                    e.dma_start(
                        out=k_tm[:, c % 2, :].rearrange("p (a c) -> p a c", a=8),
                        in_=kT_sb[:, cc, :, :],
                        transpose=True).then_inc(S["kt%d" % (c % 2)], 16)
                if prep and g >= 1:
                    _sync_prep(2 * (g - 1))
                    _sync_prep(2 * (g - 1) + 1)
            if prep:
                _sync_prep(14)
                _sync_prep(15)
                for nm in ("fsp", "wsp"):
                    e.wait_ge(S[nm + "0"], 64)
                    e.wait_ge(S[nm + "1"], 64)

        @block.gpsimd
        def _(e):
            e.dma_start(out=wq_sb[:, :, :],
                        in_=Wq.rearrange("(a p) f -> p a f", p=128)).then_inc(S["wld"], 16)
            e.dma_start(out=wk_sb[:, :, :],
                        in_=Wk.rearrange("(a p) f -> p a f", p=128)).then_inc(S["wld"], 16)
            e.dma_start(out=wv_sb[:, :, :],
                        in_=Wv.rearrange("(a p) f -> p a f", p=128)).then_inc(S["wld"], 16)
            e.dma_start(out=wf_sb[:, :, :],
                        in_=Wf.rearrange("(a p) f -> p a f", p=128)).then_inc(S["wld"], 16)
            e.dma_start(out=trin_sb[:, :], in_=trineg[:, :]).then_inc(S["wld"], 16)
            e.dma_start(out=triu_sb[:, :], in_=triu_in[:, :]).then_inc(S["wld"], 16)
            e.dma_start(out=tri4_sb[:, :], in_=tri4_in[:, :]).then_inc(S["wld"], 16)
            e.dma_start(out=ident[:, :], in_=ident_in[:, :]).then_inc(S["wld"], 16)
            W, fin = _serial(e, S["slfp"], SEMCNT)
            fin(e.memset(onesb[:, :], 1.0))
            W(); fin(e.memset(neg8c[:, :], NEG8))
            W(); fin(e.memset(epsb[:, :], EPS))
            W(); fin(e.memset(hkl[:, :, :], 0.0))
            W(); fin(e.memset(hvl[:, :, :], 0.0))
            W(); fin(e.memset(hvsp[:, :, :], 0.0))
            W(); fin(e.memset(summ_sb[:, :], 0.0))
            W(); fin(e.memset(gacc[:, :], 0.0), S["ini"])

            if prep:
                for ot in range(8):
                    if ot >= 2:
                        e.wait_ge(S["ftp%d" % (ot % 2)],
                                  16 * ((ot - 2) // 2 + 1))
                    e.dma_start(out=ftmp[:, ot % 2, :],
                                in_=fw[ot * 128:(ot + 1) * 128, :]
                                ).then_inc(S["fld%d" % (ot % 2)], 16)
                for dt in range(8):
                    if dt >= 2:
                        e.wait_ge(S["wsp%d" % (dt % 2)],
                                  16 * ((dt - 2) // 2 + 1))
                    e.dma_start(out=wotmp[:, dt % 2, :],
                                in_=Wo[dt * 128:(dt + 1) * 128, :]
                                ).then_inc(S["wol%d" % (dt % 2)], 16)

            if not sm_on:
                return
            e.wait_ge(S["hkT"], 1)
            e.wait_ge(S["hv"], NCH)
            e.wait_ge(S["gac"], NCH)
            W(); fin(e.tensor_copy(summ_sb[:, 0:1024],
                                   hktT.rearrange("p a c -> p (a c)")[:, :]))
            W(); fin(e.tensor_copy(summ_sb[:, 1024:2048],
                                   hvl.rearrange("p a m -> p (a m)")[:, :]))
            W(); fin(e.tensor_copy(summ_sb[0:1, 2048:2304], gacc[0:1, :]), S["sum_"])

        @block.scalar
        def _(e):
            W, fin = _serial(e, S["slfa"], SEMCNT)
            e.wait_ge(S["ini"], 1)

            def spills(c):
                p = c % 2
                e.wait_ge(S["gcs_"], c + 1)
                if clvl >= 3:
                    e.wait_ge(S["lps_"], c + 1)
                    e.dma_start(out=sp_glp[c],
                                in_=glp[:, p, :]).then_inc(S["spgl%d" % p], 16)
                else:
                    e.dma_start(out=sp_glp[c].rearrange(
                                    "p (a f) -> p a f", a=2)[:, 0, :],
                                in_=gcs[:, p, :]).then_inc(S["spgl%d" % p], 16)
                e.wait_ge(S["sn"], c + 1)
                e.dma_start(out=sp_s[c], in_=snt[:, p, :]).then_inc(S["sps%d" % p], 16)
                e.wait_ge(S["egt"], c + 1)
                e.dma_start(out=sp_eg[c], in_=egt_f[0:1, p, :]).then_inc(S["speg%d" % p], 16)
                if clvl >= 2:
                    e.wait_ge(S["ve"], c + 1)
                    e.dma_start(out=sp_v[c], in_=v_sb[:, p, :]).then_inc(S["spv%d" % p], 16)
                if clvl >= 4:
                    e.wait_ge(S["hvc"], c)
                    e.dma_start(out=sp_hv[c],
                                in_=hvsp[:, c % 4, :]).then_inc(S["sphv%d" % (c % 4)], 16)

            for g in range(NG):
                def _sq(t):
                    e.wait_ge(S["xl%d" % (t % 4)], 16 * (t // 4 + 1))
                    if t >= 2:
                        e.wait_ge(S["lnx"], t - 1)
                    W()
                    fin(e.activation(sqs[:, :], xin[:, t % 4, :], AF.Square,
                                     accum_out=lnst[:, t % 2, 0:1]), S["lna"])

                def _sqrt(t):
                    e.wait_ge(S["lnv"], t + 1)
                    W()
                    fin(e.activation(lnst[:, t % 2, 5:6], lnst[:, t % 2, 4:5],
                                     AF.Sqrt, bias=epsb[:, 0:1]), S["lns"])

                def _apply(t):
                    e.wait_ge(S["lnr"], t + 1)
                    if g >= 1:
                        e.wait_ge(S["xt%d" % ((g - 1) % 2)],
                                  64 * ((g - 1) // 2 + 1))
                    W()
                    fin(e.activation(xn_tm[:, t % 4, :], xin[:, t % 4, :],
                                     AF.Identity, bias=lnst[:, t % 2, 7:8],
                                     scale=lnst[:, t % 2, 6:7]), S["lnx"])
                t0 = 4 * g
                _sq(t0); _sq(t0 + 1); _sqrt(t0); _apply(t0)
                _sq(t0 + 2); _sqrt(t0 + 1); _apply(t0 + 1)
                _sq(t0 + 3); _sqrt(t0 + 2); _apply(t0 + 2)
                _sqrt(t0 + 3); _apply(t0 + 3)
                if g >= 1 and pj_on:
                    e.wait_ge(S["qsc"], 8 * g)
                    e.dma_start(out=sp_q[g - 1].rearrange("p (a c) -> p a c", a=8),
                                in_=qT_sb.rearrange("p t a c -> p (t a c)")[:, :].rearrange(
                          "p (a c) -> p a c", a=8)).then_inc(S["qsp"], 16)
                    e.wait_ge(S["xt%d" % ((g - 1) % 2)],
                              64 * ((g - 1) // 2 + 1))
                    e.dma_start(out=sp_xn[g - 1].rearrange("p (a c) -> p a c", a=8),
                                in_=xnT[:, (g - 1) % 2, :, :, :].rearrange(
                          "p t a c -> p (t a c)").rearrange(
                          "p (a c) -> p a c", a=8)).then_inc(S["xnsp%d" % ((g - 1) % 2)], 16)
                for idx in range(16):
                    if not pj_on:
                        break
                    e.wait_ge(S["qkp"], 16 * g + idx + 1)
                    if idx < 8:
                        W()
                        fin(e.activation(qsil[:, idx % 2, :], pj[idx % 2],
                                         AF.Silu), S["qke"])
                        W()
                        fin(e.activation(qT_sb[:, :, idx, :],
                                         qsil[:, idx % 2, :].rearrange(
                                             "p (a c) -> p a c", a=4),
                                         AF.Identity, scale=SCL), S["qsc"])
                    else:
                        W()
                        fin(e.activation(kT_sb[:, :, idx - 8, :], pj[idx % 2],
                                         AF.Silu), S["qke"])
                for cc in range(4):
                    if not ch_on:
                        break
                    c = 4 * g + cc
                    if c >= 2:
                        spills(c - 2)
                    e.wait_ge(S["fp"], c + 1)
                    e.wait_ge(S["gc"], c)
                    W(); fin(e.activation(fA[:, :], pf, AF.Exp, scale=-1.0))
                    W(); fin(e.activation(g0[:, :], fA[:, :], AF.Ln,
                                          bias=1.0), S["fc"])
                    if clvl >= 2:
                        e.wait_ge(S["vp"], c + 1)
                        if c >= 2:
                            e.wait_ge(S["spv%d" % (c % 2)], 16 * (c // 2))
                        W(); fin(e.activation(v_sb[:, c % 2, 0:512], pvA[:, :],
                                              AF.Identity))
                        W(); fin(e.activation(v_sb[:, c % 2, 512:1024], pvB[:, :],
                                              AF.Identity), S["ve"])
                    e.wait_ge(S["fc"], c + 1)
                    W(); fin(e.activation(eg[:, :], g0[:, :], AF.Exp,
                                          scale=NEG8), S["eg_"])
                    e.wait_ge(S["gc"], c + 1)
                    W(); fin(e.activation(EnG[:, :], pGc, AF.Exp,
                                          scale=-1.0), S["eng"])
                    if c >= 2:
                        e.wait_ge(S["speg%d" % (c % 2)], 16 * (c // 2))
                    W(); fin(e.activation(egt_f[:, c % 2, :], pGt[0:64, :],
                                          AF.Exp), S["egt"])
                    if c >= 2:
                        e.wait_ge(S["spgl%d" % (c % 2)], 16 * (c // 2))
                    W(); fin(e.activation(gcs[:, c % 2, :], pGc,
                                          AF.Identity), S["gcs_"])
            if ch_on:
                spills(30)
                spills(31)
            if not pj_on:
                e.wait_ge(S["xt0"], 64 * 4)
                e.wait_ge(S["xt1"], 64 * 4)
                return
            e.wait_ge(S["qsc"], 64)
            e.dma_start(out=sp_q[7].rearrange("p (a c) -> p a c", a=8),
                        in_=qT_sb.rearrange("p t a c -> p (t a c)")[:, :].rearrange(
                          "p (a c) -> p a c", a=8)).then_inc(S["qsp"], 16)
            e.wait_ge(S["xt1"], 64 * 4)
            e.dma_start(out=sp_xn[7].rearrange("p (a c) -> p a c", a=8),
                        in_=xnT[:, 1, :, :, :].rearrange(
                          "p t a c -> p (t a c)").rearrange(
                          "p (a c) -> p a c", a=8)).then_inc(S["xnsp1"], 16)
            if sm_on:
                e.wait_ge(S["sum_"], 1)
                e.dma_start(out=summ[:, :], in_=summ_sb[:, :]).then_inc(S["smd"], 16)
                e.wait_ge(S["smd"], 16)
            if ch_on:
                nms = ["spgl", "sps", "speg"]
                if clvl >= 2:
                    nms.append("spv")
                for nm in nms:
                    e.wait_ge(S[nm + "0"], 16 * 16)
                    e.wait_ge(S[nm + "1"], 16 * 16)
                if clvl >= 4:
                    for q_ in range(4):
                        e.wait_ge(S["sphv%d" % q_], 16 * 8)
            e.wait_ge(S["qsp"], 16 * NG)
            e.wait_ge(S["xnsp0"], 16 * 4)
            e.wait_ge(S["xnsp1"], 16 * 4)

        @block.vector
        def _(e):
            W, fin = _serial(e, S["slfv"], SEMCNT)

            def _dve_prep(i):
                if i < 8:
                    ot = i
                    e.wait_ge(S["fld%d" % (ot % 2)], 16 * (ot // 2 + 1))
                    if ot == 0:
                        e.wait_ge(S["wcl"], 32)
                    vv = ftmp[:, ot % 2, :]
                    W(); fin(e.tensor_scalar(vv, vv, 0.0, 2.0,
                                             ALU.is_ge, ALU.mult))
                    W(); fin(e.tensor_scalar_add(vv, vv, -1.0))
                    W(); fin(e.tensor_scalar_mul(vv, vv, fscc_sb[:, ot:ot + 1]),
                             S["fsg"])
                else:
                    dt = i - 8
                    e.wait_ge(S["wol%d" % (dt % 2)], 16 * (dt // 2 + 1))
                    vv = wotmp[:, dt % 2, :]
                    W(); fin(e.tensor_scalar_mul(vv, vv,
                                                 gnwc_sb[:, dt % 2:dt % 2 + 1]),
                             S["wfd"])

            for g in range(NG):
                for tt in range(4):
                    t = 4 * g + tt
                    e.wait_ge(S["xl%d" % (t % 4)], 16 * (t // 4 + 1))
                    if t >= 2:
                        e.wait_ge(S["lnx"], t - 1)
                    W(); fin(e.reduce_sum(lnst[:, t % 2, 1:2], xin[:, t % 4, :],
                                          axis=AX.X))
                    W(); fin(e.tensor_scalar_mul(lnst[:, t % 2, 2:3],
                                                 lnst[:, t % 2, 1:2], 1.0 / D))
                    W(); fin(e.tensor_mul(lnst[:, t % 2, 3:4],
                                          lnst[:, t % 2, 2:3],
                                          lnst[:, t % 2, 2:3]))
                    e.wait_ge(S["lna"], t + 1)
                    W()
                    fin(e.scalar_tensor_tensor(lnst[:, t % 2, 4:5],
                                               lnst[:, t % 2, 0:1], 1.0 / D,
                                               lnst[:, t % 2, 3:4],
                                               ALU.mult, ALU.subtract), S["lnv"])
                    e.wait_ge(S["lns"], t + 1)
                    W(); fin(e.reciprocal(lnst[:, t % 2, 6:7],
                                          lnst[:, t % 2, 5:6]))
                    W(); fin(e.scalar_tensor_tensor(lnst[:, t % 2, 7:8],
                                                    lnst[:, t % 2, 2:3], -1.0,
                                                    lnst[:, t % 2, 6:7],
                                                    ALU.mult, ALU.mult), S["lnr"])
                for cc in range(4):
                    if not ch_on:
                        break
                    c = 4 * g + cc
                    e.wait_ge(S["fc"], c + 1)
                    if c >= 1:
                        e.wait_ge(S["gc"], c)
                    W(); fin(e.tensor_copy(g0h[:, :], g0[:, :]))
                    W(); fin(e.tensor_sub(g0l[:, :], g0[:, :], g0h[:, :]),
                             S["ghl"])
                    if clvl >= 2:
                        e.wait_ge(S["st"], c + 1)
                        W(); fin(e.tensor_mul(
                            smaskT.rearrange("p a c -> p (a c)")[:, :],
                            pS[:, :], tri4_sb[:, :]), S["sm"])
                    e.wait_ge(S["eg_"], c + 1)
                    W(); fin(e.tensor_scalar(s_sb[:, :], eg[:, :], -1.0, 1.0,
                                             ALU.mult, ALU.add))
                    e.wait_ge(S["eng"], c + 1)
                    if c >= 2:
                        e.wait_ge(S["sps%d" % (c % 2)], 16 * (c // 2))
                    W(); fin(e.tensor_mul(snt[:, c % 2, :], s_sb[:, :],
                                          EnG[:, :]), S["sn"])
                    if clvl >= 2:
                        e.wait_ge(S["egt"], c + 1)
                        W(); fin(e.tensor_copy(egt_bf[:, :], egt_f[0:1, c % 2, :]))
                        ev = egt2_bf.rearrange("p (a t m) -> p a t m", a=4, t=2)
                        gv = egt_bf.rearrange("p (a m) -> p a m", a=4)
                        W(); fin(e.tensor_copy(ev[:, :, 0, :], gv[:, :, :]))
                        W(); fin(e.tensor_copy(ev[:, :, 1, :], gv[:, :, :]), S["egb"])
                    e.wait_ge(S["gc"], c + 1)
                    e.wait_ge(S["gcs_"], c + 1)
                    W(); fin(e.tensor_add(gacc[:, :], gacc[:, :], pGt[0:64, :]), S["gac"])
                    if clvl >= 2:
                        e.wait_ge(S["eb"], c + 1)
                        W(); fin(e.tensor_copy(ecol[:, :], pec), S["ece"])
                        W(); fin(e.tensor_mul(s2[:, :], snt[:, c % 2, :],
                                              pEb256), S["s2d"])
                    if clvl >= 3:
                        e.wait_ge(S["lp"], c + 1)
                        if c >= 2:
                            e.wait_ge(S["spgl%d" % (c % 2)], 16 * (c // 2))
                        W(); fin(e.tensor_copy(lps[:, c % 2, :], pLp), S["lps_"])
                        e.wait_ge(S["dk"], c + 1)
                        W(); fin(e.tensor_add(hkt[:, :],
                                              hkl.rearrange("p a m -> p (a m)")[:, :],
                                              pdk))
                        W(); fin(e.tensor_mul(hkl.rearrange("p a m -> p (a m)")[:, :],
                                              hkt[:, :], pEb512), S["hk"])
                    if clvl >= 4:
                        e.wait_ge(S["dv"], c + 1)
                        e.wait_ge(S["ece"], c + 1)
                        for h in range(H):
                            W()
                            fin(e.scalar_tensor_tensor(
                                hvl[:, h, :], hvl[:, h, :], ecol[:, h:h + 1],
                                pdv(h),
                                ALU.mult, ALU.add), S["hv"] if h == 3 else None)
                    if clvl >= 4 and c < NCH - 1:
                        e.wait_ge(S["hv"], c + 1)
                        if c >= 3:
                            e.wait_ge(S["sphv%d" % ((c + 1) % 4)],
                                      16 * ((c - 3) // 4 + 1))
                        W()
                        fin(e.tensor_copy(hvsp[:, (c + 1) % 4, :],
                                          hvl.rearrange("p a m -> p (a m)")[:, :]),
                            S["hvc"])
                if prep:
                    _dve_prep(2 * g)
                    _dve_prep(2 * g + 1)
            if sm_on:
                e.wait_ge(S["hkTp"], 1)
                W(); fin(e.tensor_copy(hktT.rearrange("p a c -> p (a c)")[:, :],
                                       PSB[0:64, :]), S["hkT"])

        @block.tensor
        def _(e):
            e.wait_ge(S["wld"], 128)
            e.wait_ge(S["ini"], 1)
            for g in range(NG):
                if not pj_on:
                    break
                e.wait_ge(S["xt%d" % (g % 2)], 64 * (g // 2 + 1))
                if g >= 1:
                    e.wait_ge(S["qsp"], 16 * g)
                    if ch_on:
                        if clvl >= 2:
                            e.wait_ge(S["st"], 4 * g)
                            e.wait_ge(S["s2d"], 4 * g)
                        if clvl >= 3:
                            e.wait_ge(S["kt0"], 16 * 2 * g)
                            e.wait_ge(S["kt1"], 16 * 2 * g)
                            e.wait_ge(S["lps_"], 4 * g)
                            e.wait_ge(S["hk"], 4 * g)
                for idx in range(16):
                    if idx >= 2:
                        e.wait_ge(S["qke"], 16 * g + idx - 1)
                    wsb = wq_sb if idx < 8 else wk_sb
                    ft = idx % 8
                    for dt in range(8):
                        op = e.matmul(pj[idx % 2],
                                      wsb[:, dt, ft * 128:(ft + 1) * 128],
                                      xnT[:, g % 2, :, dt, :],
                                      start=(dt == 0), stop=(dt == 7))
                    op.then_inc(S["qkp"], 1)
                for cc in range(4):
                    if not ch_on:
                        break
                    c = 4 * g + cc
                    ts = slice(cc * 128, (cc + 1) * 128)
                    e.wait_ge(S["fc"], c)
                    e.wait_ge(S["egt"], c)
                    e.wait_ge(S["gac"], c)
                    e.wait_ge(S["gcs_"], c)
                    for dt in range(8):
                        op = e.matmul(pf, xnT[:, g % 2, cc, dt, :], wf_sb[:, dt, :],
                                      start=(dt == 0), stop=(dt == 7))
                    op.then_inc(S["fp"], 1)
                    if clvl < 2:
                        e.wait_ge(S["fc"], c + 1)
                        e.matmul(pGc, trin_sb[:, :], g0[:, :], start=True, stop=True)
                        op = e.matmul(pGt, neg8c[:, :], g0[:, :], start=True, stop=True)
                        op.then_inc(S["gc"], 1)
                        continue
                    e.wait_ge(S["ve"], c)
                    if clvl >= 4:
                        e.wait_ge(S["hv"], c)
                    for dt in range(8):
                        op = e.matmul(pvh[0][:, :],
                                      xnT[:, g % 2, cc, dt, :],
                                      wv_sb[:, dt, 0:512],
                                      start=(dt == 0), stop=(dt == 7))
                    e.wait_ge(S["ghl"], c + 1)
                    e.wait_ge(S["eng"], c)
                    e.wait_ge(S["gcs_"], c)
                    e.matmul(pGc, trin_sb[:, :], g0h[:, :], start=True, stop=False)
                    e.matmul(pGc, trin_sb[:, :], g0l[:, :], start=False, stop=True)
                    e.matmul(pGt, neg8c[:, :], g0h[:, :], start=True, stop=False)
                    op = e.matmul(pGt, neg8c[:, :], g0l[:, :], start=False, stop=True)
                    op.then_inc(S["gc"], 1)
                    for dt in range(8):
                        op = e.matmul(pvh[1][:, :],
                                      xnT[:, g % 2, cc, dt, :],
                                      wv_sb[:, dt, 512:1024],
                                      start=(dt == 0), stop=(dt == 7))
                    op.then_inc(S["vp"], 1)
                    e.wait_ge(S["sm"], c)
                    e.wait_ge(S["ece"], c)
                    e.wait_ge(S["qsc"], 8 * (g + 1))
                    e.wait_ge(S["qke"], 16 * (g + 1))
                    for h in range(H):
                        e.matmul(pS[:, h * 128:(h + 1) * 128],
                                 kT_sb[:, cc, 2 * h, :], qT_sb[:, cc, 2 * h, :],
                                 start=True, stop=False)
                        op = e.matmul(pS[:, h * 128:(h + 1) * 128],
                                      kT_sb[:, cc, 2 * h + 1, :],
                                      qT_sb[:, cc, 2 * h + 1, :],
                                      start=False, stop=True)
                    op.then_inc(S["st"], 1)
                    e.wait_ge(S["egb"], c + 1)
                    e.wait_ge(S["s2d"], c)
                    e.wait_ge(S["lps_"], c)
                    e.wait_ge(S["hk"], c)
                    e.wait_ge(S["sm"], c + 1)
                    e.matmul(pEb256, onesb[:, :], egt_bf[:, :], start=True, stop=True)
                    e.matmul(pEb512, onesb[:, :], egt2_bf[:, :], start=True, stop=True)
                    for h in range(H):
                        op = e.matmul(pec[:, h:h + 1],
                                      egt_bf[0:1, h * 64:(h + 1) * 64],
                                      onesb[0:1, 0:1], start=True, stop=True)
                    op.then_inc(S["eb"], 1)
                    if clvl < 3:
                        continue
                    e.wait_ge(S["sn"], c + 1)
                    e.wait_ge(S["s2d"], c + 1)
                    e.wait_ge(S["lps_"], c)
                    for h in range(H):
                        e.matmul(pLp[:, h * 64:(h + 1) * 64],
                                 qT_sb[:, cc, 2 * h, :], hkl[:, 2 * h, :],
                                 start=True, stop=False)
                        e.matmul(pLp[:, h * 64:(h + 1) * 64],
                                 qT_sb[:, cc, 2 * h + 1, :], hkl[:, 2 * h + 1, :],
                                 start=False, stop=False)
                        op = e.matmul(pLp[:, h * 64:(h + 1) * 64],
                                      smaskT[:, h, :],
                                      snt[:, c % 2, h * 64:(h + 1) * 64],
                                      start=False, stop=True)
                    op.then_inc(S["lp"], 1)
                    e.wait_ge(S["s2d"], c + 1)
                    e.wait_ge(S["kt%d" % (c % 2)], 16 * (c // 2 + 1))
                    for blk in range(8):
                        h = blk // 2
                        op = e.matmul(pdk[:, blk * 64:(blk + 1) * 64],
                                      k_tm[:, c % 2, blk * 128:(blk + 1) * 128],
                                      snt[:, c % 2, h * 64:(h + 1) * 64],
                                      start=True, stop=True)
                    op.then_inc(S["dk"], 1)
                    if clvl < 4:
                        continue
                    e.wait_ge(S["ve"], c + 1)
                    for h in range(H):
                        op = e.matmul(pdv(h),
                                      s2[:, h * 64:(h + 1) * 64],
                                      v_sb[:, c % 2, h * 256:(h + 1) * 256],
                                      start=True, stop=True)
                    op.then_inc(S["dv"], 1)
            if sm_on:
                e.wait_ge(S["hk"], NCH)
                for blk in range(8):
                    op = e.transpose(PSB[0:64, blk * 128:(blk + 1) * 128],
                                     hkl[:, blk, :], ident[:, :])
                op.then_inc(S["hkTp"], 1)



def _decl_l2_inputs(nc):
    d = {}
    d["Wo"] = nc.dram_tensor("Wo", [D, D], F32, kind="ExternalInput").ap()
    d["fw"] = nc.dram_tensor("fw", [D, D], F32, kind="ExternalInput").ap()
    d["fscc"] = nc.dram_tensor("fscc", [128, 8], F32, kind="ExternalInput").ap()
    d["gnwc"] = nc.dram_tensor("gnwc", [128, 2], F32, kind="ExternalInput").ap()
    d["fbias"] = nc.dram_tensor("fbias", [D], F32, kind="ExternalInput").ap()
    d["sel9"] = nc.dram_tensor("sel9", [9, 8], BF16, kind="ExternalInput").ap()
    d["negrow"] = nc.dram_tensor("negrow", [1, 256], F32, kind="ExternalInput").ap()
    return d


def build_l2():
    nc = bass.Bass("TRN2", target_bir_lowering=False, debug=False)
    d = _decl_l2_inputs(nc)
    d["xseg"] = nc.dram_tensor("xseg", [NT, D], F32, kind="ExternalInput").ap()
    d["triu_in"] = nc.dram_tensor("triu_in", [C, C], BF16, kind="ExternalInput").ap()
    d["tri4_in"] = nc.dram_tensor("tri4_in", [C, 512], BF16, kind="ExternalInput").ap()
    d["ident_in"] = nc.dram_tensor("ident_in", [C, C], BF16, kind="ExternalInput").ap()
    d["ag"] = nc.dram_tensor("ag", [512, 2304], F32, kind="ExternalInput").ap()
    for nm, shp, dt in L1_SPILLS:
        d[nm] = nc.dram_tensor(nm, shp, dt, kind="ExternalInput").ap()
    d["outseg"] = nc.dram_tensor("outseg", [NT, D], F32, kind="ExternalOutput").ap()
    _emit_l2(nc, d, cc=None)
    return nc


def _emit_l2(nc, dmap, cc=None):
    xseg = dmap["xseg"]; fbias = dmap["fbias"]; triu_in = dmap["triu_in"]
    tri4_in = dmap["tri4_in"]; ident_in = dmap["ident_in"]
    sel9 = dmap["sel9"]; negrow = dmap["negrow"]
    sp_wo = dmap["sp_wo"]; sp_fwT = dmap["sp_fwT"]
    ag = dmap["ag"]
    sp_q = dmap["sp_q"]; sp_xn = dmap["sp_xn"]; sp_s = dmap["sp_s"]
    sp_glp = dmap["sp_glp"]; sp_v = dmap["sp_v"]
    sp_hv = dmap["sp_hv"]; sp_eg = dmap["sp_eg"]; outseg = dmap["outseg"]

    with ExitStack() as ctx:
        E = ctx.enter_context
        wo_sb = E(nc.sbuf_tensor("wo_sb", [128, 8, 1024], BF16))
        fwT = E(nc.sbuf_tensor("fwT", [128, 8, 8, 128], BF16))
        fbi_r = E(nc.sbuf_tensor("fbi_r", [1, 1024], F32))
        fbi_b = E(nc.sbuf_tensor("fbi_b", [1, 1024], BF16))
        triu_sb = E(nc.sbuf_tensor("triu_sb2", [128, 128], BF16))
        tri4_sb = E(nc.sbuf_tensor("tri4_sb2", [128, 512], BF16))
        ident = E(nc.sbuf_tensor("ident2", [128, 128], BF16))
        onesb = E(nc.sbuf_tensor("onesb2", [1, 128], BF16))
        one1 = E(nc.sbuf_tensor("one1", [1, 1], BF16))
        epsb = E(nc.sbuf_tensor("epsb2", [128, 1], F32))
        sel_sb = E(nc.sbuf_tensor("sel_sb", [9, 8], BF16))
        gall = E(nc.sbuf_tensor("gall", [9, 256], F32))
        gallb = E(nc.sbuf_tensor("gallb", [9, 256], BF16))
        cfT4 = E(nc.sbuf_tensor("cfT4", [64, 4, 8], F32))
        hkr = E(nc.sbuf_tensor("hkr", [64, 4, 1024], F32))
        hvr = E(nc.sbuf_tensor("hvr", [64, 4, 1024], F32))
        hkiT = E(nc.sbuf_tensor("hkiT", [64, 1024], F32))
        hkiTb = E(nc.sbuf_tensor("hkiTb", [64, 8, 128], BF16))
        hvi = E(nc.sbuf_tensor("hvi", [64, 4, 256], F32))
        hki = E(nc.sbuf_tensor("hki", [128, 8, 64], BF16))
        qg = E(nc.sbuf_tensor("qg", [128, 2, 4, 8, 128], BF16))
        linit = E(nc.sbuf_tensor("linit", [128, 32, 256], BF16))
        xnT2 = E(nc.sbuf_tensor("xnT2", [128, 2, 4, 8, 128], BF16))
        glp2 = E(nc.sbuf_tensor("glp2", [128, 2, 512], F32))
        gc2 = glp2[:, :, 0:256]
        lp2 = glp2[:, :, 256:512]
        sn2 = E(nc.sbuf_tensor("sn2", [128, 3, 256], BF16))
        v2 = E(nc.sbuf_tensor("v2", [128, 2, 1024], BF16))
        hv2 = E(nc.sbuf_tensor("hv2", [64, 2, 1024], BF16))
        x2 = E(nc.sbuf_tensor("x2", [128, 2, 1024], F32))
        eg2 = E(nc.sbuf_tensor("eg2", [1, 4, 256], F32))
        drow = E(nc.sbuf_tensor("drow", [1, 256], F32))
        dbf = E(nc.sbuf_tensor("dbf", [1, 256], BF16))
        dcol = E(nc.sbuf_tensor("dcol", [64, 4], F32))
        A_sb = E(nc.sbuf_tensor("A_sb", [128, 2, 256], F32))
        lscr = E(nc.sbuf_tensor("lscr", [128, 256], F32))
        lscr2 = E(nc.sbuf_tensor("lscr2", [128, 256], F32))
        lsub = E(nc.sbuf_tensor("lsub", [128, 256], F32))
        esm = E(nc.sbuf_tensor("esm", [128, 256], F32))
        nmax = E(nc.sbuf_tensor("nmax", [128, 4], F32))
        rsum = E(nc.sbuf_tensor("rsum", [128, 4], F32))
        rcp = E(nc.sbuf_tensor("rcp", [128, 4], F32))
        pd_sb = E(nc.sbuf_tensor("pd_sb", [128, 2, 256], BF16))
        tp_sb = E(nc.sbuf_tensor("tp_sb", [64, 8, 128], BF16))
        wm_sb = E(nc.sbuf_tensor("wm_sb", [128, 4, 128], BF16))
        hv0 = E(nc.sbuf_tensor("hv0", [64, 2, 4, 256], BF16))
        sqscr = E(nc.sbuf_tensor("sqscr", [128, 256], F32))
        ssum = E(nc.sbuf_tensor("ssum", [128, 4], F32))
        rstd = E(nc.sbuf_tensor("rstd", [128, 4], F32))
        o_n = E(nc.sbuf_tensor("o_n", [128, 1024], BF16))
        onT = E(nc.sbuf_tensor("onT", [128, 2, 8, 128], BF16))
        ffnc = E(nc.sbuf_tensor("ffnc", [128, 4, 1024], F32))
        out_sb = E(nc.sbuf_tensor("out_sb", [128, 2, 1024], F32))

        pDbc = E(nc.psum_tensor("pDbc", [128, 512], F32))
        pWc = E(nc.psum_tensor("pWc", [128, 512], F32))
        poA = E(nc.psum_tensor("poA", [128, 512], F32))
        poB = E(nc.psum_tensor("poB", [128, 512], F32))
        pfat = E(nc.psum_tensor("pfat", [128, 1024], F32))
        PSB = E(nc.psum_tensor("PSB2", [128, 1024], BF16))
        pLib = E(nc.psum_tensor("pLib", [128, 512], F32))
        pDb = pDbc[:, 0:256]
        pDc = pDbc[0:64, 256:260]
        pLi = pLib[:, 0:256]
        pGsT = pWc[0:64, 0:32]
        pHkT = PSB[:, 0:512]
        pW = pWc[:, :]
        pfa = pfat[:, :]

        def po(h):
            t = poA if h < 2 else poB
            return t[:, (h % 2) * 256:(h % 2) * 256 + 256]

        S = _sems(nc, E, [
            "ldc", "ldw", "ini2", "galb", "galL", "gst", "cft", "cmb", "hkib",
            "hkitp", "hki_",
            "fbi", "li", "lie",
            "aexp", "lnd", "dbf_", "pdm",
            "dcol_", "lt", "lass", "nmax_", "lsub_", "e_", "rcp_", "pd",
            "T_", "tp", "wt", "wm", "hv0_", "o_", "oev", "sq", "rstd_", "on",
            "ffn", "ffnc_", "wo", "out1", "out2",
            "ldr0", "ldr1", "ldr2", "ldr3", "qg0", "qg1", "xng0", "xng1",
            "v2d0", "v2d1", "hv2d0", "hv2d1", "x2d0", "x2d1",
            "gld0", "gld1", "sn2d0", "sn2d1", "sn2d2",
            "egd0", "egd1", "egd2", "egd3", "ont0", "ont1", "ost0", "ost1",
            "ccd", "slfp2", "slfa2", "slfv2"])
        SEMCNT = {}

        block = E(nc.Block())

        @block.sync
        def _(e):
            e.dma_start(out=wo_sb[:, :, :],
                        in_=sp_wo.rearrange("(a p) f -> p a f", p=128)
                        ).then_inc(S["ldw"], 16)
            e.dma_start(out=fwT.rearrange("p a b c -> p a (b c)"),
                        in_=sp_fwT.rearrange("(a p) f -> p a f", p=128)
                        ).then_inc(S["ldw"], 16)
            e.dma_start(out=triu_sb[:, :], in_=triu_in[:, :]).then_inc(S["ldc"], 16)
            e.dma_start(out=tri4_sb[:, :], in_=tri4_in[:, :]).then_inc(S["ldc"], 16)
            e.dma_start(out=ident[:, :], in_=ident_in[:, :]).then_inc(S["ldc"], 16)
            e.dma_start(out=sel_sb[:, :], in_=sel9[:, :]).then_inc(S["ldc"], 16)
            e.dma_start(out=fbi_r[:, :],
                        in_=fbias.rearrange("(a d) -> a d", a=1)).then_inc(S["ldc"], 16)
            if cc is not None:
                e.wait_ge(S["ccd"], 1)
            e.dma_start(out=gall[0:8, :],
                        in_=ag.rearrange("(r i) c -> r i c", i=64)[:, 0, 2048:2304]
                        ).then_inc(S["galL"], 16)
            e.dma_start(out=gall[8:9, :], in_=negrow[:, :]).then_inc(S["galL"], 16)
            for r in range(8):
                if r >= 4:
                    e.wait_ge(S["cmb"], r - 3)
                e.dma_start(out=hkr[:, r % 4, :],
                            in_=ag[r * 64:r * 64 + 64, 0:1024]).then_inc(S["ldr%d" % (r % 4)], 16)
                e.dma_start(out=hvr[:, r % 4, :],
                            in_=ag[r * 64:r * 64 + 64, 1024:2048]).then_inc(S["ldr%d" % (r % 4)], 16)
            def _qgload(g):
                e.dma_start(out=qg[:, g % 2, :, :, :].rearrange(
                                "p t a c -> p (t a c)").rearrange(
                                "p (a c) -> p a c", a=8),
                            in_=sp_q[g].rearrange("p (a c) -> p a c", a=8)
                            ).then_inc(S["qg%d" % (g % 2)], 16)

            _qgload(0)
            _qgload(1)
            for c in range(NCH):
                g = c // 4
                if c % 4 == 0:
                    if g >= 2:
                        e.wait_ge(S["ffn"], 4 * (g - 1))
                    e.dma_start(out=xnT2[:, g % 2, :, :, :].rearrange(
                                    "p t a c -> p (t a c)").rearrange(
                                    "p (a c) -> p a c", a=8),
                                in_=sp_xn[g].rearrange("p (a c) -> p a c", a=8)
                                ).then_inc(S["xng%d" % (g % 2)], 16)
                    if c >= 4 and g + 1 < NG:
                        e.wait_ge(S["li"], c)
                        _qgload(g + 1)
                if c >= 2:
                    e.wait_ge(S["o_"], c - 1)
                e.dma_start(out=v2[:, c % 2, :], in_=sp_v[c]).then_inc(S["v2d%d" % (c % 2)], 16)
                if c >= 2:
                    e.wait_ge(S["hv0_"], c - 1)
                e.dma_start(out=hv2[:, c % 2, :], in_=sp_hv[c]).then_inc(S["hv2d%d" % (c % 2)], 16)
                if c >= 2:
                    e.wait_ge(S["out1"], c - 1)
                e.dma_start(out=x2[:, c % 2, :],
                            in_=xseg[c * 128:(c + 1) * 128, :]).then_inc(S["x2d%d" % (c % 2)], 16)
                if c >= 3:
                    e.wait_ge(S["out2"], c - 2)
                    e.dma_start(out=outseg[(c - 3) * 128:(c - 2) * 128, :],
                                in_=out_sb[:, (c - 3) % 2, :]
                                ).then_inc(S["ost%d" % ((c - 3) % 2)], 16)
            for ct in (NCH - 3, NCH - 2, NCH - 1):
                e.wait_ge(S["out2"], ct + 1)
                e.dma_start(out=outseg[ct * 128:(ct + 1) * 128, :],
                            in_=out_sb[:, ct % 2, :]).then_inc(S["ost%d" % (ct % 2)], 16)
            e.wait_ge(S["ost0"], 16 * 16)
            e.wait_ge(S["ost1"], 16 * 16)

        @block.gpsimd
        def _(e):
            if cc is not None:
                e.collective_compute(
                    "AllGather", ALU.bypass,
                    replica_groups=[list(range(NCORES))],
                    ins=[cc[0]], outs=[cc[1]],
                ).then_inc(S["ccd"])
            W, fin = _serial(e, S["slfp2"], SEMCNT)
            fin(e.memset(onesb[:, :], 1.0))
            W(); fin(e.memset(one1[:, :], 1.0))
            W(); fin(e.memset(epsb[:, :], EPS))
            W(); fin(e.memset(hkiT[:, :], 0.0))
            W(); fin(e.memset(hvi[:, :, :], 0.0))
            W(); fin(e.memset(drow[:, :], 1.0), S["ini2"])
            for c in range(NCH):
                if c >= 1:
                    e.wait_ge(S["egd%d" % ((c - 1) % 4)], 16 * ((c - 1) // 4 + 1))
                    W(); fin(e.tensor_mul(drow[:, :], drow[:, :],
                                          eg2[:, (c - 1) % 4, :]), S["lnd"])
                else:
                    W(); fin(e.engine_nop(), S["lnd"])
                if c >= 1:
                    e.wait_ge(S["pdm"], c)
                W(); fin(e.tensor_copy(dbf[:, :], drow[:, :]), S["dbf_"])
                if c >= 3:
                    t3 = c - 3
                    e.wait_ge(S["out1"], t3 + 1)
                    e.wait_ge(S["ffnc_"], t3 + 1)
                    W(); fin(e.tensor_add(out_sb[:, t3 % 2, :],
                                          out_sb[:, t3 % 2, :],
                                          ffnc[:, t3 % 4, :]), S["out2"])
            for t3 in (NCH - 3, NCH - 2, NCH - 1):
                e.wait_ge(S["out1"], t3 + 1)
                e.wait_ge(S["ffnc_"], t3 + 1)
                W(); fin(e.tensor_add(out_sb[:, t3 % 2, :],
                                      out_sb[:, t3 % 2, :],
                                      ffnc[:, t3 % 4, :]), S["out2"])

        @block.scalar
        def _(e):
            W, fin = _serial(e, S["slfa2"], SEMCNT)
            e.wait_ge(S["ini2"], 1)
            e.wait_ge(S["gst"], 1)
            W(); fin(e.activation(cfT4.rearrange("p a r -> p (a r)")[:, :],
                                  pGsT, AF.Exp), S["cft"])
            for c0 in range(2):
                e.dma_start(out=glp2[:, c0 % 2, :],
                            in_=sp_glp[c0]).then_inc(S["gld%d" % c0], 16)
                e.dma_start(out=sn2[:, c0 % 3, :], in_=sp_s[c0]).then_inc(S["sn2d%d" % c0], 16)
                e.dma_start(out=eg2[:, c0 % 4, :], in_=sp_eg[c0]).then_inc(S["egd%d" % c0], 16)

            e.wait_ge(S["li"], 1)
            W(); fin(e.activation(linit[:, 0, :], pLi, AF.Identity), S["lie"])

            def _act_tail(t):
                e.wait_ge(S["o_"], t + 1)
                for h in range(H):
                    W()
                    fin(e.activation(sqscr[:, :], po(h),
                                     AF.Square, accum_out=ssum[:, h:h + 1]),
                        S["sq"])
                W(); fin(e.activation(rstd[:, :], ssum[:, :], AF.Ln,
                                      scale=1.0 / 256.0, bias=epsb[:, 0:1]))
                W(); fin(e.activation(rstd[:, :], rstd[:, :], AF.Exp,
                                      scale=-0.5), S["rstd_"])
                if t >= 1:
                    e.wait_ge(S["ont%d" % ((t - 1) % 2)], 16 * ((t - 1) // 2 + 1))
                for h in range(H):
                    W()
                    fin(e.activation(o_n[:, h * 256:(h + 1) * 256], po(h),
                                     AF.Identity, scale=rstd[:, h:h + 1]),
                        S["on"] if h == 3 else None)
                e.wait_ge(S["on"], t + 1)
                if t >= 2:
                    e.wait_ge(S["wo"], t - 1)
                e.dma_start(out=onT[:, t % 2, :, :],
                            in_=o_n.rearrange("p (a c) -> p a c", a=8),
                            transpose=True).then_inc(S["ont%d" % (t % 2)], 16)

            for c in range(NCH):
                if c + 1 < NCH:
                    e.wait_ge(S["li"], c + 2)
                    W(); fin(e.activation(linit[:, c + 1, :], pLi,
                                          AF.Identity), S["lie"])
                e.wait_ge(S["gld%d" % (c % 2)], 16 * (c // 2 + 1))
                if c >= 2:
                    e.wait_ge(S["pd"], c - 1)
                W(); fin(e.activation(A_sb[:, c % 2, :], gc2[:, c % 2, :],
                                      AF.Exp), S["aexp"])
                e.wait_ge(S["lsub_"], c + 1)
                for h in range(H):
                    W()
                    fin(e.activation(esm[:, h * 64:(h + 1) * 64],
                                     lsub[:, h * 64:(h + 1) * 64], AF.Exp,
                                     accum_out=rsum[:, h:h + 1]),
                        S["e_"] if h == 3 else None)
                if c >= 1:
                    _act_tail(c - 1)
                if c + 2 < NCH:
                    cn = c + 2
                    e.wait_ge(S["lass"], c + 1)
                    e.dma_start(out=glp2[:, cn % 2, :],
                                in_=sp_glp[cn]).then_inc(S["gld%d" % (cn % 2)], 16)
                    e.wait_ge(S["T_"], cn - 2)
                    e.dma_start(out=sn2[:, cn % 3, :], in_=sp_s[cn]).then_inc(S["sn2d%d" % (cn % 3)], 16)
                    if c >= 1:
                        e.wait_ge(S["lnd"], c - 1)
                    e.dma_start(out=eg2[:, cn % 4, :], in_=sp_eg[cn]).then_inc(S["egd%d" % (cn % 4)], 16)
            _act_tail(NCH - 1)

        @block.vector
        def _(e):
            W, fin = _serial(e, S["slfv2"], SEMCNT)
            e.wait_ge(S["ldc"], 80)
            fin(e.tensor_copy(fbi_b[:, :], fbi_r[:, :]), S["fbi"])
            e.wait_ge(S["galL"], 32)
            W(); fin(e.tensor_copy(gallb[:, :], gall[:, :]), S["galb"])
            e.wait_ge(S["cft"], 1)
            for r in range(8):
                e.wait_ge(S["ldr%d" % (r % 4)], 32 * (r // 4 + 1))
                for blk in range(8):
                    hh = blk // 2
                    W()
                    fin(e.scalar_tensor_tensor(
                        hkiT[:, blk * 128:(blk + 1) * 128],
                        hkr[:, r % 4, blk * 128:(blk + 1) * 128],
                        cfT4[:, hh, r:r + 1],
                        hkiT[:, blk * 128:(blk + 1) * 128],
                        ALU.mult, ALU.add))
                for hh in range(H):
                    W()
                    fin(e.scalar_tensor_tensor(
                        hvi[:, hh, :], hvr[:, r % 4, hh * 256:(hh + 1) * 256],
                        cfT4[:, hh, r:r + 1], hvi[:, hh, :],
                        ALU.mult, ALU.add), S["cmb"] if hh == 3 else None)
            W(); fin(e.tensor_copy(hkiTb.rearrange("p a c -> p (a c)")[:, :],
                                   hkiT[:, :]), S["hkib"])
            e.wait_ge(S["hkitp"], 1)
            W(); fin(e.tensor_copy(hki.rearrange("p a m -> p (a m)")[:, :],
                                   pHkT), S["hki_"])
            def _dve_clip(t):
                e.wait_ge(S["ffn"], t + 1)
                if t >= 4:
                    e.wait_ge(S["out2"], t - 3)
                W(); fin(e.tensor_scalar(ffnc[:, t % 4, :], pfa, 100.0, -100.0,
                                         ALU.min, ALU.max), S["ffnc_"])

            def _dve_tp(t):
                e.wait_ge(S["T_"], t + 1)
                W(); fin(e.tensor_copy(tp_sb.rearrange("p a c -> p (a c)")[:, :],
                                       PSB[0:64, :]), S["tp"])

            def _dve_wm(t):
                e.wait_ge(S["wt"], t + 1)
                W(); fin(e.tensor_mul(wm_sb.rearrange("p a c -> p (a c)")[:, :],
                                      pW[:, :], tri4_sb[:, :]), S["wm"])

            def _dve_tail2(t):
                e.wait_ge(S["wo"], t + 1)
                e.wait_ge(S["x2d%d" % (t % 2)], 16 * (t // 2 + 1))
                if t >= 2:
                    e.wait_ge(S["ost%d" % (t % 2)], 16 * (t // 2))
                W(); fin(e.tensor_add(out_sb[:, t % 2, :], x2[:, t % 2, :],
                                      pfa), S["out1"])

            for c in range(NCH):
                if c >= 1:
                    _dve_clip(c - 1)
                e.wait_ge(S["pdm"], c + 1)
                W(); fin(e.tensor_copy(dcol[:, :], pDc), S["dcol_"])
                e.wait_ge(S["lie"], c + 1)
                W(); fin(e.tensor_mul(lscr[:, :], linit[:, c, :], pDb), S["lt"])
                e.wait_ge(S["gld%d" % (c % 2)], 16 * (c // 2 + 1))
                W(); fin(e.tensor_add(lscr2[:, :], lscr[:, :], lp2[:, c % 2, :]))
                if c >= 1:
                    _dve_tp(c - 1)
                if c >= 2:
                    _dve_tail2(c - 2)
                e.wait_ge(S["aexp"], c + 1)
                W(); fin(e.tensor_mul(lscr2[:, :], lscr2[:, :], A_sb[:, c % 2, :]),
                         S["lass"])
                W(); fin(e.reduce_max(nmax[:, :],
                                      lscr2.rearrange("p (s m) -> p s m", s=4),
                                      axis=AX.X, negate=True), S["nmax_"])
                if c >= 1:
                    _dve_wm(c - 1)
                for h in range(H):
                    W()
                    fin(e.tensor_scalar_add(lsub[:, h * 64:(h + 1) * 64],
                                            lscr2[:, h * 64:(h + 1) * 64],
                                            nmax[:, h:h + 1]),
                        S["lsub_"] if h == 3 else None)
                e.wait_ge(S["e_"], c + 1)
                W(); fin(e.reciprocal(rcp[:, :], rsum[:, :]), S["rcp_"])
                if c >= 3:
                    e.wait_ge(S["T_"], c - 1)
                for h in range(H):
                    W()
                    fin(e.scalar_tensor_tensor(pd_sb[:, c % 2, h * 64:(h + 1) * 64],
                                               esm[:, h * 64:(h + 1) * 64],
                                               rcp[:, h:h + 1],
                                               A_sb[:, c % 2, h * 64:(h + 1) * 64],
                                               ALU.mult, ALU.mult),
                        S["pd"] if h == 3 else None)
                if c >= 2:
                    e.wait_ge(S["o_"], c - 1)
                e.wait_ge(S["hv2d%d" % (c % 2)], 16 * (c // 2 + 1))
                for h in range(H):
                    W()
                    fin(e.scalar_tensor_tensor(hv0[:, c % 2, h, :], hvi[:, h, :],
                                               dcol[:, h:h + 1],
                                               hv2[:, c % 2, h * 256:(h + 1) * 256],
                                               ALU.mult, ALU.add),
                        S["hv0_"] if h == 3 else None)
            _dve_clip(NCH - 1)
            _dve_tp(NCH - 1)
            _dve_wm(NCH - 1)
            _dve_tail2(NCH - 2)
            _dve_tail2(NCH - 1)

        @block.tensor
        def _(e):
            e.wait_ge(S["galb"], 1)
            e.wait_ge(S["ldc"], 80)
            for h in range(H):
                op = e.matmul(pGsT[:, h * 8:(h + 1) * 8],
                              gallb[:, h * 64:(h + 1) * 64], sel_sb[:, :],
                              start=True, stop=True)
            op.then_inc(S["gst"], 1)
            e.wait_ge(S["hkib"], 1)
            for blk in range(8):
                op = e.transpose(pHkT[:, blk * 64:(blk + 1) * 64],
                                 hkiTb[:, blk, :], ident[0:64, 0:64])
            op.then_inc(S["hkitp"], 1)
            def _linit(c2):
                g2 = c2 // 4
                cc2 = c2 % 4
                if c2 == 0:
                    e.wait_ge(S["hki_"], 1)
                e.wait_ge(S["qg%d" % (g2 % 2)], 16 * (g2 // 2 + 1))
                e.wait_ge(S["lie"], c2)
                for h in range(H):
                    e.matmul(pLi[:, h * 64:(h + 1) * 64],
                             qg[:, g2 % 2, cc2, 2 * h, :],
                             hki[:, 2 * h, :], start=True, stop=False)
                    op = e.matmul(pLi[:, h * 64:(h + 1) * 64],
                                  qg[:, g2 % 2, cc2, 2 * h + 1, :],
                                  hki[:, 2 * h + 1, :], start=False, stop=True)
                op.then_inc(S["li"], 1)

            _linit(0)
            _linit(1)

            def _pe_d(p):
                e.wait_ge(S["dbf_"], p + 1)
                e.wait_ge(S["lt"], p)
                e.wait_ge(S["dcol_"], p)
                e.matmul(pDb, onesb[:, :], dbf[:, :], start=True, stop=True)
                for h in range(H):
                    op = e.matmul(pDc[:, h:h + 1], dbf[0:1, h * 64:(h + 1) * 64],
                                  one1[:, :], start=True, stop=True)
                op.then_inc(S["pdm"], 1)

            def _pe_ffn(p, half):
                g = p // 4
                cc = p % 4
                if half == 0:
                    e.wait_ge(S["xng%d" % (g % 2)], 16 * (g // 2 + 1))
                    e.wait_ge(S["ldw"], 32)
                    e.wait_ge(S["fbi"], 1)
                    e.wait_ge(S["ffnc_"], p)
                    if p >= 1:
                        e.wait_ge(S["out1"], p - 1)
                for dt in range(8):
                    e.matmul(pfa[:, half * 512:(half + 1) * 512],
                             xnT2[:, g % 2, cc, dt, :],
                             fwT[:, half * 4:(half + 1) * 4, dt, :],
                             start=(dt == 0), stop=False)
                op = e.matmul(pfa[:, half * 512:(half + 1) * 512],
                              onesb[:, :],
                              fbi_b[0:1, half * 512:(half + 1) * 512],
                              start=False, stop=True)
                if half == 1:
                    op.then_inc(S["ffn"], 1)

            def _pe_T(p):
                e.wait_ge(S["pd"], p + 1)
                e.wait_ge(S["sn2d%d" % (p % 3)], 16 * (p // 3 + 1))
                e.wait_ge(S["tp"], p)
                if p == 0:
                    e.wait_ge(S["hki_"], 1)
                for h in range(H):
                    e.transpose(PSB[0:64, h * 128:(h + 1) * 128],
                                sn2[:, p % 3, h * 64:(h + 1) * 64], ident[:, :])
                for h in range(H):
                    op = e.transpose(PSB[0:64, 512 + h * 128:512 + (h + 1) * 128],
                                     pd_sb[:, p % 2, h * 64:(h + 1) * 64],
                                     ident[:, :])
                op.then_inc(S["T_"], 1)

            def _pe_wt(p):
                e.wait_ge(S["tp"], p + 1)
                e.wait_ge(S["wm"], p)
                for h in range(H):
                    op = e.matmul(pW[:, h * 128:(h + 1) * 128],
                                  tp_sb[:, h, :], tp_sb[:, 4 + h, :],
                                  start=True, stop=True)
                op.then_inc(S["wt"], 1)

            def _pe_o(p):
                e.wait_ge(S["wm"], p + 1)
                e.wait_ge(S["hv0_"], p + 1)
                e.wait_ge(S["v2d%d" % (p % 2)], 16 * (p // 2 + 1))
                if p >= 1:
                    e.wait_ge(S["on"], p)
                for h in range(H):
                    e.matmul(po(h),
                             wm_sb[:, h, :], v2[:, p % 2, h * 256:(h + 1) * 256],
                             start=True, stop=False)
                    op = e.matmul(po(h),
                                  tp_sb[:, 4 + h, :], hv0[:, p % 2, h, :],
                                  start=False, stop=True)
                op.then_inc(S["o_"], 1)

            def _pe_wo(t):
                e.wait_ge(S["ont%d" % (t % 2)], 16 * (t // 2 + 1))
                e.wait_ge(S["ffnc_"], min(t + 2, NCH))
                e.wait_ge(S["out1"], t)
                for vt in range(8):
                    for half in range(2):
                        op = e.matmul(pfa[:, half * 512:(half + 1) * 512],
                                      onT[:, t % 2, vt, :],
                                      wo_sb[:, vt, half * 512:(half + 1) * 512],
                                      start=(vt == 0), stop=(vt == 7))
                op.then_inc(S["wo"], 1)

            for c in range(NCH):
                _pe_d(c)
                if c >= 1:
                    _pe_T(c - 1)
                if c >= 2:
                    _pe_wo(c - 2)
                if c >= 1:
                    _pe_wt(c - 1)
                    _pe_o(c - 1)
                _pe_ffn(c, 0)
                _pe_ffn(c, 1)
                if c + 2 < NCH:
                    _linit(c + 2)
            _pe_T(NCH - 1)
            _pe_wo(NCH - 2)
            _pe_wt(NCH - 1)
            _pe_o(NCH - 1)
            _pe_wo(NCH - 1)


def build_fused():
    nc = bass.Bass("TRN2", target_bir_lowering=False, debug=False)
    d1 = _decl_l1_inputs(nc)
    d2 = _decl_l2_inputs(nc)
    d = {**d1, **d2}
    d["xseg"] = d1["xseg"]
    d["triu_in"] = d1["triu_in"]
    d["ident_in"] = d1["ident_in"]
    for nm, shp, dt in L1_SPILLS:
        d[nm] = nc.dram_tensor(nm, shp, dt).ap()
    summ_h = nc.dram_tensor("summ_int", [64, 2304], F32)
    ag_h = nc.dram_tensor("ag_int", [512, 2304], F32)
    d["summ"] = summ_h.ap()
    d["ag"] = ag_h.ap()
    d["outseg"] = nc.dram_tensor("outseg", [NT, D], F32,
                                 kind="ExternalOutput").ap()
    _emit_l1(nc, d)
    nc.reset()
    _emit_l2(nc, d, cc=(summ_h.ap().opt(), ag_h.ap().opt()))
    return nc



def _host_consts():
    j = np.arange(C)[:, None]
    i = np.arange(C)[None, :]
    tri_ji = (j <= i).astype(np.float32)
    trineg = np.ascontiguousarray((tri_ji * np.float32(NEG8)).astype(ml_dtypes.bfloat16))
    triu_b = np.ascontiguousarray(tri_ji.astype(ml_dtypes.bfloat16))
    tri4_b = np.ascontiguousarray(np.tile(tri_ji, (1, 4)).astype(ml_dtypes.bfloat16))
    ident = np.ascontiguousarray(np.eye(C, dtype=ml_dtypes.bfloat16))
    return trineg, triu_b, tri4_b, ident


def _sel9(core):
    b0 = 4 * (core // 4)
    rank = core
    sel = np.zeros((9, 8), np.float32)
    for r in range(8):
        if b0 <= r < rank:
            for u in range(8):
                if r < u < rank:
                    sel[u, r] = 1.0
        else:
            sel[8, r] = 1.0
    return np.ascontiguousarray(sel.astype(ml_dtypes.bfloat16))


_NCF = None


def kernel(**inputs):
    global _NCF
    x = np.ascontiguousarray(np.asarray(inputs["L0_attn_x"], np.float32))
    B, T, _ = x.shape
    xf = x.reshape(NCORES, NT, D)
    trineg, triu_b, tri4_b, ident = _host_consts()
    negrow = np.full((1, 256), -1e9, np.float32)

    if _NCF is None:
        _NCF = build_fused()
    ins = []
    for cidx in range(NCORES):
        ins.append({
            "xseg": np.ascontiguousarray(xf[cidx]),
            "Wq": np.asarray(inputs["L0_attn_Wq"], np.float32),
            "Wk": np.asarray(inputs["L0_attn_Wk"], np.float32),
            "Wv": np.asarray(inputs["L0_attn_Wv"], np.float32),
            "Wf": np.asarray(inputs["L0_attn_Wf"], np.float32),
            "Wo": np.asarray(inputs["L0_attn_Wo"], np.float32),
            "gnwc": np.ascontiguousarray(
                np.asarray(inputs["L0_attn_g_norm_w"], np.float32
                           ).reshape(2, 128).T),
            "fw": np.asarray(inputs["L0_ffn_weights"], np.float32),
            "fscc": np.ascontiguousarray(
                np.asarray(inputs["L0_ffn_scale"], np.float32
                           ).reshape(8, 128).T),
            "fbias": np.asarray(inputs["L0_ffn_bias"], np.float32),
            "trineg": trineg, "triu_in": triu_b, "tri4_in": tri4_b,
            "ident_in": ident,
            "sel9": _sel9(cidx), "negrow": negrow,
        })
    res = run_bass_kernel_spmd(_NCF, ins, core_ids=list(range(NCORES)))
    out = np.concatenate([res.results[cidx]["outseg"][None]
                          for cidx in range(NCORES)], axis=0)
    return np.ascontiguousarray(out.reshape(B, T, D).astype(np.float32))
